# version 04
# baseline (speedup 1.0000x reference)
import contextlib
import numpy as np
import ml_dtypes
import concourse.bass as bass
import concourse.mybir as mybir
from concourse.bass_utils import run_bass_kernel_spmd

F32 = mybir.dt.float32
BF16 = mybir.dt.bfloat16
I32 = mybir.dt.int32
AF = mybir.ActivationFunctionType
ALU = mybir.AluOpType
AX = mybir.AxisListType

T = 2048
D = 1024
NTILE = T // 128
HALF = T // 2
DHID = 4096
LN_EPS = 1e-5
ARENA_WORDS = 52000

ENGINES = ("tensor", "vector", "scalar", "gpsimd", "sync")


class Sched:
    def __init__(self, nc):
        self.nc = nc
        self.ops = {e: [] for e in ENGINES}
        self.sems = {}
        self.cnt = {"T": 0, "V": 0, "A": 0, "P": 0, "DS": 0, "DG": 0}

    def op(self, eng, fn, waits=(), inc=None, inc_amt=1):
        val = None
        if inc is not None:
            self.cnt[inc] += inc_amt
            val = self.cnt[inc]
        ws = tuple(w for w in waits if w is not None and w[1] is not None)
        self.ops[eng].append((fn, ws, inc, val))
        return val

    def dma(self, queue, out, in_, waits=(), allow_slow=False):
        sem = "DS" if queue == "sync" else "DG"

        def fn(eng):
            return eng.dma_start(out=out, in_=in_,
                                 allow_slow_non_contiguous=allow_slow)
        return self.op(queue, fn, waits=waits, inc=sem, inc_amt=16)

    def guard(self):
        return [("A", self.cnt["A"] or None), ("V", self.cnt["V"] or None),
                ("T", self.cnt["T"] or None)]

    def tensor_fence(self):
        return self.op("tensor", lambda t: t.nop(), waits=self.guard(),
                       inc="T")

    def emit(self, block):
        def run(eng_obj, eng_name):
            for fn, waits, inc, val in self.ops[eng_name]:
                for (s, v) in waits:
                    eng_obj.wait_ge(self.sems[s], v)
                inst = fn(eng_obj)
                if inc is not None:
                    amt = 16 if inc in ("DS", "DG") else 1
                    inst.then_inc(self.sems[inc], amt)

        @block.tensor
        def _(t):
            run(t, "tensor")

        @block.vector
        def _(v):
            run(v, "vector")

        @block.scalar
        def _(sc):
            run(sc, "scalar")

        @block.gpsimd
        def _(g):
            run(g, "gpsimd")

        @block.sync
        def _(s):
            run(s, "sync")


class Arena:
    def __init__(self, tensor, words):
        self.t = tensor
        self.total = words
        self.off = 0
        self.marks = []

    def push(self):
        self.marks.append(self.off)

    def pop(self):
        self.off = self.marks.pop()

    def alloc(self, shape, dtype):
        free = int(np.prod(shape[1:]))
        words = free if dtype in (F32, I32) else (free + 1) // 2
        words = (words + 7) & ~7
        assert self.off + words <= self.total, \
            f"arena overflow {self.off}+{words}>{self.total}"
        view = self.t[:, self.off:self.off + words]
        self.off += words
        if dtype == BF16:
            view = view.bitcast(BF16)
        elif dtype == I32:
            view = view.bitcast(I32)
        total_free = int(np.prod(shape[1:]))
        view = view[:, :total_free]
        if len(shape) > 2:
            names = " ".join(f"d{i}" for i in range(len(shape) - 1))
            kw = {f"d{i}": int(shape[i + 1]) for i in range(len(shape) - 1)}
            view = view.rearrange(f"p ({names}) -> p {names}", **kw)
        return view


def mm_group(S, psum_ap, pairs, waits=(), start=True, stop=True):
    def fn(t):
        last = None
        n = len(pairs)
        for i, (lh, rh) in enumerate(pairs):
            last = t.matmul(out=psum_ap, lhsT=lh, rhs=rh,
                            start=(start and i == 0),
                            stop=(stop and i == n - 1))
        return last
    return S.op("tensor", fn, waits=waits, inc="T")




STAT_KEYS = ("nsum", "ssq", "nmean", "msq", "veps", "lnl", "rstd", "mrs",
             "rstd_v", "mrs_v")


def make_stats(scr, j):
    base = 16 * j
    st = {k: scr[:, base + i:base + i + 1]
          for i, k in enumerate(STAT_KEYS)}
    st["psrc"] = scr[:, base + 6:base + 8]
    st["pdst"] = scr[:, base + 8:base + 10]
    return st

def ln_tile(S, x_sb, xn_bf, scrap, st, pre_waits=(), out_waits=(),
            rms_only=False):
    if rms_only:
        S.op("scalar", lambda sc: sc.activation(
            out=scrap, in_=x_sb, func=AF.Square, accum_out=st["ssq"]),
            waits=pre_waits)
        S.op("scalar", lambda sc: sc.activation(
            out=st["lnl"], in_=st["ssq"], func=AF.Ln, scale=1.0 / D,
            bias=LN_EPS))
        ar1 = S.op("scalar", lambda sc: sc.activation(
            out=st["rstd"], in_=st["lnl"], func=AF.Exp, scale=-0.5),
            inc="A")
        b1 = S.op("vector", lambda v: v.engine_nop(), waits=[("A", ar1)],
                  inc="V")
        vo = S.op("scalar", lambda sc: sc.mul(
            out=xn_bf, in_=x_sb, mul=st["rstd"]),
            waits=list(out_waits) + [("V", b1)], inc="A")
        return ("A", vo)
    vsum = S.op("vector", lambda v: v.reduce_sum(
        out=st["nsum"], in_=x_sb, axis=AX.X, negate=True),
        waits=pre_waits, inc="V")
    S.op("scalar", lambda sc: sc.activation(
        out=scrap, in_=x_sb, func=AF.Square, accum_out=st["ssq"]),
        waits=pre_waits)
    S.op("scalar", lambda sc: sc.mul(
        out=st["nmean"], in_=st["nsum"], mul=1.0 / D),
        waits=[("V", vsum)])
    aml = S.op("scalar", lambda sc: sc.activation(
        out=st["msq"], in_=st["nmean"], func=AF.Square), inc="A")
    vv = S.op("vector", lambda v: v.scalar_tensor_tensor(
        out=st["veps"], in0=st["ssq"], scalar=1.0 / D, in1=st["msq"],
        op0=ALU.mult, op1=ALU.subtract), waits=[("A", aml)], inc="V")
    S.op("scalar", lambda sc: sc.activation(
        out=st["lnl"], in_=st["veps"], func=AF.Ln, bias=LN_EPS),
        waits=[("V", vv)])
    ar1 = S.op("scalar", lambda sc: sc.activation(
        out=st["rstd"], in_=st["lnl"], func=AF.Exp, scale=-0.5), inc="A")
    b1 = S.op("vector", lambda v: v.engine_nop(), waits=[("A", ar1)],
              inc="V")
    am = S.op("scalar", lambda sc: sc.mul(
        out=st["mrs"], in_=st["nmean"], mul=st["rstd"]),
        waits=[("V", b1)], inc="A")
    b2 = S.op("vector", lambda v: v.engine_nop(), waits=[("A", am)],
              inc="V")
    vo = S.op("scalar", lambda sc: sc.activation(
        out=xn_bf, in_=x_sb, func=AF.Identity, scale=st["rstd"],
        bias=st["mrs"]), waits=list(out_waits) + [("V", b2)], inc="A")
    return ("A", vo)



def emit_l0ffn(S, nc, ar, cst, W, in_ap, out_ap, PB, PBT):
    ar.push()
    S.tensor_fence()
    xnT = ar.alloc((128, 8, HALF), BF16)
    hA = ar.alloc((128, 32, HALF), BF16)
    hB = ar.alloc((128, 32, HALF), BF16)
    w12 = [[ar.alloc((128, 8, 128), BF16) for _ in range(2)] for _ in range(2)]
    hwb = [ar.alloc((128, 32, 128), BF16) for _ in range(2)]
    w3b = [ar.alloc((128, D), BF16) for _ in range(2)]
    xtile = [ar.alloc((128, D), F32) for _ in range(2)]
    scr = ar.alloc((128, 2592), F32)
    w3s_rep = ar.alloc((128, D), F32)
    t1fa = ar.alloc((128, 512), F32)
    scrap = scr[:, 0:1024]
    xn_bf = [scr[:, 1024 + i * 512:1536 + i * 512].bitcast(BF16)
             for i in range(2)]
    stats = [make_stats(scr[:, 2048:2080], j) for j in range(2)]
    tmpS = [scr[:, 2080 + 256 * i:2336 + 256 * i].bitcast(BF16)
            for i in range(2)]
    xres = [scr[:, 0:1024], scr[:, 1024:2048]]

    for dg in range(2):
        tm = mm_group(S, PB[6][:, 0:512],
                      [(cst["ones_f"][0:1, :],
                        cst["w3s_row"][0:1, dg * 512:(dg + 1) * 512])],
                      waits=([("DS", cst["loaded"]), ("DG", cst["loaded_g"])]
                             if dg == 0 else []) + S.guard())
        S.op("vector", lambda v, dg=dg: v.tensor_scalar_mul(
            out=w3s_rep[:, dg * 512:(dg + 1) * 512], in0=PB[6][:, 0:512],
            scalar1=0.5), waits=[("T", tm)], inc="V")

    pb_read = {i: ("V", S.cnt["V"]) for i in range(8)}
    last_store = None

    for half in range(2):
        tile0 = half * 8
        aouts = {}
        tdone = {}
        for i in range(8):
            tl = tile0 + i
            ld = S.dma("sync", xtile[i % 2],
                       in_ap[tl * 128:(tl + 1) * 128, :],
                       waits=([aouts[i - 2]] if i >= 2 else
                              S.guard() + [("DS", last_store)]))
            pw = [("DS", ld)]
            if i >= 1:
                pw.append(aouts[i - 1])
            ow = [("T", tdone[i - 2])] if i >= 2 else []
            a = ln_tile(S, xtile[i % 2], xn_bf[i % 2], scrap, stats[i % 2],
                        pre_waits=pw, out_waits=ow)
            aouts[i] = a

            def tr_fn(t, i=i):
                last = None
                for ic in range(8):
                    last = t.transpose(
                        out=PBT[:, ic * 128:(ic + 1) * 128],
                        in_=xn_bf[i % 2][:, ic * 128:(ic + 1) * 128],
                        identity=cst["ident_bf"])
                return last
            tdone[i] = S.op("tensor", tr_fn, waits=[a, pb_read[7]],
                            inc="T")
            pv = S.op("vector", lambda v, i=i: v.tensor_copy(
                out=xnT[:, :, i * 128:(i + 1) * 128],
                in_=PBT[:, :].rearrange("p (ic q) -> p ic q", ic=8)),
                waits=[("T", tdone[i])], inc="V")
            pb_read[7] = ("V", pv)
        xnT_ready = S.cnt["V"]

        ev_hist = {}
        for oc in range(32):
            b = oc % 2
            wwait = ([("V", ev_hist.get((oc - 2, 1)))] if oc >= 2
                     else S.guard())
            S.dma("gpsimd", w12[b][0],
                  W["w1T"][:, oc * 128:(oc + 1) * 128].rearrange(
                      "(ic p) o -> p ic o", p=128), waits=wwait)
            w2l = S.dma("gpsimd", w12[b][1],
                        W["w2T"][:, oc * 128:(oc + 1) * 128].rearrange(
                            "(ic p) o -> p ic o", p=128))
            for ng in range(2):
                bkA, bkB = 2 * ng, 2 * ng + 1
                mw = [("DG", w2l), pb_read[bkA], pb_read[bkB]]
                if oc == 0 and ng == 0:
                    mw.append(("V", xnT_ready))
                tmA = mm_group(S, PB[bkA][:, :],
                               [(w12[b][0][:, ic, :],
                                 xnT[:, ic, ng * 512:(ng + 1) * 512])
                                for ic in range(8)], waits=mw)
                tmB = mm_group(S, PB[bkB][:, :],
                               [(w12[b][1][:, ic, :],
                                 xnT[:, ic, ng * 512:(ng + 1) * 512])
                                for ic in range(8)])
                sA = S.op("scalar", lambda sc, oc=oc, ng=ng, bkA=bkA: sc.activation(
                    out=tmpS[ng], in_=PB[bkA][:, :], func=AF.Silu,
                    scale=W["s1m"][:, oc:oc + 1]),
                    waits=[("T", tmA), ("V", ev_hist.get((oc - 1, ng)))],
                    inc="A")
                ev = S.op("vector", lambda v, oc=oc, ng=ng, bkB=bkB: v.scalar_tensor_tensor(
                    out=hA[:, oc, ng * 512:(ng + 1) * 512], in0=PB[bkB][:, :],
                    scalar=W["s2m"][:, oc:oc + 1], in1=tmpS[ng],
                    op0=ALU.mult, op1=ALU.mult),
                    waits=[("A", sA), ("T", tmB)], inc="V")
                pb_read[bkA] = ("A", sA)
                pb_read[bkB] = ("V", ev)
                ev_hist[(oc, ng)] = ev
        src_ready = ("V", S.cnt["V"])

        for (wname, scol, src, dst) in (("h1T", "h1m", hA, hB),
                                        ("h2T", "h2m", hB, hA)):
            ev2 = {}
            for oc in range(32):
                b = oc % 2
                wwait = ([("A", ev2.get(oc - 2))] if oc >= 2 else S.guard())
                wl = S.dma("gpsimd", hwb[b],
                           W[wname][:, oc * 128:(oc + 1) * 128].rearrange(
                               "(ic p) o -> p ic o", p=128), waits=wwait)
                for ng in range(2):
                    bk = (2 * oc + ng) % 4
                    mw = [("DG", wl), pb_read[bk]]
                    if oc == 0 and ng == 0:
                        mw.append(src_ready)
                    tm = mm_group(S, PB[bk][:, :],
                                  [(hwb[b][:, ic, :],
                                    src[:, ic, ng * 512:(ng + 1) * 512])
                                   for ic in range(32)], waits=mw)
                    ev = S.op("scalar", lambda sc, oc=oc, ng=ng, bk=bk, dst=dst, scol=scol: sc.activation(
                        out=dst[:, oc, ng * 512:(ng + 1) * 512],
                        in_=PB[bk][:, :], func=AF.Gelu,
                        scale=W[scol][:, oc:oc + 1]),
                        waits=[("T", tm)], inc="A")
                    pb_read[bk] = ("A", ev)
                    ev2[oc] = ev
            src_ready = ("A", S.cnt["A"])
        hC_ready = src_ready

        PB8 = PB[:7] + [PBT.bitcast(F32)[:, 0:512]]
        w3_hist = {}
        for tg in range(2):
            for hc in range(32):
                b = hc % 2
                if (tg, hc) in ((0, 0), (0, 1)):
                    wwait = S.guard()
                else:
                    prev = w3_hist.get((tg, hc - 2), w3_hist.get((tg - 1,
                                                                  30 + hc)))
                    wwait = [("T", prev)]
                wl = S.dma("gpsimd", w3b[b],
                           W["w3T"][hc * 128:(hc + 1) * 128, :], waits=wwait)

                def mm5_fn(t, hc=hc, b=b, tg=tg):
                    last = None
                    for ti in range(4):
                        tloc = tg * 4 + ti
                        for dg in range(2):
                            last = t.matmul(
                                out=PB8[ti * 2 + dg][:, :],
                                lhsT=hA[:, hc, tloc * 128:(tloc + 1) * 128],
                                rhs=w3b[b][:, dg * 512:(dg + 1) * 512],
                                start=(hc == 0), stop=(hc == 31))
                    return last
                mw = [("DG", wl)]
                if hc == 0:
                    mw += [hC_ready] + [pb_read[k] for k in range(8)]
                w3_hist[(tg, hc)] = S.op("tensor", mm5_fn, waits=mw, inc="T")
            mm_done = w3_hist[(tg, 31)]
            for ti in range(4):
                tloc = tg * 4 + ti
                tl = tile0 + tloc
                xl = S.dma("sync", xtile[ti % 2],
                           in_ap[tl * 128:(tl + 1) * 128, :],
                           waits=[("DS", last_store), aouts[7]])
                evs = None
                for dg in range(2):
                    bk = ti * 2 + dg
                    S.op("vector", lambda v, bk=bk, dg=dg: v.tensor_tensor(
                        out=t1fa, in0=PB8[bk][:, :],
                        in1=w3s_rep[:, dg * 512:(dg + 1) * 512], op=ALU.mult),
                        waits=[("T", mm_done), ("DS", xl)])
                    evs = S.op("vector", lambda v, ti=ti, dg=dg: v.tensor_add(
                        out=xres[ti % 2][:, dg * 512:(dg + 1) * 512],
                        in0=t1fa,
                        in1=xtile[ti % 2][:, dg * 512:(dg + 1) * 512]),
                        inc="V")
                    pb_read[bk] = ("V", evs)
                last_store = S.dma(
                    "sync", out_ap[tl * 128:(tl + 1) * 128, :],
                    xres[ti % 2], waits=[("V", evs)])
    ar.pop()
    return last_store



C0 = float(1.0 / np.sqrt(1.0 + LN_EPS))


def emit_l0attn(S, nc, ar, cst, W, in_ap, out_ap, PB, PBT, dbg=None):
    S.tensor_fence()

    def dump(name, ap, waits):
        if dbg is not None and name in dbg:
            dv = S.dma("sync", dbg[name], ap, waits=waits)
            S.op("vector", lambda v: v.engine_nop(), waits=[("DS", dv)],
                 inc="V")
            S.op("scalar", lambda sc: sc.nop(), waits=[("DS", dv)],
                 inc="A")
            S.op("gpsimd", lambda g: g.engine_nop(), waits=[("DS", dv)],
                 inc="P")
    ar.push()
    Wg = ar.alloc((128, 8, 4096), BF16)
    Sst = ar.alloc((128, 8, D), F32)
    Sbf = ar.alloc((128, 8, D), BF16)
    tT = ar.alloc((128, 8, 128), BF16)
    qpT = ar.alloc((128, 8, 128), BF16)
    kpT = ar.alloc((128, 8, 128), BF16)
    kg_f = ar.alloc((128, D), F32)
    gate_f = ar.alloc((128, D), F32)
    qp_bf = ar.alloc((128, D), BF16)
    kp_bf = ar.alloc((128, D), BF16)
    v_bf = ar.alloc((128, D), BF16)
    A_bf = ar.alloc((128, 128), BF16)
    t_bf = ar.alloc((128, D), BF16)
    xtile = [ar.alloc((128, D), F32) for _ in range(2)]
    xout_sb = [ar.alloc((128, D), F32) for _ in range(2)]
    scr0 = ar.alloc((128, 512), F32)
    scr1 = ar.alloc((128, 512), F32)
    scrD = ar.alloc((128, D), F32)
    scrLN = ar.alloc((128, D), F32)
    o1f = ar.alloc((128, D), F32)
    den = ar.alloc((128, 1), F32)
    den2 = ar.alloc((128, 1), F32)
    rden = ar.alloc((128, 1), F32)
    rowpool = ar.alloc((128, 6656), F32)
    rowbf = rowpool.bitcast(BF16)
    bvec = rowpool[0:1, 0:4096]
    bvec_bf = rowbf[0:1, 8192:12288]
    ksum_bf = rowbf[0:1, 12288:13312]
    brow_sb = rowpool[32:33, 0:4096]
    ksum = rowpool[64:65, 0:1024]
    lng = ar.alloc((128, 8), F32)
    lnb_bf = ar.alloc((128, 8), BF16)
    scrsm = ar.alloc((128, 32), F32)
    st2 = [make_stats(scrsm, j) for j in range(2)]

    wl = S.dma("gpsimd", Wg,
               W["l0a_w"].rearrange("(ic p) o -> p ic o", p=128),
               waits=S.guard())
    lnbl = S.dma("gpsimd", lnb_bf, W["l0a_lnb"])
    lngl = S.dma("sync", lng, W["l0a_lng"], waits=S.guard())
    browl = S.dma("sync", brow_sb, W["l0a_brow"])

    vb = None
    for og in range(8):
        tm = mm_group(S, PB[0][0:1, :],
                      [(lnb_bf[:, ic:ic + 1],
                        Wg[:, ic, og * 512:(og + 1) * 512])
                       for ic in range(8)],
                      waits=[("DG", max(wl, lnbl)), ("DS", browl),
                             ("V", vb)] + (S.guard() if og == 0 else []))
        vb = S.op("vector", lambda v, og=og: v.tensor_add(
            out=bvec[:, og * 512:(og + 1) * 512], in0=PB[0][0:1, :],
            in1=brow_sb[:, og * 512:(og + 1) * 512]), waits=[("T", tm)],
            inc="V")
    vbf = S.op("vector", lambda v: v.tensor_copy(out=bvec_bf, in_=bvec),
               inc="V")
    wsc = None
    bvec_T = S.cnt["T"]
    for ic in range(8):
        wsc = S.op("vector", lambda v, ic=ic: v.tensor_scalar(
            out=Wg[:, ic, :], in0=Wg[:, ic, :], scalar1=lng[:, ic:ic + 1],
            scalar2=C0, op0=ALU.mult, op1=ALU.mult),
            waits=([("T", bvec_T), ("DS", lngl)] if ic == 0 else []),
            inc="V")
    S.op("vector", lambda v: v.memset(Sst[:, :, :], 0.0))
    S.op("vector", lambda v: v.memset(ksum, 0.0))
    vz = S.op("vector", lambda v: v.memset(ksum_bf, 0.0), inc="V")

    pb_read = {i: ("V", S.cnt["V"]) for i in range(8)}
    last_store = None
    sbf_done = None
    prev_exp = prev_stt = None

    def tr_make(src_bf, dst, ready):
        def trf(t):
            last = None
            for ic in range(8):
                last = t.transpose(
                    out=PBT[:, ic * 128:(ic + 1) * 128],
                    in_=src_bf[:, ic * 128:(ic + 1) * 128],
                    identity=cst["ident_bf"])
            return last
        tm = S.op("tensor", trf, waits=[ready, pb_read[7],
                                        ("DG", cst["loaded_g"])], inc="T")
        pv = S.op("vector", lambda v: v.tensor_copy(
            out=dst[:, :, :],
            in_=PBT[:, :].rearrange("p (ic q) -> p ic q", ic=8)),
            waits=[("T", tm)], inc="V")
        pb_read[7] = ("V", pv)
        return ("V", pv)

    for c in range(NTILE):
        par = c % 2
        ld = S.dma("sync", xtile[par], in_ap[c * 128:(c + 1) * 128, :],
                   waits=[("DS", last_store)] + (S.guard() if c < 2 else []))
        a = ln_tile(S, xtile[par], t_bf, scrLN, st2[par],
                    pre_waits=[("DS", ld)])
        tT_ok = tr_make(t_bf, tT, a)

        og_order = [0, 1, 6, 7, 2, 3, 4, 5]
        gate_ready = {}
        kmul = {}
        qp_done = kp_done = v_done = None
        for idx, og in enumerate(og_order):
            bk = idx % 4
            mw = [tT_ok, pb_read[bk]]
            if c == 0 and idx == 0:
                mw += [("V", vbf), ("V", wsc), ("V", vz)]
            tm = mm_group(S, PB[bk][:, :],
                          [(cst["ones_bf"][0:1, :],
                            bvec_bf[:, og * 512:(og + 1) * 512])] +
                          [(tT[:, ic, :], Wg[:, ic, og * 512:(og + 1) * 512])
                           for ic in range(8)], waits=mw)
            if og in (0, 1):
                vm = S.op("vector", lambda v, bk=bk: v.tensor_scalar_min(
                    out=scr0, in0=PB[bk][:, :], scalar1=0.0),
                    waits=[("T", tm), ("A", prev_exp)], inc="V")
                a1 = S.op("scalar", lambda sc: sc.activation(
                    out=scr1, in_=scr0, func=AF.Exp),
                    waits=[("V", vm), ("V", prev_stt)], inc="A")
                prev_exp = a1
                qp_done = S.op("vector", lambda v, bk=bk, og=og: v.scalar_tensor_tensor(
                    out=qp_bf[:, og * 512:(og + 1) * 512], in0=PB[bk][:, :],
                    scalar=0.0, in1=scr1, op0=ALU.max, op1=ALU.add),
                    waits=[("A", a1)], inc="V")
                prev_stt = qp_done
                pb_read[bk] = ("V", qp_done)
            elif og in (6, 7):
                g1 = S.op("scalar", lambda sc, bk=bk, og=og: sc.activation(
                    out=gate_f[:, (og - 6) * 512:(og - 5) * 512],
                    in_=PB[bk][:, :], func=AF.Sigmoid),
                    waits=[("T", tm), ("V", kmul.get(og - 6))], inc="A")
                gate_ready[og - 6] = g1
                pb_read[bk] = ("A", g1)
            elif og in (2, 3):
                blk = og - 2
                km = S.op("vector", lambda v, bk=bk, blk=blk: v.tensor_tensor(
                    out=kg_f[:, blk * 512:(blk + 1) * 512], in0=PB[bk][:, :],
                    in1=gate_f[:, blk * 512:(blk + 1) * 512], op=ALU.mult),
                    waits=[("T", tm), ("A", gate_ready[blk])], inc="V")
                kmul[blk] = km
                pb_read[bk] = ("V", km)
                vm = S.op("vector", lambda v, blk=blk: v.tensor_scalar_min(
                    out=scr0, in0=kg_f[:, blk * 512:(blk + 1) * 512],
                    scalar1=0.0), waits=[("A", prev_exp)], inc="V")
                a1 = S.op("scalar", lambda sc: sc.activation(
                    out=scr1, in_=scr0, func=AF.Exp),
                    waits=[("V", vm), ("V", prev_stt)], inc="A")
                prev_exp = a1
                kp_done = S.op("vector", lambda v, blk=blk: v.scalar_tensor_tensor(
                    out=kp_bf[:, blk * 512:(blk + 1) * 512],
                    in0=kg_f[:, blk * 512:(blk + 1) * 512],
                    scalar=0.0, in1=scr1, op0=ALU.max, op1=ALU.add),
                    waits=[("A", a1)], inc="V")
                prev_stt = kp_done
            else:
                v_done = S.op("vector", lambda v, bk=bk, og=og: v.tensor_copy(
                    out=v_bf[:, (og - 4) * 512:(og - 3) * 512],
                    in_=PB[bk][:, :]), waits=[("T", tm)], inc="V")
                pb_read[bk] = ("V", v_done)

        if c == 0:
            dump("d_tbf", t_bf, [a])
            dump("d_qp", qp_bf, [("V", qp_done)])
            dump("d_kp", kp_bf, [("V", kp_done)])
            dump("d_gate", gate_f, [("A", gate_ready[1])])
            dump("d_v", v_bf, [("V", v_done)])
        qpT_ok = tr_make(qp_bf, qpT, ("V", qp_done))
        kpT_ok = tr_make(kp_bf, kpT, ("V", kp_done))

        kc = {}
        for dg in range(2):
            pairs = [(cst["tri_bf"], kp_bf[:, dg * 512:(dg + 1) * 512])]
            if c > 0:
                pairs = [(cst["ones_bf"][0:1, :],
                          ksum_bf[:, dg * 512:(dg + 1) * 512])] + pairs
            kc[dg] = mm_group(S, PB[4 + dg][:, :], pairs,
                              waits=[("V", kp_done), pb_read[4 + dg]])
        ks = {}
        for dg in range(2):
            ks[dg] = mm_group(S, PB[6][0:1, :],
                              [(cst["ones_bf"][:, 0:1],
                                kp_bf[:, dg * 512:(dg + 1) * 512])],
                              waits=[pb_read[6], ("T", kc[1])])
            kv = S.op("vector", lambda v, dg=dg: v.tensor_tensor(
                out=ksum[:, dg * 512:(dg + 1) * 512], in0=PB[6][0:1, :],
                in1=ksum[:, dg * 512:(dg + 1) * 512], op=ALU.add),
                waits=[("T", ks[dg])], inc="V")
            pb_read[6] = ("V", kv)
        kbf = S.op("vector", lambda v: v.tensor_copy(out=ksum_bf, in_=ksum),
                   inc="V")

        dm = None
        for dg in range(2):
            dm = S.op("vector", lambda v, dg=dg: v.tensor_tensor(
                out=scrD[:, dg * 512:(dg + 1) * 512], in0=PB[4 + dg][:, :],
                in1=qp_bf[:, dg * 512:(dg + 1) * 512], op=ALU.mult),
                waits=[("T", kc[dg])], inc="V")
            pb_read[4 + dg] = ("V", dm)
        vr = S.op("vector", lambda v: v.reduce_sum(
            out=den, in_=scrD, axis=AX.X), inc="V")
        ad = S.op("scalar", lambda sc: sc.add(out=den2, in_=den, add=1e-6),
                  waits=[("V", vr)], inc="A")
        rv = S.op("vector", lambda v: v.reciprocal(out=rden, in_=den2),
                  waits=[("A", ad)], inc="V")
        if c == 0:
            dump("d_kcum", scrD, [("V", dm)])
            dump("d_den", den2, [("A", ad)])
        if c == 1:
            dump("e_qp", qp_bf, [("V", qp_done)])
            dump("e_kp", kp_bf, [("V", kp_done)])
            dump("e_kcum", scrD, [("V", dm)])
            dump("e_den", den2, [("A", ad)])

        am = mm_group(S, PB[2][:, 0:128],
                      [(kpT[:, ic, :], qpT[:, ic, :]) for ic in range(8)],
                      waits=[kpT_ok, qpT_ok, pb_read[2]])
        ab = S.op("vector", lambda v: v.tensor_tensor(
            out=A_bf, in0=PB[2][:, 0:128], in1=cst["tri_f"], op=ALU.mult),
            waits=[("T", am)], inc="V")
        pb_read[2] = ("V", ab)

        om = {}
        for dg in range(2):
            pairs = []
            if c > 0:
                pairs += [(qpT[:, ic, :], Sbf[:, ic, dg * 512:(dg + 1) * 512])
                          for ic in range(8)]
            pairs += [(A_bf, v_bf[:, dg * 512:(dg + 1) * 512])]
            mw = [("V", ab), ("V", v_done), pb_read[dg]]
            if c > 0:
                mw.append(("A", sbf_done))
            om[dg] = mm_group(S, PB[dg][:, :], pairs, waits=mw)
        ev = None
        for dg in range(2):
            o1 = S.op("scalar", lambda sc, dg=dg: sc.mul(
                out=o1f[:, dg * 512:(dg + 1) * 512], in_=PB[dg][:, :],
                mul=rden), waits=[("T", om[dg]), ("V", rv)], inc="A")
            ev = S.op("vector", lambda v, dg=dg, par=par: v.tensor_add(
                out=xout_sb[par][:, dg * 512:(dg + 1) * 512],
                in0=o1f[:, dg * 512:(dg + 1) * 512],
                in1=xtile[par][:, dg * 512:(dg + 1) * 512]),
                waits=[("A", o1)], inc="V")
            pb_read[dg] = ("A", o1)
        if c == 0:
            dump("d_abf", A_bf, [("V", ab)])
            dump("d_o1", o1f, [("A", pb_read[1][1])])
        if c == 1:
            dump("e_o1", o1f, [("A", pb_read[1][1])])
        if c == 2:
            dump("f_sbf", Sbf[:, :, :].rearrange("p a b -> p (a b)"),
                 [("P", sbf_done)])
        last_store = S.dma("sync", out_ap[c * 128:(c + 1) * 128, :],
                           xout_sb[par], waits=[("V", ev)])

        sadds = []
        for ic in range(8):
            for dg in range(2):
                bk = 3 + (2 * ic + dg) % 3
                sm = mm_group(S, PB[bk][:, :],
                              [(kp_bf[:, ic * 128:(ic + 1) * 128],
                                v_bf[:, dg * 512:(dg + 1) * 512])],
                              waits=[pb_read[bk], ("V", kp_done),
                                     ("V", v_done)])
                sv = S.op("vector", lambda v, ic=ic, dg=dg, bk=bk: v.tensor_tensor(
                    out=Sst[:, ic, dg * 512:(dg + 1) * 512], in0=PB[bk][:, :],
                    in1=Sst[:, ic, dg * 512:(dg + 1) * 512], op=ALU.add),
                    waits=[("T", sm)], inc="V")
                pb_read[bk] = ("V", sv)
            sadds.append(sv)
            sbf_done = S.op("scalar", lambda sc, ic=ic: sc.copy(
                out=Sbf[:, ic, :], in_=Sst[:, ic, :]),
                waits=[("V", sadds[ic])], inc="A")
    ar.pop()
    return last_store


def l0attn_host_arrays(inputs):
    w = np.concatenate([inputs["L0_attn_qkv_w"], inputs["L0_attn_gate_w"]],
                       axis=1)
    brow = np.concatenate([inputs["L0_attn_qkv_b"],
                           inputs["L0_attn_gate_b"]]).reshape(1, 4096)
    return {
        "l0a_w": np.ascontiguousarray(w),
        "l0a_brow": np.ascontiguousarray(brow),
        "l0a_lng": np.ascontiguousarray(
            inputs["L0_attn_ln_g"].reshape(8, 128).T),
        "l0a_lnb": np.ascontiguousarray(
            inputs["L0_attn_ln_b"].reshape(8, 128).T),
    }


def declare_l0attn(nc):
    return {
        "l0a_w": nc.dram_tensor("l0a_w", [D, 4096], F32,
                                kind="ExternalInput").ap(),
        "l0a_brow": nc.dram_tensor("l0a_brow", [1, 4096], F32,
                                   kind="ExternalInput").ap(),
        "l0a_lng": nc.dram_tensor("l0a_lng", [128, 8], F32,
                                  kind="ExternalInput").ap(),
        "l0a_lnb": nc.dram_tensor("l0a_lnb", [128, 8], F32,
                                  kind="ExternalInput").ap(),
    }


def build_l0attn_program(dbg_names=()):
    nc = bass.Bass("TRN2", target_bir_lowering=False, debug=False)
    xin = nc.dram_tensor("xin", [T, D], F32, kind="ExternalInput").ap()
    xout = nc.dram_tensor("xout", [T, D], F32, kind="ExternalOutput").ap()
    dbg = {}
    for nm in dbg_names:
        dt = BF16 if nm in ("d_tbf", "d_qp", "d_kp", "d_v", "d_abf",
                            "e_qp", "e_kp", "f_sbf") else F32
        shp = [128, 128] if nm == "d_abf" else (
            [128, 1] if nm in ("d_den", "e_den") else (
                [128, 8 * D] if nm == "f_sbf" else [128, D]))
        dbg[nm] = nc.dram_tensor(nm, shp, dt, kind="ExternalOutput").ap()
    S = Sched(nc)
    with contextlib.ExitStack() as stk:
        ar, PB, PBT = _alloc_common(stk, nc, S)
        cst = declare_consts(S, nc, ar)
        W = declare_l0attn(nc)
        emit_l0attn(S, nc, ar, cst, W, xin, xout, PB, PBT, dbg=dbg or None)
        S.op("sync", lambda s: s.wait_ge(S.sems["DS"], S.cnt["DS"]))
        block = stk.enter_context(nc.Block())
        S.emit(block)
    return nc



DT_SSM = 0.01
SQRT_DT = float(np.sqrt(np.float32(DT_SSM)))
TWO_PI = float(2.0 * np.pi)
HALF_PI = float(0.5 * np.pi)


def emit_l1attn(S, nc, ar, cst, W, in_ap, out_ap, PB, PBT, dbg=None):
    ar.push()
    S.tensor_fence()
    uT = ar.alloc((128, 8, T), BF16)
    bu_r = ar.alloc((128, T), F32)
    bu_i = ar.alloc((128, T), F32)
    cosb = ar.alloc((128, T), F32)
    sinb = ar.alloc((128, T), F32)
    rtb = ar.alloc((128, T), F32)
    rmtb = ar.alloc((128, T), F32)
    hrb = ar.alloc((128, T), F32)
    hib = ar.alloc((128, T), F32)
    c1 = ar.alloc((128, T), F32)
    c2 = ar.alloc((128, T), F32)
    ibuf = ar.alloc((128, T), I32)
    ibuf2 = ar.alloc((128, T), I32)
    onesb = ar.alloc((128, T), F32)
    trow = ar.alloc((128, T), F32)
    h_bf = ar.alloc((128, T), BF16)
    u_bf = ar.alloc((128, D), BF16)
    xtile = [ar.alloc((128, D), F32) for _ in range(2)]
    xout_sb = [ar.alloc((128, D), F32) for _ in range(2)]
    BTb = ar.alloc((128, 8, 128), BF16)
    CTb = ar.alloc((128, D), BF16)
    scrLN = ar.alloc((128, D), F32)
    cols = ar.alloc((128, 64), F32)
    cn = {}
    for i, k in enumerate(
        ("ar0", "ai0", "denr", "deni", "numr", "numi", "m1", "d2", "rinv",
         "rinv2", "invr", "invi", "adr", "adi", "t1", "t2", "r2", "lnr2",
         "lr", "nlr", "ratio", "th", "ath", "sgn", "sivr", "sivi")):
        cn[k] = cols[:, i:i + 1]
    scrsm = ar.alloc((128, 32), F32)
    st2 = [make_stats(scrsm, j) for j in range(2)]

    btl = S.dma("gpsimd", BTb, W["l1a_BT"].rearrange("(ic p) n -> p ic n",
                                                     p=128), waits=S.guard())
    ctl = S.dma("gpsimd", CTb, W["l1a_CT"])
    arl = S.dma("sync", cn["ar0"], W["l1a_Ar"], waits=S.guard())
    ail = S.dma("sync", cn["ai0"], W["l1a_Ai"])
    trl = S.dma("sync", trow, W["c_trow"])

    hdt = DT_SSM / 2.0
    S.op("scalar", lambda sc: sc.activation(
        out=cn["denr"], in_=cn["ar0"], func=AF.Copy, scale=-hdt, bias=1.0),
        waits=[("DS", ail)])
    S.op("scalar", lambda sc: sc.mul(out=cn["deni"], in_=cn["ai0"],
                                     mul=-hdt))
    S.op("scalar", lambda sc: sc.activation(
        out=cn["numr"], in_=cn["ar0"], func=AF.Copy, scale=hdt, bias=1.0))
    S.op("scalar", lambda sc: sc.mul(out=cn["numi"], in_=cn["ai0"],
                                     mul=hdt))
    S.op("scalar", lambda sc: sc.activation(out=cn["m1"], in_=cn["denr"],
                                            func=AF.Square))
    aq = S.op("scalar", lambda sc: sc.activation(out=cn["t1"], in_=cn["deni"],
                                                 func=AF.Square), inc="A")
    vd2 = S.op("vector", lambda v: v.tensor_add(
        out=cn["d2"], in0=cn["m1"], in1=cn["t1"]), waits=[("A", aq)],
        inc="V")
    ad2 = S.op("scalar", lambda sc: sc.copy(out=cn["t2"], in_=cn["d2"]),
               waits=[("V", vd2)], inc="A")
    rv = S.op("vector", lambda v: v.reciprocal(out=cn["rinv"], in_=cn["t2"]),
              waits=[("A", ad2)], inc="V")
    S.op("scalar", lambda sc: sc.mul(out=cn["invr"], in_=cn["denr"],
                                     mul=cn["rinv"]), waits=[("V", rv)])
    S.op("scalar", lambda sc: sc.mul(out=cn["t1"], in_=cn["deni"],
                                     mul=cn["rinv"]))
    S.op("scalar", lambda sc: sc.mul(out=cn["invi"], in_=cn["t1"],
                                     mul=-1.0))
    S.op("scalar", lambda sc: sc.mul(out=cn["t1"], in_=cn["numr"],
                                     mul=cn["invr"]))
    aq = S.op("scalar", lambda sc: sc.mul(out=cn["t2"], in_=cn["numi"],
                                          mul=cn["invi"]), inc="A")
    va = S.op("vector", lambda v: v.tensor_sub(
        out=cn["adr"], in0=cn["t1"], in1=cn["t2"]), waits=[("A", aq)],
        inc="V")
    S.op("scalar", lambda sc: sc.mul(out=cn["t1"], in_=cn["numr"],
                                     mul=cn["invi"]), waits=[("V", va)])
    aq = S.op("scalar", lambda sc: sc.mul(out=cn["t2"], in_=cn["numi"],
                                          mul=cn["invr"]), inc="A")
    vb = S.op("vector", lambda v: v.tensor_add(
        out=cn["adi"], in0=cn["t1"], in1=cn["t2"]), waits=[("A", aq)],
        inc="V")
    S.op("scalar", lambda sc: sc.activation(out=cn["t1"], in_=cn["adr"],
                                            func=AF.Square),
         waits=[("V", vb)])
    aq = S.op("scalar", lambda sc: sc.activation(out=cn["t2"], in_=cn["adi"],
                                                 func=AF.Square), inc="A")
    vr2 = S.op("vector", lambda v: v.tensor_add(
        out=cn["r2"], in0=cn["t1"], in1=cn["t2"]), waits=[("A", aq)],
        inc="V")
    S.op("scalar", lambda sc: sc.activation(out=cn["lnr2"], in_=cn["r2"],
                                            func=AF.Ln), waits=[("V", vr2)])
    S.op("scalar", lambda sc: sc.mul(out=cn["lr"], in_=cn["lnr2"], mul=0.5))
    S.op("scalar", lambda sc: sc.mul(out=cn["nlr"], in_=cn["lr"], mul=-1.0))
    aadr = S.op("scalar", lambda sc: sc.copy(out=cn["t1"], in_=cn["adr"]),
                inc="A")
    rv2 = S.op("vector", lambda v: v.reciprocal(out=cn["t2"], in_=cn["t1"]),
               waits=[("A", aadr)], inc="V")
    S.op("scalar", lambda sc: sc.mul(out=cn["ratio"], in_=cn["adi"],
                                     mul=cn["t2"]), waits=[("V", rv2)])
    S.op("scalar", lambda sc: sc.activation(out=cn["th"], in_=cn["ratio"],
                                            func=AF.Arctan))
    S.op("scalar", lambda sc: sc.activation(out=cn["ath"], in_=cn["th"],
                                            func=AF.Abs))
    a3 = S.op("scalar", lambda sc: sc.activation(out=cn["sgn"], in_=cn["th"],
                                                 func=AF.Sign), inc="A")
    S.op("scalar", lambda sc: sc.mul(out=cn["sivr"], in_=cn["invr"],
                                     mul=SQRT_DT))
    sv2 = S.op("scalar", lambda sc: sc.mul(out=cn["sivi"], in_=cn["invi"],
                                           mul=SQRT_DT), inc="A")

    S.op("vector", lambda v: v.memset(onesb, 1.0))
    a1 = S.op("scalar", lambda sc: sc.mul(
        out=c1, in_=trow, mul=cn["ath"]),
        waits=[("DS", trl), ("A", sv2)])
    a2 = S.op("scalar", lambda sc: sc.activation(
        out=c2, in_=trow, func=AF.Identity, scale=cn["ath"],
        bias=float(0.5 * np.pi)), inc="A")
    S.op("vector", lambda v: v.tensor_scalar_mul(
        out=hrb, in0=c1, scalar1=float(1.0 / TWO_PI)), waits=[("A", a2)])
    S.op("vector", lambda v: v.tensor_scalar_mul(
        out=hib, in0=c2, scalar1=float(1.0 / TWO_PI)))
    S.op("vector", lambda v: v.tensor_copy(out=ibuf, in_=hrb))
    S.op("vector", lambda v: v.tensor_copy(out=ibuf2, in_=hib))
    S.op("vector", lambda v: v.tensor_copy(out=hrb, in_=ibuf))
    S.op("vector", lambda v: v.tensor_copy(out=hib, in_=ibuf2))
    S.op("vector", lambda v: v.scalar_tensor_tensor(
        out=hrb, in0=hrb, scalar=-TWO_PI, in1=c1, op0=ALU.mult,
        op1=ALU.add))
    vm = S.op("vector", lambda v: v.scalar_tensor_tensor(
        out=hib, in0=hib, scalar=-TWO_PI, in1=c2, op0=ALU.mult,
        op1=ALU.add), inc="V")
    art = S.op("scalar", lambda sc: sc.activation(
        out=rtb, in_=trow, func=AF.Exp, scale=cn["lr"]),
        waits=[("V", vm)], inc="A")
    armt = S.op("scalar", lambda sc: sc.activation(
        out=rmtb, in_=trow, func=AF.Exp, scale=cn["nlr"]), inc="A")
    S.op("scalar", lambda sc: sc.activation(
        out=sinb, in_=hrb, func=AF.Sin))
    am2 = S.op("scalar", lambda sc: sc.activation(
        out=cosb, in_=hib, func=AF.Sin), inc="A")
    sgn_done = S.op("scalar", lambda sc: sc.mul(
        out=sinb, in_=sinb, mul=cn["sgn"]), inc="A")

    pb_read = {i: ("V", S.cnt["V"]) for i in range(8)}
    aouts = {}
    last_store = None
    for i in range(NTILE):
        ld = S.dma("sync", xtile[i % 2], in_ap[i * 128:(i + 1) * 128, :],
                   waits=([aouts[i - 2]] if i >= 2 else S.guard()))
        pw = [("DS", ld)]
        if i >= 1:
            pw.append(aouts[i - 1])
        a = ln_tile(S, xtile[i % 2], u_bf, scrLN, st2[i % 2], pre_waits=pw)
        aouts[i] = a

        def trf(t, i=i):
            last = None
            for ic in range(8):
                last = t.transpose(
                    out=PBT[:, ic * 128:(ic + 1) * 128],
                    in_=u_bf[:, ic * 128:(ic + 1) * 128],
                    identity=cst["ident_bf"])
            return last
        tm = S.op("tensor", trf, waits=[a, pb_read[7],
                                        ("DG", cst["loaded_g"])], inc="T")
        pv = S.op("vector", lambda v, i=i: v.tensor_copy(
            out=uT[:, :, i * 128:(i + 1) * 128],
            in_=PBT[:, :].rearrange("p (ic q) -> p ic q", ic=8)),
            waits=[("T", tm)], inc="V")
        pb_read[7] = ("V", pv)
    uT_ok = S.cnt["V"]

    bu_done = None
    for tg in range(4):
        bk = tg % 2
        tm = mm_group(S, PB[bk][:, :],
                      [(BTb[:, ic, :], uT[:, ic, tg * 512:(tg + 1) * 512])
                       for ic in range(8)],
                      waits=[("V", uT_ok), ("DG", btl), pb_read[bk]])
        S.op("scalar", lambda sc, bk=bk, tg=tg: sc.mul(
            out=bu_r[:, tg * 512:(tg + 1) * 512], in_=PB[bk][:, :],
            mul=cn["sivr"]), waits=[("T", tm), ("A", sv2)])
        bu_done = S.op("scalar", lambda sc, bk=bk, tg=tg: sc.mul(
            out=bu_i[:, tg * 512:(tg + 1) * 512], in_=PB[bk][:, :],
            mul=cn["sivi"]), inc="A")
        pb_read[bk] = ("A", bu_done)

    wv = [("A", bu_done), ("A", sgn_done), ("A", am2), ("A", armt)]
    S.op("vector", lambda v: v.tensor_mul(out=c1, in0=cosb, in1=rmtb),
         waits=wv)
    S.op("vector", lambda v: v.tensor_mul(out=c2, in0=sinb, in1=rmtb))
    S.op("vector", lambda v: v.tensor_mul(out=hrb, in0=bu_r, in1=c1))
    S.op("vector", lambda v: v.tensor_mul(out=ibuf.bitcast(F32), in0=bu_i,
                                          in1=c2))
    S.op("vector", lambda v: v.tensor_add(out=hrb, in0=hrb,
                                          in1=ibuf.bitcast(F32)))
    S.op("vector", lambda v: v.tensor_mul(out=hib, in0=bu_i, in1=c1))
    S.op("vector", lambda v: v.tensor_mul(out=ibuf.bitcast(F32), in0=bu_r,
                                          in1=c2))
    S.op("vector", lambda v: v.tensor_sub(out=hib, in0=hib,
                                          in1=ibuf.bitcast(F32)))
    S.op("vector", lambda v: v.tensor_tensor_scan(
        out=bu_r, data0=onesb, data1=hrb, initial=0.0, op0=ALU.mult,
        op1=ALU.add))
    S.op("vector", lambda v: v.tensor_tensor_scan(
        out=bu_i, data0=onesb, data1=hib, initial=0.0, op0=ALU.mult,
        op1=ALU.add))
    S.op("vector", lambda v: v.tensor_mul(out=c1, in0=cosb, in1=rtb))
    S.op("vector", lambda v: v.tensor_mul(out=c2, in0=sinb, in1=rtb))
    S.op("vector", lambda v: v.tensor_mul(out=hrb, in0=bu_r, in1=c1))
    S.op("vector", lambda v: v.tensor_mul(out=hib, in0=bu_i, in1=c2))
    S.op("vector", lambda v: v.tensor_sub(out=hrb, in0=hrb, in1=hib))
    hv = S.op("vector", lambda v: v.tensor_copy(out=h_bf, in_=hrb), inc="V")
    if dbg:
        for nm, ap in (("g_cols", cols), ("g_rmt", rmtb), ("g_rt", rtb),
                       ("g_cos", cosb), ("g_sin", sinb), ("g_bur", bu_r),
                       ("g_hr", hrb)):
            if nm in dbg:
                S.dma("sync", dbg[nm], ap, waits=[("V", hv)])

    ev = None
    for i in range(NTILE):
        bks = (2 + (i % 2) * 2, 3 + (i % 2) * 2)
        tm = {}
        for dg in range(2):
            tm[dg] = mm_group(S, PB[bks[dg]][:, :],
                              [(h_bf[:, i * 128:(i + 1) * 128],
                                CTb[:, dg * 512:(dg + 1) * 512])],
                              waits=[("V", hv), ("DG", ctl),
                                     pb_read[bks[dg]]])
        ld = S.dma("sync", xtile[i % 2], in_ap[i * 128:(i + 1) * 128, :],
                   waits=[("DS", last_store), aouts[NTILE - 1]])
        for dg in range(2):
            ev = S.op("vector", lambda v, i=i, dg=dg, bks=bks: v.tensor_add(
                out=xout_sb[i % 2][:, dg * 512:(dg + 1) * 512],
                in0=PB[bks[dg]][:, :],
                in1=xtile[i % 2][:, dg * 512:(dg + 1) * 512]),
                waits=[("T", tm[dg]), ("DS", ld)], inc="V")
            pb_read[bks[dg]] = ("V", ev)
        last_store = S.dma("sync", out_ap[i * 128:(i + 1) * 128, :],
                           xout_sb[i % 2], waits=[("V", ev)])
    ar.pop()
    return last_store


def l1attn_host_arrays(inputs):
    return {
        "l1a_BT": np.ascontiguousarray(inputs["L1_attn_B"].T),
        "l1a_CT": np.ascontiguousarray(inputs["L1_attn_C"].T),
        "l1a_Ar": np.ascontiguousarray(
            inputs["L1_attn_A_real"].reshape(128, 1)),
        "l1a_Ai": np.ascontiguousarray(
            inputs["L1_attn_A_imag"].reshape(128, 1)),
    }


def declare_l1attn(nc):
    return {
        "l1a_BT": nc.dram_tensor("l1a_BT", [D, 128], F32,
                                 kind="ExternalInput").ap(),
        "l1a_CT": nc.dram_tensor("l1a_CT", [128, D], F32,
                                 kind="ExternalInput").ap(),
        "l1a_Ar": nc.dram_tensor("l1a_Ar", [128, 1], F32,
                                 kind="ExternalInput").ap(),
        "l1a_Ai": nc.dram_tensor("l1a_Ai", [128, 1], F32,
                                 kind="ExternalInput").ap(),
        "c_trow": nc.dram_tensor("c_trow", [128, T], F32,
                                 kind="ExternalInput").ap(),
    }


def build_l1attn_program(dbg_names=()):
    nc = bass.Bass("TRN2", target_bir_lowering=False, debug=False)
    xin = nc.dram_tensor("xin", [T, D], F32, kind="ExternalInput").ap()
    xout = nc.dram_tensor("xout", [T, D], F32, kind="ExternalOutput").ap()
    dbg = {}
    for nm in dbg_names:
        shp = [128, 64] if nm == "g_cols" else [128, T]
        dbg[nm] = nc.dram_tensor(nm, shp, F32, kind="ExternalOutput").ap()
    S = Sched(nc)
    with contextlib.ExitStack() as stk:
        ar, PB, PBT = _alloc_common(stk, nc, S)
        cst = declare_consts(S, nc, ar)
        W = declare_l1attn(nc)
        emit_l1attn(S, nc, ar, cst, W, xin, xout, PB, PBT, dbg=dbg or None)
        S.op("sync", lambda s: s.wait_ge(S.sems["DS"], S.cnt["DS"]))
        block = stk.enter_context(nc.Block())
        S.emit(block)
    return nc



INV_SQRT32 = float(1.0 / np.sqrt(32.0))


def emit_l1ffn(S, nc, ar, cst, W, in_ap, out_ap, xb2, PB, PBT, upto=None,
               dbg=None):
    def bail(extra=()):
        vz = S.op("vector", lambda v: v.memset(xtile[0], 0.0),
                  waits=S.guard(), inc="V")
        st = S.dma("sync", out_ap[0:128, :], xtile[0], waits=[("V", vz)])
        if dbg:
            for nm, ap in extra:
                if nm in dbg:
                    st = S.dma("sync", dbg[nm], ap, waits=[("V", vz)])
        return st
    S.tensor_fence()
    ar.push()
    tTreg = ar.alloc((128, 8, T), BF16)
    hT = tTreg
    tT = tTreg
    xtile = [ar.alloc((128, D), F32) for _ in range(2)]
    gate_all = ar.alloc((128, 16, 8), F32)
    ar.push()
    xlnb = ar.alloc((128, D), F32)
    t0f = ar.alloc((128, D), F32)
    h_bf = ar.alloc((128, D), BF16)
    scrLN = ar.alloc((128, D), F32)
    g2rep = ar.alloc((128, D), F32)
    qss = ar.alloc((128, 16, 16), F32)
    kss = ar.alloc((128, 16, 16), F32)
    biascols = ar.alloc((128, 16), F32)
    mrow = ar.alloc((128, 64), F32)
    rowp = ar.alloc((128, 1056), F32)
    rowbf = rowp.bitcast(BF16)
    g2row = rowp[0:1, 0:1024]
    nbrow_bf = rowbf[0:1, 2048:2080]
    scrsm = ar.alloc((128, 32), F32)
    st2 = [make_stats(scrsm, j) for j in range(2)]
    gsc = ar.alloc((128, 64), F32)
    e_sb = gsc[:, 0:8]
    ism = gsc[:, 8:16]
    em_s = gsc[:, 16:24]
    ind_s = gsc[:, 24:32]
    ge_s = gsc[:, 32:40]
    nmx8 = gsc[:, 40:41]
    m1c = gsc[:, 41:42]
    m2c = gsc[:, 42:43]
    s12c = gsc[:, 43:44]
    rsumc = gsc[:, 44:45]
    m1g = gsc[:, 45:46]
    m2g = gsc[:, 46:47]
    s12g = gsc[:, 47:48]
    rsg = gsc[:, 48:49]
    m1v8 = gsc[:, 56:64]
    m2v8 = mrow[:, 32:40]
    rsv8 = mrow[:, 40:48]

    wdsb = ar.alloc((128, 8, 512), BF16)
    wqsb = ar.alloc((128, 4, 512), BF16)
    wksb = ar.alloc((128, 4, 512), BF16)
    wvsb = ar.alloc((128, 4, 512), BF16)
    wupsb = ar.alloc((128, 4, D), BF16)
    rsb = ar.alloc((128, 8, 8), BF16)
    g1m = ar.alloc((128, 8), F32)
    wdl = S.dma("gpsimd", wdsb,
                W["l1f_wdown"].rearrange("(ic p) l -> p ic l", p=128),
                waits=S.guard())
    S.dma("gpsimd", wqsb, W["l1f_wq"].rearrange("(lc p) o -> p lc o", p=128))
    S.dma("gpsimd", wksb, W["l1f_wk"].rearrange("(lc p) o -> p lc o", p=128))
    wvl = S.dma("gpsimd", wvsb,
                W["l1f_wv"].rearrange("(lc p) o -> p lc o", p=128))
    wul = S.dma("gpsimd", wupsb,
                W["l1f_wup"].rearrange("(lc p) d -> p lc d", p=128))
    rl = S.dma("gpsimd", rsb, W["l1f_router"].rearrange(
        "(ic p) e -> p ic e", p=128))
    g1l = S.dma("sync", g1m, W["l1f_g1m"], waits=S.guard())
    g2l = S.dma("sync", g2row, W["l1f_g2row"])

    wsc = None
    for ic in range(8):
        wsc = S.op("vector", lambda v, ic=ic: v.tensor_scalar(
            out=wdsb[:, ic, :], in0=wdsb[:, ic, :],
            scalar1=g1m[:, ic:ic + 1], scalar2=C0, op0=ALU.mult,
            op1=ALU.mult),
            waits=[("DG", wdl)] + ([("DS", g1l)] if ic == 0 else []),
            inc="V")

    pb_read = {i: ("V", S.cnt["V"]) for i in range(8)}
    for dg in range(2):
        tm = mm_group(S, PB[6][:, 0:512],
                      [(cst["ones_f"][0:1, :],
                        g2row[0:1, dg * 512:(dg + 1) * 512])],
                      waits=[("DS", g2l), ("DG", cst["loaded_g"]),
                             pb_read[6]])
        g2d = S.op("vector", lambda v, dg=dg: v.tensor_copy(
            out=g2rep[:, dg * 512:(dg + 1) * 512], in0=PB[6][:, 0:512])
            if False else v.tensor_scalar_mul(
            out=g2rep[:, dg * 512:(dg + 1) * 512], in0=PB[6][:, 0:512],
            scalar1=1.0), waits=[("T", tm)], inc="V")
        pb_read[6] = ("V", g2d)

    aouts = {}
    last_st2 = None
    for i in range(NTILE):
        ld = S.dma("sync", xtile[i % 2], in_ap[i * 128:(i + 1) * 128, :],
                   waits=([aouts[i - 2]] if i >= 2 else S.guard()))
        pw = [("DS", ld)]
        if i >= 1:
            pw.append(aouts[i - 1])
        a = ln_tile(S, xtile[i % 2], xlnb, scrLN, st2[i % 2], pre_waits=pw,
                    out_waits=[("DS", last_st2)])
        aouts[i] = a
        last_st2 = S.dma("sync", xb2[i * 128:(i + 1) * 128, :], xlnb,
                         waits=[a])
        hv = S.op("vector", lambda v: v.tensor_copy(out=h_bf, in_=xlnb),
                  waits=[a], inc="V")

        def trf(t, i=i):
            last = None
            for ic in range(8):
                last = t.transpose(
                    out=PBT[:, ic * 128:(ic + 1) * 128],
                    in_=h_bf[:, ic * 128:(ic + 1) * 128],
                    identity=cst["ident_bf"])
            return last
        tm = S.op("tensor", trf, waits=[("V", hv), pb_read[7]], inc="T")
        pv = S.op("vector", lambda v, i=i: v.tensor_copy(
            out=hT[:, :, i * 128:(i + 1) * 128],
            in_=PBT[:, :].rearrange("p (ic q) -> p ic q", ic=8)),
            waits=[("T", tm)], inc="V")
        pb_read[7] = ("V", pv)
        aouts[i] = ("V", pv)
    hT_ok = S.cnt["V"]
    if upto == "p1":
        return bail()

    cTreg = ar.alloc((128, 4, T), BF16)
    cT = cTreg
    oT = cTreg
    qT = ar.alloc((128, 4, T), BF16)
    kT = ar.alloc((128, 4, T), BF16)
    v_tok = ar.alloc((128, 16, 512), BF16)
    qk_bf = ar.alloc((128, 512), BF16)
    rsc = [ar.alloc((128, 256), F32) for _ in range(2)]
    costb = [ar.alloc((128, 256), F32) for _ in range(2)]
    sintb = [ar.alloc((128, 256), F32) for _ in range(2)]
    Abuf = ar.alloc((128, 4, 512), BF16)
    rl_sb = ar.alloc((128, 512), F32)
    rl2_sb = ar.alloc((128, 512), F32)

    last_ev = None
    for lc in range(4):
        for tg in range(4):
            bk = (lc * 4 + tg) % 3
            mw = [pb_read[bk], ("V", wsc)]
            if lc == 0 and tg == 0:
                mw.append(("V", hT_ok))
            tm = mm_group(S, PB[bk][:, :],
                          [(wdsb[:, ic, lc * 128:(lc + 1) * 128],
                            hT[:, ic, tg * 512:(tg + 1) * 512])
                           for ic in range(8)], waits=mw)
            last_ev = S.op("vector", lambda v, lc=lc, tg=tg, bk=bk: v.tensor_copy(
                out=cT[:, lc, tg * 512:(tg + 1) * 512], in_=PB[bk][:, :]),
                waits=[("T", tm)], inc="V")
            pb_read[bk] = ("V", last_ev)
    cT_ok = S.cnt["V"]
    if upto == "p2":
        return bail([("d_cT", cT[:, :, :].rearrange("p a b -> p (a b)")), ("d_hT", hT[:, :, :].rearrange("p a b -> p (a b)")), ("d_st", scrsm)])

    prev_tr = {}
    vvv = None
    for i in range(NTILE):
        cl = S.dma("sync", costb[i % 2], W["c_cos"][:, i * 256:(i + 1) * 256],
                   waits=[("V", prev_tr.get(("k", i - 2)))])
        sl = S.dma("sync", sintb[i % 2], W["c_sin"][:, i * 256:(i + 1) * 256])
        for which, wsb, dstT in (("q", wqsb, qT), ("k", wksb, kT)):
            bk = 3 if which == "q" else 4
            mw = [pb_read[bk], ("DG", wvl)]
            if i == 0 and which == "q":
                mw.append(("V", cT_ok))
            tm = mm_group(S, PB[bk][:, :],
                          [(cT[:, lc, i * 128:(i + 1) * 128],
                            wsb[:, lc, :]) for lc in range(4)], waits=mw)
            ps = PB[bk][:, :]
            S.op("vector", lambda v, ps=ps, i=i: v.tensor_tensor(
                out=rsc[0], in0=ps.rearrange("p (c two) -> p c two", two=2
                                             )[:, :, 0],
                in1=costb[i % 2], op=ALU.mult),
                waits=[("T", tm), ("DS", sl)])
            S.op("vector", lambda v, ps=ps, i=i: v.tensor_tensor(
                out=rsc[1], in0=ps.rearrange("p (c two) -> p c two", two=2
                                             )[:, :, 1],
                in1=sintb[i % 2], op=ALU.mult))
            S.op("vector", lambda v: v.tensor_sub(
                out=qk_bf.rearrange("p (c two) -> p c two", two=2)[:, :, 0],
                in0=rsc[0], in1=rsc[1]))
            S.op("vector", lambda v, ps=ps, i=i: v.tensor_tensor(
                out=rsc[0], in0=ps.rearrange("p (c two) -> p c two", two=2
                                             )[:, :, 0],
                in1=sintb[i % 2], op=ALU.mult))
            S.op("vector", lambda v, ps=ps, i=i: v.tensor_tensor(
                out=rsc[1], in0=ps.rearrange("p (c two) -> p c two", two=2
                                             )[:, :, 1],
                in1=costb[i % 2], op=ALU.mult))
            rdone = S.op("vector", lambda v: v.tensor_add(
                out=qk_bf.rearrange("p (c two) -> p c two", two=2)[:, :, 1],
                in0=rsc[0], in1=rsc[1]), inc="V")
            pb_read[bk] = ("V", rdone)
            dst_ss = qss if which == "q" else kss
            S.op("vector", lambda v, dst_ss=dst_ss, i=i: v.reduce_sum(
                out=dst_ss[:, :, i:i + 1],
                in_=qk_bf.rearrange("p (h hd) -> p h hd", h=16), axis=AX.X,
                apply_absolute_value=False, negate=False))

            def trf2(t, i=i, dstT=dstT):
                last = None
                for lc in range(4):
                    last = t.transpose(
                        out=PBT[:, lc * 128:(lc + 1) * 128],
                        in_=qk_bf[:, lc * 128:(lc + 1) * 128],
                        identity=cst["ident_bf"])
                return last
            tr = S.op("tensor", trf2, waits=[("V", rdone), pb_read[7]],
                      inc="T")
            pv = S.op("vector", lambda v, i=i, dstT=dstT: v.tensor_copy(
                out=dstT[:, :, i * 128:(i + 1) * 128],
                in_=PBT[:, 0:512].rearrange("p (lc q) -> p lc q", lc=4)),
                waits=[("T", tr)], inc="V")
            pb_read[7] = ("V", pv)
            prev_tr[(which, i)] = pv
        tm = mm_group(S, PB[5][:, :],
                      [(cT[:, lc, i * 128:(i + 1) * 128],
                        wvsb[:, lc, :]) for lc in range(4)],
                      waits=[pb_read[5]])
        vvv = S.op("vector", lambda v, i=i: v.tensor_copy(
            out=v_tok[:, i, :], in_=PB[5][:, :]), waits=[("T", tm)],
            inc="V")
        pb_read[5] = ("V", vvv)
    qkv_ok = S.cnt["V"]
    if upto == "p3":
        return bail([("d_qT", qT[:, :, :].rearrange("p a b -> p (a b)")), ("d_kT", kT[:, :, :].rearrange("p a b -> p (a b)")), ("d_v", v_tok[:, :, :].rearrange("p a b -> p (a b)"))])

    vq = S.op("vector", lambda v: v.reduce_max(
        out=mrow[:, 0:16].rearrange("p (h one) -> p h one", one=1),
        in_=qss[:, :, :], axis=AX.X), waits=[("V", qkv_ok)])
    vk = S.op("vector", lambda v: v.reduce_max(
        out=mrow[:, 16:32].rearrange("p (h one) -> p h one", one=1),
        in_=kss[:, :, :], axis=AX.X), inc="V")
    mbf = mrow.bitcast(BF16)
    ab1 = S.op("scalar", lambda sc: sc.copy(out=mbf[:, 64:96],
                                            in_=mrow[:, 0:32]),
               waits=[("V", vk)], inc="A")
    trm = S.op("tensor", lambda t: t.transpose(
        out=PBT[0:16, 0:128], in_=mbf[:, 64:80],
        identity=cst["ident_bf"]), waits=[("A", ab1), pb_read[7]])
    trm = S.op("tensor", lambda t: t.transpose(
        out=PBT[0:16, 128:256], in_=mbf[:, 80:96],
        identity=cst["ident_bf"]), inc="T")
    vgq = S.op("vector", lambda v: v.reduce_max(
        out=gsc[0:16, 49:50], in_=PBT[0:16, 0:128], axis=AX.X),
        waits=[("T", trm)])
    vgk = S.op("vector", lambda v: v.reduce_max(
        out=gsc[0:16, 50:51], in_=PBT[0:16, 128:256], axis=AX.X), inc="V")
    pb_read[7] = ("V", vgk)
    aq = S.op("scalar", lambda sc: sc.copy(out=gsc[0:16, 51:52],
                                           in_=gsc[0:16, 49:50]),
              waits=[("V", vgk)])
    aq = S.op("scalar", lambda sc: sc.copy(out=gsc[0:16, 52:53],
                                           in_=gsc[0:16, 50:51]), inc="A")
    vm2 = S.op("vector", lambda v: v.tensor_mul(
        out=gsc[0:16, 53:54], in0=gsc[0:16, 51:52], in1=gsc[0:16, 52:53]),
        waits=[("A", aq)], inc="V")
    S.op("scalar", lambda sc: sc.activation(
        out=gsc[0:16, 54:55], in_=gsc[0:16, 53:54], func=AF.Sqrt),
        waits=[("V", vm2)])
    S.op("scalar", lambda sc: sc.mul(out=gsc[0:16, 55:56],
                                     in_=gsc[0:16, 54:55], mul=INV_SQRT32))
    S.op("scalar", lambda sc: sc.mul(out=gsc[0:16, 56:57],
                                     in_=gsc[0:16, 55:56], mul=-1.0))
    anb = S.op("scalar", lambda sc: sc.copy(
        out=gsc.bitcast(BF16)[0:16, 120:121], in_=gsc[0:16, 56:57]),
        inc="A")
    trn = S.op("tensor", lambda t: t.transpose(
        out=PBT[0:1, 0:16], in_=gsc.bitcast(BF16)[0:16, 120:121],
        identity=cst["ident_bf"][0:16, 0:16]), waits=[("A", anb), pb_read[7]], inc="T")
    vnb = S.op("vector", lambda v: v.tensor_copy(
        out=nbrow_bf[0:1, 0:16], in_=PBT[0:1, 0:16]), waits=[("T", trn)],
        inc="V")
    pb_read[7] = ("V", vnb)
    def bc_fn(t):
        last = None
        for h in range(16):
            last = t.matmul(out=PB[6][:, h:h + 1],
                            lhsT=cst["ones_bf"][0:1, :],
                            rhs=nbrow_bf[0:1, h:h + 1],
                            start=True, stop=True)
        return last
    tmb = S.op("tensor", bc_fn, waits=[("V", vnb), pb_read[6]], inc="T")
    vbc = S.op("vector", lambda v: v.tensor_copy(
        out=biascols, in_=PB[6][:, 0:16]), waits=[("T", tmb)], inc="V")
    pb_read[6] = ("V", vbc)
    if upto == "p4":
        return bail([("d_bias", biascols)])

    for hg in range(4):
        for tg in range(4):
            nchunks = 4 * tg + 4
            av_done = None
            for j in range(nchunks):
                r = j - 4 * tg
                t_lo = max(r, 0) * 128
                n_t = 512 - t_lo
                def smm(t, hg=hg, tg=tg, j=j, t_lo=t_lo, n_t=n_t):
                    last = None
                    for a in range(4):
                        last = t.matmul(
                            out=PB[a][:, t_lo:512],
                            lhsT=kT[32 * a:32 * a + 32, hg,
                                    j * 128:(j + 1) * 128],
                            rhs=qT[32 * a:32 * a + 32, hg,
                                   tg * 512 + t_lo:(tg + 1) * 512],
                            start=True, stop=True,
                            tile_position=(32 * a, 0))
                    return last
                mw = [pb_read[a] for a in range(4)]
                if hg == 0 and tg == 0 and j == 0:
                    mw.append(("V", qkv_ok))
                tms = S.op("tensor", smm, waits=mw, inc="T")
                aev = {}
                for a in range(4):
                    aev[a] = S.op("scalar", lambda sc, a=a, hg=hg, t_lo=t_lo: sc.activation(
                        out=Abuf[:, a, t_lo:512], in_=PB[a][:, t_lo:512],
                        func=AF.Exp, scale=INV_SQRT32,
                        bias=biascols[:, 4 * hg + a:4 * hg + a + 1]),
                        waits=[("T", tms), ("V", vbc)] if a == 0
                        else [], inc="A")
                    pb_read[a] = ("A", aev[a])
                msk = None
                if r >= 0:
                    msk = None
                    for a in range(4):
                        msk = S.op("vector", lambda v, a=a, t_lo=t_lo: v.tensor_tensor(
                            out=Abuf[:, a, t_lo:t_lo + 128],
                            in0=Abuf[:, a, t_lo:t_lo + 128],
                            in1=cst["tri_f"], op=ALU.mult),
                            waits=[("A", aev[3])], inc="V")
                def avmm(t, j=j, hg=hg, t_lo=t_lo):
                    last = None
                    for a in range(4):
                        last = t.matmul(
                            out=PB[4][32 * a:32 * a + 32, t_lo:512],
                            lhsT=v_tok[:, j, (4 * hg + a) * 32:
                                       (4 * hg + a) * 32 + 32],
                            rhs=Abuf[:, a, t_lo:512],
                            start=(j == 0), stop=False,
                            tile_position=(0, 32 * a))
                    for a in range(4):
                        last = t.matmul(
                            out=PB[5][32 * a:32 * a + 32, t_lo:512],
                            lhsT=cst["ones_bf"][:, 0:32],
                            rhs=Abuf[:, a, t_lo:512],
                            start=(j == 0), stop=False,
                            tile_position=(0, 32 * a))
                    return last
                avw = [("A", aev[3])]
                if msk is not None:
                    avw.append(("V", msk))
                if j == 0:
                    avw += [pb_read[4], pb_read[5]]
                av_done = S.op("tensor", avmm, waits=avw, inc="T")
            vrl = S.op("vector", lambda v: v.reciprocal(
                out=rl_sb, in_=PB[5][:, :]), waits=[("T", av_done)],
                inc="V")
            pb_read[5] = ("V", vrl)
            arl2 = S.op("scalar", lambda sc: sc.copy(out=rl2_sb, in_=rl_sb),
                        waits=[("V", vrl)], inc="A")
            vo = S.op("vector", lambda v, hg=hg, tg=tg: v.tensor_tensor(
                out=oT[:, hg, tg * 512:(tg + 1) * 512], in0=PB[4][:, :],
                in1=rl2_sb, op=ALU.mult), waits=[("A", arl2)], inc="V")
            pb_read[4] = ("V", vo)
    oT_ok = S.cnt["V"]
    if upto == "p5":
        return bail([("d_oT", oT[:, :, :].rearrange("p a b -> p (a b)"))])

    xmid_sb = ar.alloc((128, D), F32)
    tbf2 = ar.alloc((128, D), BF16)
    xt6 = [ar.alloc((128, D), F32) for _ in range(2)]
    last_xm = None
    rt_done = {}
    for i in range(NTILE):
        mw = [pb_read[6], ("DG", wul)]
        if i == 0:
            mw.append(("V", oT_ok))
        tm6 = mm_group(S, PB[6][:, :],
                       [(oT[:, hg, i * 128:(i + 1) * 128],
                         wupsb[:, hg, 0:512]) for hg in range(4)], waits=mw)
        tm0 = mm_group(S, PB[0][:, :],
                       [(oT[:, hg, i * 128:(i + 1) * 128],
                         wupsb[:, hg, 512:1024]) for hg in range(4)],
                       waits=[pb_read[0]])
        xl = S.dma("sync", xt6[i % 2], xb2[i * 128:(i + 1) * 128, :],
                   waits=[("DS", last_xm), rt_done.get(i - 2)])
        S.op("vector", lambda v, i=i: v.tensor_add(
            out=xmid_sb[:, 0:512], in0=PB[6][:, :],
            in1=xt6[i % 2][:, 0:512]),
            waits=[("T", tm0), ("DS", xl)])
        xme = S.op("vector", lambda v, i=i: v.tensor_add(
            out=xmid_sb[:, 512:1024], in0=PB[0][:, :],
            in1=xt6[i % 2][:, 512:1024]), inc="V")
        pb_read[6] = ("V", xme)
        pb_read[0] = ("V", xme)
        last_xm = S.dma("sync", xb2[i * 128:(i + 1) * 128, :], xmid_sb,
                        waits=[("V", xme)])
        a2 = ln_tile(S, xmid_sb, t0f, scrLN, st2[i % 2], rms_only=True,
                     pre_waits=[("V", xme)], out_waits=[("DS", last_xm)])
        vt = S.op("vector", lambda v: v.tensor_mul(
            out=tbf2, in0=t0f, in1=g2rep), waits=[a2], inc="V")

        def trf3(t, i=i):
            last = None
            for ic in range(8):
                last = t.transpose(
                    out=PBT[:, ic * 128:(ic + 1) * 128],
                    in_=tbf2[:, ic * 128:(ic + 1) * 128],
                    identity=cst["ident_bf"])
            return last
        tr = S.op("tensor", trf3, waits=[("V", vt), pb_read[7]], inc="T")
        pv = S.op("vector", lambda v, i=i: v.tensor_copy(
            out=tT[:, :, i * 128:(i + 1) * 128],
            in_=PBT[:, :].rearrange("p (ic q) -> p ic q", ic=8)),
            waits=[("T", tr)], inc="V")
        pb_read[7] = ("V", pv)
        rt_done[i] = ("V", pv)
        tmr = mm_group(S, PB[1][:, 0:8],
                       [(tT[:, ic, i * 128:(i + 1) * 128], rsb[:, ic, :])
                        for ic in range(8)],
                       waits=[pb_read[1], ("V", pv), ("DG", rl)])
        v1 = S.op("vector", lambda v: v.reduce_max(
            out=nmx8, in_=PB[1][:, 0:8], axis=AX.X, negate=True),
            waits=[("T", tmr)], inc="V")
        ae = S.op("scalar", lambda sc: sc.activation(
            out=e_sb, in_=PB[1][:, 0:8], func=AF.Exp, bias=nmx8),
            waits=[("V", v1)], inc="A")
        pb_read[1] = ("A", ae)
        vm1 = S.op("vector", lambda v: v.reduce_max(
            out=m1c, in_=e_sb, axis=AX.X), waits=[("A", ae)], inc="V")
        am1b = S.op("scalar", lambda sc: sc.mul(
            out=m1v8, in_=cst["ones_f"][:, 0:8], mul=m1c),
            waits=[("V", vm1)], inc="A")
        vi = S.op("vector", lambda v: v.tensor_tensor(
            out=ism, in0=e_sb, in1=m1v8, op=ALU.is_ge),
            waits=[("A", am1b)], inc="V")
        hb = S.op("scalar", lambda sc: sc.nop(), waits=[("V", vi)], inc="A")
        vi = S.op("vector", lambda v: v.tensor_scalar(
            out=ism, in0=ism, scalar1=-1.0, scalar2=1.0, op0=ALU.mult,
            op1=ALU.add), waits=[("A", hb)], inc="V")
        hb = S.op("scalar", lambda sc: sc.nop(), waits=[("V", vi)], inc="A")
        vem = S.op("vector", lambda v: v.tensor_mul(
            out=em_s, in0=e_sb, in1=ism), waits=[("A", hb)], inc="V")
        hb = S.op("scalar", lambda sc: sc.nop(), waits=[("V", vem)],
                  inc="A")
        vm2b = S.op("vector", lambda v: v.reduce_max(
            out=m2c, in_=em_s, axis=AX.X), waits=[("A", hb)], inc="V")
        am2b = S.op("scalar", lambda sc: sc.mul(
            out=m2v8, in_=cst["ones_f"][:, 0:8], mul=m2c),
            waits=[("V", vm2b)], inc="A")
        vi = S.op("vector", lambda v: v.tensor_tensor(
            out=ind_s, in0=e_sb, in1=m2v8, op=ALU.is_ge),
            waits=[("A", am2b)], inc="V")
        hb = S.op("scalar", lambda sc: sc.nop(), waits=[("V", vi)], inc="A")
        vge = S.op("vector", lambda v: v.tensor_mul(
            out=ge_s, in0=e_sb, in1=ind_s), waits=[("A", hb)], inc="V")
        hb = S.op("scalar", lambda sc: sc.nop(), waits=[("V", vge)],
                  inc="A")
        vs12 = S.op("vector", lambda v: v.reduce_sum(
            out=s12c, in_=ge_s, axis=AX.X), waits=[("A", hb)], inc="V")
        as12 = S.op("scalar", lambda sc: sc.copy(out=s12g, in_=s12c),
                    waits=[("V", vs12)], inc="A")
        vrs = S.op("vector", lambda v: v.reciprocal(out=rsg, in_=s12g),
                   waits=[("A", as12)], inc="V")
        arsb = S.op("scalar", lambda sc: sc.mul(
            out=rsv8, in_=cst["ones_f"][:, 0:8], mul=rsg),
            waits=[("V", vrs)], inc="A")
        vg = S.op("vector", lambda v, i=i: v.tensor_tensor(
            out=gate_all[:, i, :], in0=ge_s, in1=rsv8, op=ALU.mult),
            waits=[("A", arsb)], inc="V")
    gate_ok = S.cnt["V"]
    tT_ok2 = S.cnt["V"]
    if upto == "p6":
        return bail([("d_gate", gate_all[:, :, :].rearrange("p a b -> p (a b)")), ("d_tT", tT[:, :, :].rearrange("p a b -> p (a b)")), ("d_oT", oT[:, :, :].rearrange("p a b -> p (a b)")), ("d_gsc", gsc)])
    ar.pop()

    w1sb = ar.alloc((128, 8, 2048), BF16)
    w3sb = ar.alloc((128, 8, 2048), BF16)
    w2sb = ar.alloc((128, 16, 1024), BF16)
    hid = ar.alloc((128, 16, 512), BF16)
    sil = ar.alloc((128, 512), BF16)
    y_sb = ar.alloc((128, 8, D), F32)
    xm2 = ar.alloc((128, D), F32)
    hid_T = {"hid": None, "y": None, "w13": None, "w2": None}
    for th in range(2):
        ym = S.op("vector", lambda v: v.memset(y_sb[:, :, :], 0.0),
                  waits=S.guard() if th == 0 else [], inc="V")
        y_last = {}
        for e in range(8):
            w13w = S.guard() if (th == 0 and e == 0) else                 [("T", hid_T["w13"])]
            w1l = S.dma("gpsimd", w1sb,
                        W["l1f_w1"][e].rearrange("(ic p) f -> p ic f",
                                                 p=128), waits=w13w)
            w3l = S.dma("gpsimd", w3sb,
                        W["l1f_w3"][e].rearrange("(ic p) f -> p ic f",
                                                 p=128))
            w2w = S.guard() if (th == 0 and e == 0) else                 [("T", hid_T["w2"])]
            w2l = S.dma("gpsimd", w2sb,
                        W["l1f_w2"][e].rearrange("(fc p) d -> p fc d",
                                                 p=128), waits=w2w)
            last_hid_T = None
            for tg in range(2):
                if upto == "p7t0e" and th == 0 and e == 0 and tg == 1:
                    return bail()
                hid_evs = None
                for fc in range(16):
                    bkA = (fc % 2) * 2
                    bkB = bkA + 1
                    mwA = [pb_read[bkA], ("DG", w3l)]
                    if th == 0 and e == 0 and tg == 0 and fc == 0:
                        mwA.append(("V", tT_ok2))
                    tcol = th * 1024 + tg * 512
                    tmA = mm_group(S, PB[bkA][:, :],
                                   [(w1sb[:, ic, fc * 128:(fc + 1) * 128],
                                     tT[:, ic, tcol:tcol + 512])
                                    for ic in range(8)], waits=mwA)
                    tmB = mm_group(S, PB[bkB][:, :],
                                   [(w3sb[:, ic, fc * 128:(fc + 1) * 128],
                                     tT[:, ic, tcol:tcol + 512])
                                    for ic in range(8)],
                                   waits=[pb_read[bkB]])
                    sA = S.op("scalar", lambda sc, bkA=bkA: sc.activation(
                        out=sil, in_=PB[bkA][:, :], func=AF.Silu),
                        waits=[("T", tmA), ("V", hid_evs)], inc="A")
                    hid_evs = S.op("vector", lambda v, bkB=bkB, fc=fc: v.tensor_tensor(
                        out=hid[:, fc, :], in0=PB[bkB][:, :], in1=sil,
                        op=ALU.mult), waits=[("A", sA), ("T", tmB)],
                        inc="V")
                    pb_read[bkA] = ("A", sA)
                    pb_read[bkB] = ("V", hid_evs)
                last_hidmm_T = tmB
                if upto == "p7hid" and th == 0 and e == 0 and tg == 0:
                    return bail()
                for tl in range(4):
                    tile = th * 8 + tg * 4 + tl
                    for dg in range(2):
                        bk = 4 + dg
                        tmy = mm_group(
                            S, PB[bk][:, :],
                            [(hid[:, fc, tl * 128:(tl + 1) * 128],
                              w2sb[:, fc, dg * 512:(dg + 1) * 512])
                             for fc in range(16)],
                            waits=[pb_read[bk], ("V", hid_evs),
                                   ("DG", w2l)])
                        yv = S.op("vector", lambda v, tg=tg, tl=tl, dg=dg, bk=bk, tile=tile, e=e: v.scalar_tensor_tensor(
                            out=y_sb[:, tg * 4 + tl,
                                     dg * 512:(dg + 1) * 512],
                            in0=PB[bk][:, :],
                            scalar=gate_all[:, tile, e:e + 1],
                            in1=y_sb[:, tg * 4 + tl,
                                     dg * 512:(dg + 1) * 512],
                            op0=ALU.mult, op1=ALU.add),
                            waits=[("T", tmy), ("V", ym), ("V", gate_ok),
                                   ("V", y_last.get((tg, tl, dg)))],
                            inc="V")
                        pb_read[bk] = ("V", yv)
                        y_last[(tg, tl, dg)] = yv
                if upto == "p7y" and th == 0 and e == 0 and tg == 0:
                    return bail()
                last_hid_T = tmy
                if upto == "p7n" and th == 0 and e == 0 and tg == 0:
                    return bail()
            hid_T["w13"] = last_hidmm_T
            hid_T["w2"] = last_hid_T
            if upto == "p7a" and th == 0 and e == 0:
                return bail()
        if upto == "p7b" and th == 0:
            return bail()
        for tl8 in range(8):
            tile = th * 8 + tl8
            xl = S.dma("sync", xtile[tl8 % 2],
                       in_ap[tile * 128:(tile + 1) * 128, :],
                       waits=[("DS", last_xm)] if tl8 < 2 else
                       [("DS", last_xm)])
            xm = S.dma("sync", xm2, xb2[tile * 128:(tile + 1) * 128, :],
                       waits=[("V", y_last.get((1, 3, 1)))] if tl8 == 0
                       else [])
            vv1 = S.op("vector", lambda v, tl8=tl8: v.tensor_add(
                out=xm2, in0=xm2, in1=y_sb[:, tl8, :]
                if False else y_sb[:, tl8, :]),
                waits=[("DS", xm), ("V", y_last.get((1, 3, 1)))])
            vv2 = S.op("vector", lambda v, tl8=tl8: v.scalar_tensor_tensor(
                out=xm2, in0=xm2, scalar=0.5,
                in1=xtile[tl8 % 2], op0=ALU.mult, op1=ALU.add),
                waits=[("DS", xl)], inc="V")
            last_xm = S.dma("sync", out_ap[tile * 128:(tile + 1) * 128, :],
                            xm2, waits=[("V", vv2)])
        if upto == "p7c" and th == 0:
            return bail()
        if dbg and "d_y" in dbg and th == 1:
            S.dma("sync", dbg["d_y"],
                  y_sb[:, :, :].rearrange("p a b -> p (a b)"),
                  waits=[("V", y_last.get((1, 3, 1)))])
        if dbg and "d_hid" in dbg and th == 1:
            S.dma("sync", dbg["d_hid"],
                  hid[:, :, :].rearrange("p a b -> p (a b)"),
                  waits=[("V", y_last.get((1, 3, 1)))])
    ar.pop()
    return last_xm


def l1ffn_host_arrays(inputs):
    HD = 32
    inv_freq = 1.0 / (10000.0 ** (np.arange(0, HD, 2, dtype=np.float64)
                                  / HD))
    pos = np.arange(T, dtype=np.float64)
    cos = np.zeros((128, NTILE * 256), np.float32)
    sin = np.zeros((128, NTILE * 256), np.float32)
    for i in range(NTILE):
        p = pos[i * 128:(i + 1) * 128][:, None]
        ang = p * inv_freq[None, :]
        cos_t = np.cos(ang).astype(np.float32)
        sin_t = np.sin(ang).astype(np.float32)
        cos[:, i * 256:(i + 1) * 256] = np.tile(cos_t, (1, 16))
        sin[:, i * 256:(i + 1) * 256] = np.tile(sin_t, (1, 16))
    return {
        "l1f_wdown": np.ascontiguousarray(inputs["L1_ffn_w_down"]),
        "l1f_wq": np.ascontiguousarray(inputs["L1_ffn_wq"]),
        "l1f_wk": np.ascontiguousarray(inputs["L1_ffn_wk"]),
        "l1f_wv": np.ascontiguousarray(inputs["L1_ffn_wv"]),
        "l1f_wup": np.ascontiguousarray(inputs["L1_ffn_w_up"]),
        "l1f_router": np.ascontiguousarray(inputs["L1_ffn_router_w"]),
        "l1f_w1": np.ascontiguousarray(inputs["L1_ffn_w1"]),
        "l1f_w3": np.ascontiguousarray(inputs["L1_ffn_w3"]),
        "l1f_w2": np.ascontiguousarray(inputs["L1_ffn_w2"]),
        "l1f_g1m": np.ascontiguousarray(
            inputs["L1_ffn_g1"].reshape(8, 128).T),
        "l1f_g2row": np.ascontiguousarray(
            inputs["L1_ffn_g2"].reshape(1, D)),
        "c_cos": cos,
        "c_sin": sin,
    }


def declare_l1ffn(nc):
    def dt_(name, shape):
        return nc.dram_tensor(name, shape, F32, kind="ExternalInput").ap()
    return {
        "l1f_wdown": dt_("l1f_wdown", [D, 512]),
        "l1f_wq": dt_("l1f_wq", [512, 512]),
        "l1f_wk": dt_("l1f_wk", [512, 512]),
        "l1f_wv": dt_("l1f_wv", [512, 512]),
        "l1f_wup": dt_("l1f_wup", [512, D]),
        "l1f_router": dt_("l1f_router", [D, 8]),
        "l1f_w1": dt_("l1f_w1", [8, D, 2048]),
        "l1f_w3": dt_("l1f_w3", [8, D, 2048]),
        "l1f_w2": dt_("l1f_w2", [8, 2048, D]),
        "l1f_g1m": dt_("l1f_g1m", [128, 8]),
        "l1f_g2row": dt_("l1f_g2row", [1, D]),
        "c_cos": dt_("c_cos", [128, NTILE * 256]),
        "c_sin": dt_("c_sin", [128, NTILE * 256]),
    }


L1F_DBG_SHAPES = {
    "d_cT": ([128, 4 * T], BF16), "d_hT": ([128, 8 * T], BF16),
    "d_qT": ([128, 4 * T], BF16), "d_kT": ([128, 4 * T], BF16),
    "d_v": ([128, 16 * 512], BF16), "d_bias": ([128, 16], F32),
    "d_oT": ([128, 4 * T], BF16), "d_gate": ([128, 16 * 8], F32),
    "d_tT": ([128, 8 * T], BF16),
    "d_st": ([128, 32], F32),
    "d_gsc": ([128, 64], F32),
    "d_y": ([128, 8 * D], F32),
    "d_hid": ([128, 16 * 512], BF16),
}


def build_l1ffn_program(upto=None, dbg_names=(), xb2_out=False):
    nc = bass.Bass("TRN2", target_bir_lowering=False, debug=False)
    xin = nc.dram_tensor("xin", [T, D], F32, kind="ExternalInput").ap()
    xout = nc.dram_tensor("xout", [T, D], F32, kind="ExternalOutput").ap()
    xb2 = nc.dram_tensor("xb2", [T, D], F32,
                         kind="ExternalOutput" if xb2_out else
                         "Internal").ap()
    dbg = {}
    for nm in dbg_names:
        shp, dt = L1F_DBG_SHAPES[nm]
        dbg[nm] = nc.dram_tensor(nm, shp, dt, kind="ExternalOutput").ap()
    S = Sched(nc)
    with contextlib.ExitStack() as stk:
        ar, PB, PBT = _alloc_common(stk, nc, S)
        cst = declare_consts(S, nc, ar)
        W = declare_l1ffn(nc)
        emit_l1ffn(S, nc, ar, cst, W, xin, xout, xb2, PB, PBT, upto=upto,
                   dbg=dbg or None)
        S.op("sync", lambda s: s.wait_ge(S.sems["DS"], S.cnt["DS"]))
        block = stk.enter_context(nc.Block())
        S.emit(block)
    return nc





ACT_CONSTS = [1.0 / D, -0.5, LN_EPS, 1e-6, float(1.0 / np.sqrt(32.0)),
              0.5, 2.0, 1.0 / 512.0, 1.0, -1.0, 0.005, -0.005,
              float(np.sqrt(np.float32(0.01))), float(0.5 * np.pi)]


def make_const_arrays():
    return {
        "c_ident": np.eye(128, dtype=np.float32),
        "c_tri": np.triu(np.ones((128, 128), np.float32)),
        "c_ones": np.ones((128, 128), np.float32),
        "c_actc": np.tile(np.asarray(ACT_CONSTS, np.float32)[None, :],
                          (128, 1)),
        "c_trow": np.tile(np.arange(T, dtype=np.float32)[None, :], (128, 1)),
    }


def declare_consts(S, nc, ar):
    cst = {}
    d_ident = nc.dram_tensor("c_ident", [128, 128], F32,
                             kind="ExternalInput").ap()
    d_tri = nc.dram_tensor("c_tri", [128, 128], F32,
                           kind="ExternalInput").ap()
    d_ones = nc.dram_tensor("c_ones", [128, 128], F32,
                            kind="ExternalInput").ap()
    cst["ident_bf"] = ar.alloc((128, 128), BF16)
    cst["tri_f"] = ar.alloc((128, 128), F32)
    cst["tri_bf"] = ar.alloc((128, 128), BF16)
    cst["ones_f"] = ar.alloc((128, 128), F32)
    cst["ones_bf"] = ar.alloc((128, 128), BF16)
    d_actc = nc.dram_tensor("c_actc", [128, len(ACT_CONSTS)], F32,
                            kind="ExternalInput").ap()
    actc = ar.alloc((128, len(ACT_CONSTS)), F32)
    for i, v in enumerate(ACT_CONSTS):
        nc.const_aps.register(F32, float(v), actc[:, i:i + 1])
    S.dma("gpsimd", cst["ident_bf"], d_ident)
    S.dma("gpsimd", cst["tri_bf"], d_tri)
    S.dma("gpsimd", cst["ones_bf"], d_ones)
    cst["loaded_g"] = S.cnt["DG"]
    S.dma("sync", cst["tri_f"], d_tri)
    S.dma("sync", cst["ones_f"], d_ones)
    ld = S.dma("sync", actc, d_actc)
    cst["loaded"] = ld
    return cst


def l0ffn_host_arrays(inputs):
    out = {
        "l0_w1T": np.ascontiguousarray(inputs["L0_ffn_w1_w"].T),
        "l0_w2T": np.ascontiguousarray(inputs["L0_ffn_w2_w"].T),
        "l0_h1T": np.ascontiguousarray(inputs["L0_ffn_h1_w"].T),
        "l0_h2T": np.ascontiguousarray(inputs["L0_ffn_h2_w"].T),
        "l0_w3T": np.ascontiguousarray(inputs["L0_ffn_w3_w"].T),
        "l0_w3s_row": np.ascontiguousarray(
            inputs["L0_ffn_w3_s"].reshape(1, D)),
    }
    for nm, key in (("s1m", "L0_ffn_w1_s"), ("s2m", "L0_ffn_w2_s"),
                    ("h1m", "L0_ffn_h1_s"), ("h2m", "L0_ffn_h2_s")):
        out["l0_" + nm] = np.ascontiguousarray(
            inputs[key].reshape(32, 128).T)
    return out


def declare_l0ffn(S, nc, ar, cst):
    W = {}
    W["w1T"] = nc.dram_tensor("l0_w1T", [D, DHID], F32,
                              kind="ExternalInput").ap()
    W["w2T"] = nc.dram_tensor("l0_w2T", [D, DHID], F32,
                              kind="ExternalInput").ap()
    W["h1T"] = nc.dram_tensor("l0_h1T", [DHID, DHID], F32,
                              kind="ExternalInput").ap()
    W["h2T"] = nc.dram_tensor("l0_h2T", [DHID, DHID], F32,
                              kind="ExternalInput").ap()
    W["w3T"] = nc.dram_tensor("l0_w3T", [DHID, D], F32,
                              kind="ExternalInput").ap()
    for nm in ("s1m", "s2m", "h1m", "h2m"):
        d = nc.dram_tensor("l0_" + nm, [128, 32], F32,
                           kind="ExternalInput").ap()
        W[nm] = ar.alloc((128, 32), F32)
        S.dma("sync", W[nm], d)
    d = nc.dram_tensor("l0_w3s_row", [1, D], F32, kind="ExternalInput").ap()
    w3row_full = ar.alloc((128, D), F32)[0:1, :]
    ld = S.dma("sync", w3row_full, d)
    cst["w3s_row"] = w3row_full
    cst["loaded"] = ld
    return W



def _alloc_common(stk, nc, S):
    big = stk.enter_context(nc.sbuf_tensor("arena", [128, ARENA_WORDS], F32))
    PB = [stk.enter_context(nc.psum_tensor(f"pb{i}", [128, 512], F32))
          for i in range(7)]
    PBT = stk.enter_context(nc.psum_tensor("pbt", [128, 1024], BF16))
    for name in ("T", "V", "A", "P", "DS", "DG"):
        S.sems[name] = stk.enter_context(nc.semaphore(name))
    return Arena(big, ARENA_WORDS), PB, PBT


def build_l0ffn_program():
    nc = bass.Bass("TRN2", target_bir_lowering=False, debug=False)
    xin = nc.dram_tensor("xin", [T, D], F32, kind="ExternalInput").ap()
    xout = nc.dram_tensor("xout", [T, D], F32, kind="ExternalOutput").ap()
    S = Sched(nc)
    with contextlib.ExitStack() as stk:
        ar, PB, PBT = _alloc_common(stk, nc, S)
        cst = declare_consts(S, nc, ar)
        W = declare_l0ffn(S, nc, ar, cst)
        emit_l0ffn(S, nc, ar, cst, W, xin, xout, PB, PBT)
        S.op("sync", lambda s: s.wait_ge(S.sems["DS"], S.cnt["DS"]))
        block = stk.enter_context(nc.Block())
        S.emit(block)
    return nc


def build_fused_program():
    nc = bass.Bass("TRN2", target_bir_lowering=False, debug=False)
    xin = nc.dram_tensor("xin", [T, D], F32, kind="ExternalInput").ap()
    xout = nc.dram_tensor("xout", [T, D], F32, kind="ExternalOutput").ap()
    xbA = nc.dram_tensor("xbA", [T, D], F32, kind="Internal").ap()
    xbB = nc.dram_tensor("xbB", [T, D], F32, kind="Internal").ap()
    xb2 = nc.dram_tensor("xb2", [T, D], F32, kind="Internal").ap()
    S = Sched(nc)
    with contextlib.ExitStack() as stk:
        ar, PB, PBT = _alloc_common(stk, nc, S)
        cst = declare_consts(S, nc, ar)
        W0f = declare_l0ffn(S, nc, ar, cst)
        W0a = declare_l0attn(nc)
        W1a = declare_l1attn(nc)
        W1f = declare_l1ffn(nc)
        emit_l0ffn(S, nc, ar, cst, W0f, xin, xbA, PB, PBT)
        emit_l0attn(S, nc, ar, cst, W0a, xbA, xbB, PB, PBT)
        emit_l0ffn(S, nc, ar, cst, W0f, xbB, xbA, PB, PBT)
        emit_l1ffn(S, nc, ar, cst, W1f, xbA, xbB, xb2, PB, PBT)
        emit_l1attn(S, nc, ar, cst, W1a, xbB, xbA, PB, PBT)
        emit_l1ffn(S, nc, ar, cst, W1f, xbA, xout, xb2, PB, PBT)
        S.op("sync", lambda s: s.wait_ge(S.sems["DS"], S.cnt["DS"]))
        block = stk.enter_context(nc.Block())
        S.emit(block)
    return nc


_PROGRAMS = {}


def _get_program(name):
    if name not in _PROGRAMS:
        if name == "fused":
            _PROGRAMS[name] = build_fused_program()
        elif name == "l0ffn":
            _PROGRAMS[name] = build_l0ffn_program()
        elif name == "l0attn":
            _PROGRAMS[name] = build_l0attn_program()
        elif name == "l1ffn":
            _PROGRAMS[name] = build_l1ffn_program()
        elif name == "l1attn":
            _PROGRAMS[name] = build_l1attn_program()
    return _PROGRAMS[name]


def kernel(**inputs):
    inputs = {k: np.asarray(v) for k, v in inputs.items()}
    n = 8
    x = np.ascontiguousarray(inputs["L0_attn_x"], dtype=np.float32)

    base = dict(make_const_arrays())
    base.update(l0ffn_host_arrays(inputs))
    base.update(l0attn_host_arrays(inputs))
    base.update(l1attn_host_arrays(inputs))
    base.update(l1ffn_host_arrays(inputs))

    def run(name, xs):
        nc = _get_program(name)
        names = {t for t in nc.to_json()["tensors"]} if False else None
        in_maps = []
        for c in range(n):
            m = dict(base)
            m["xin"] = np.ascontiguousarray(xs[c])
            in_maps.append(m)
        res = run_bass_kernel_spmd(nc, in_maps, core_ids=list(range(n)))
        return [res.results[c]["xout"] for c in range(n)]

    xs = [x[c] for c in range(n)]
    import os
    if os.environ.get("MK_UNFUSED"):
        for stage in ("l0ffn", "l0attn", "l0ffn", "l1ffn", "l1attn",
                      "l1ffn"):
            xs = run(stage, xs)
    else:
        xs = run("fused", xs)
    out = np.stack([np.asarray(s) for s in xs], axis=0).astype(np.float32)
    return out


# version 05
# speedup vs baseline: 1.0226x; 1.0226x over previous
import contextlib
import numpy as np
import ml_dtypes
import concourse.bass as bass
import concourse.mybir as mybir
from concourse.bass_utils import run_bass_kernel_spmd

F32 = mybir.dt.float32
BF16 = mybir.dt.bfloat16
I32 = mybir.dt.int32
AF = mybir.ActivationFunctionType
ALU = mybir.AluOpType
AX = mybir.AxisListType

T = 2048
D = 1024
NTILE = T // 128
HALF = T // 2
DHID = 4096
LN_EPS = 1e-5
ARENA_WORDS = 52000

ENGINES = ("tensor", "vector", "scalar", "gpsimd", "sync")


class Sched:
    def __init__(self, nc):
        self.nc = nc
        self.ops = {e: [] for e in ENGINES}
        self.sems = {}
        self.cnt = {"T": 0, "V": 0, "A": 0, "P": 0, "DS": 0, "DG": 0}

    def op(self, eng, fn, waits=(), inc=None, inc_amt=1):
        val = None
        if inc is not None:
            self.cnt[inc] += inc_amt
            val = self.cnt[inc]
        ws = tuple(w for w in waits if w is not None and w[1] is not None)
        self.ops[eng].append((fn, ws, inc, val))
        return val

    def dma(self, queue, out, in_, waits=(), allow_slow=False):
        sem = "DS" if queue == "sync" else "DG"

        def fn(eng):
            return eng.dma_start(out=out, in_=in_,
                                 allow_slow_non_contiguous=allow_slow)
        return self.op(queue, fn, waits=waits, inc=sem, inc_amt=16)

    def guard(self):
        return [("A", self.cnt["A"] or None), ("V", self.cnt["V"] or None),
                ("T", self.cnt["T"] or None)]

    def tensor_fence(self):
        return self.op("tensor", lambda t: t.nop(), waits=self.guard(),
                       inc="T")

    def emit(self, block):
        def run(eng_obj, eng_name):
            for fn, waits, inc, val in self.ops[eng_name]:
                for (s, v) in waits:
                    eng_obj.wait_ge(self.sems[s], v)
                inst = fn(eng_obj)
                if inc is not None:
                    amt = 16 if inc in ("DS", "DG") else 1
                    inst.then_inc(self.sems[inc], amt)

        @block.tensor
        def _(t):
            run(t, "tensor")

        @block.vector
        def _(v):
            run(v, "vector")

        @block.scalar
        def _(sc):
            run(sc, "scalar")

        @block.gpsimd
        def _(g):
            run(g, "gpsimd")

        @block.sync
        def _(s):
            run(s, "sync")


class Arena:
    def __init__(self, tensor, words):
        self.t = tensor
        self.total = words
        self.off = 0
        self.marks = []

    def push(self):
        self.marks.append(self.off)

    def pop(self):
        self.off = self.marks.pop()

    def alloc(self, shape, dtype):
        free = int(np.prod(shape[1:]))
        words = free if dtype in (F32, I32) else (free + 1) // 2
        words = (words + 7) & ~7
        assert self.off + words <= self.total, \
            f"arena overflow {self.off}+{words}>{self.total}"
        view = self.t[:, self.off:self.off + words]
        self.off += words
        if dtype == BF16:
            view = view.bitcast(BF16)
        elif dtype == I32:
            view = view.bitcast(I32)
        total_free = int(np.prod(shape[1:]))
        view = view[:, :total_free]
        if len(shape) > 2:
            names = " ".join(f"d{i}" for i in range(len(shape) - 1))
            kw = {f"d{i}": int(shape[i + 1]) for i in range(len(shape) - 1)}
            view = view.rearrange(f"p ({names}) -> p {names}", **kw)
        return view


def mm_group(S, psum_ap, pairs, waits=(), start=True, stop=True):
    def fn(t):
        last = None
        n = len(pairs)
        for i, (lh, rh) in enumerate(pairs):
            last = t.matmul(out=psum_ap, lhsT=lh, rhs=rh,
                            start=(start and i == 0),
                            stop=(stop and i == n - 1))
        return last
    return S.op("tensor", fn, waits=waits, inc="T")




STAT_KEYS = ("nsum", "ssq", "nmean", "msq", "veps", "lnl", "rstd", "mrs",
             "rstd_v", "mrs_v")


def make_stats(scr, j):
    base = 16 * j
    st = {k: scr[:, base + i:base + i + 1]
          for i, k in enumerate(STAT_KEYS)}
    st["psrc"] = scr[:, base + 6:base + 8]
    st["pdst"] = scr[:, base + 8:base + 10]
    return st

def ln_tile(S, x_sb, xn_bf, scrap, st, pre_waits=(), out_waits=(),
            rms_only=False):
    if rms_only:
        S.op("scalar", lambda sc: sc.activation(
            out=scrap, in_=x_sb, func=AF.Square, accum_out=st["ssq"]),
            waits=pre_waits)
        S.op("scalar", lambda sc: sc.activation(
            out=st["lnl"], in_=st["ssq"], func=AF.Ln, scale=1.0 / D,
            bias=LN_EPS))
        ar1 = S.op("scalar", lambda sc: sc.activation(
            out=st["rstd"], in_=st["lnl"], func=AF.Exp, scale=-0.5),
            inc="A")
        b1 = S.op("vector", lambda v: v.engine_nop(), waits=[("A", ar1)],
                  inc="V")
        vo = S.op("scalar", lambda sc: sc.mul(
            out=xn_bf, in_=x_sb, mul=st["rstd"]),
            waits=list(out_waits) + [("V", b1)], inc="A")
        return ("A", vo)
    vsum = S.op("vector", lambda v: v.reduce_sum(
        out=st["nsum"], in_=x_sb, axis=AX.X, negate=True),
        waits=pre_waits, inc="V")
    S.op("scalar", lambda sc: sc.activation(
        out=scrap, in_=x_sb, func=AF.Square, accum_out=st["ssq"]),
        waits=pre_waits)
    S.op("scalar", lambda sc: sc.mul(
        out=st["nmean"], in_=st["nsum"], mul=1.0 / D),
        waits=[("V", vsum)])
    aml = S.op("scalar", lambda sc: sc.activation(
        out=st["msq"], in_=st["nmean"], func=AF.Square), inc="A")
    vv = S.op("vector", lambda v: v.scalar_tensor_tensor(
        out=st["veps"], in0=st["ssq"], scalar=1.0 / D, in1=st["msq"],
        op0=ALU.mult, op1=ALU.subtract), waits=[("A", aml)], inc="V")
    S.op("scalar", lambda sc: sc.activation(
        out=st["lnl"], in_=st["veps"], func=AF.Ln, bias=LN_EPS),
        waits=[("V", vv)])
    ar1 = S.op("scalar", lambda sc: sc.activation(
        out=st["rstd"], in_=st["lnl"], func=AF.Exp, scale=-0.5), inc="A")
    b1 = S.op("vector", lambda v: v.engine_nop(), waits=[("A", ar1)],
              inc="V")
    am = S.op("scalar", lambda sc: sc.mul(
        out=st["mrs"], in_=st["nmean"], mul=st["rstd"]),
        waits=[("V", b1)], inc="A")
    b2 = S.op("vector", lambda v: v.engine_nop(), waits=[("A", am)],
              inc="V")
    vo = S.op("scalar", lambda sc: sc.activation(
        out=xn_bf, in_=x_sb, func=AF.Identity, scale=st["rstd"],
        bias=st["mrs"]), waits=list(out_waits) + [("V", b2)], inc="A")
    return ("A", vo)



def emit_l0ffn(S, nc, ar, cst, W, in_ap, out_ap, PB, PBT):
    ar.push()
    S.tensor_fence()
    xnT = ar.alloc((128, 8, HALF), BF16)
    hA = ar.alloc((128, 32, HALF), BF16)
    hB = ar.alloc((128, 32, HALF), BF16)
    w12 = [[ar.alloc((128, 8, 128), BF16) for _ in range(2)] for _ in range(2)]
    hwb = [ar.alloc((128, 32, 128), BF16) for _ in range(2)]
    w3b = [ar.alloc((128, D), BF16) for _ in range(2)]
    xtile = [ar.alloc((128, D), F32) for _ in range(2)]
    scr = ar.alloc((128, 2592), F32)
    w3s_rep = ar.alloc((128, D), F32)
    t1fa = ar.alloc((128, 512), F32)
    scrap = scr[:, 0:1024]
    xn_bf = [scr[:, 1024 + i * 512:1536 + i * 512].bitcast(BF16)
             for i in range(2)]
    stats = [make_stats(scr[:, 2048:2080], j) for j in range(2)]
    tmpS = [scr[:, 2080 + 256 * i:2336 + 256 * i].bitcast(BF16)
            for i in range(2)]
    xres = [scr[:, 0:1024], scr[:, 1024:2048]]

    for dg in range(2):
        tm = mm_group(S, PB[6][:, 0:512],
                      [(cst["ones_f"][0:1, :],
                        cst["w3s_row"][0:1, dg * 512:(dg + 1) * 512])],
                      waits=([("DS", cst["loaded"]), ("DG", cst["loaded_g"])]
                             if dg == 0 else []) + S.guard())
        S.op("vector", lambda v, dg=dg: v.tensor_scalar_mul(
            out=w3s_rep[:, dg * 512:(dg + 1) * 512], in0=PB[6][:, 0:512],
            scalar1=0.5), waits=[("T", tm)], inc="V")

    pb_read = {i: ("V", S.cnt["V"]) for i in range(8)}
    last_store = None

    for half in range(2):
        tile0 = half * 8
        aouts = {}
        tdone = {}
        for i in range(8):
            tl = tile0 + i
            ld = S.dma("sync", xtile[i % 2],
                       in_ap[tl * 128:(tl + 1) * 128, :],
                       waits=([aouts[i - 2]] if i >= 2 else
                              S.guard() + [("DS", last_store)]))
            pw = [("DS", ld)]
            if i >= 1:
                pw.append(aouts[i - 1])
            ow = [("T", tdone[i - 2])] if i >= 2 else []
            a = ln_tile(S, xtile[i % 2], xn_bf[i % 2], scrap, stats[i % 2],
                        pre_waits=pw, out_waits=ow)
            aouts[i] = a

            def tr_fn(t, i=i):
                last = None
                for ic in range(8):
                    last = t.transpose(
                        out=PBT[:, ic * 128:(ic + 1) * 128],
                        in_=xn_bf[i % 2][:, ic * 128:(ic + 1) * 128],
                        identity=cst["ident_bf"])
                return last
            tdone[i] = S.op("tensor", tr_fn, waits=[a, pb_read[7]],
                            inc="T")
            pv = S.op("vector", lambda v, i=i: v.tensor_copy(
                out=xnT[:, :, i * 128:(i + 1) * 128],
                in_=PBT[:, :].rearrange("p (ic q) -> p ic q", ic=8)),
                waits=[("T", tdone[i])], inc="V")
            pb_read[7] = ("V", pv)
        xnT_ready = S.cnt["V"]

        ev_hist = {}
        for oc in range(32):
            b = oc % 2
            wwait = ([("V", ev_hist.get((oc - 2, 1)))] if oc >= 2
                     else S.guard())
            S.dma("gpsimd", w12[b][0],
                  W["w1T"][:, oc * 128:(oc + 1) * 128].rearrange(
                      "(ic p) o -> p ic o", p=128), waits=wwait)
            w2l = S.dma("gpsimd", w12[b][1],
                        W["w2T"][:, oc * 128:(oc + 1) * 128].rearrange(
                            "(ic p) o -> p ic o", p=128))
            for ng in range(2):
                bkA, bkB = 2 * ng, 2 * ng + 1
                mw = [("DG", w2l), pb_read[bkA], pb_read[bkB]]
                if oc == 0 and ng == 0:
                    mw.append(("V", xnT_ready))
                tmA = mm_group(S, PB[bkA][:, :],
                               [(w12[b][0][:, ic, :],
                                 xnT[:, ic, ng * 512:(ng + 1) * 512])
                                for ic in range(8)], waits=mw)
                tmB = mm_group(S, PB[bkB][:, :],
                               [(w12[b][1][:, ic, :],
                                 xnT[:, ic, ng * 512:(ng + 1) * 512])
                                for ic in range(8)])
                sA = S.op("scalar", lambda sc, oc=oc, ng=ng, bkA=bkA: sc.activation(
                    out=tmpS[ng], in_=PB[bkA][:, :], func=AF.Silu,
                    scale=W["s1m"][:, oc:oc + 1]),
                    waits=[("T", tmA), ("V", ev_hist.get((oc - 1, ng)))],
                    inc="A")
                ev = S.op("vector", lambda v, oc=oc, ng=ng, bkB=bkB: v.scalar_tensor_tensor(
                    out=hA[:, oc, ng * 512:(ng + 1) * 512], in0=PB[bkB][:, :],
                    scalar=W["s2m"][:, oc:oc + 1], in1=tmpS[ng],
                    op0=ALU.mult, op1=ALU.mult),
                    waits=[("A", sA), ("T", tmB)], inc="V")
                pb_read[bkA] = ("A", sA)
                pb_read[bkB] = ("V", ev)
                ev_hist[(oc, ng)] = ev
        src_ready = ("V", S.cnt["V"])

        for (wname, scol, src, dst) in (("h1T", "h1m", hA, hB),
                                        ("h2T", "h2m", hB, hA)):
            ev2 = {}
            for oc in range(32):
                b = oc % 2
                wwait = ([("A", ev2.get(oc - 2))] if oc >= 2 else S.guard())
                wl = S.dma("gpsimd", hwb[b],
                           W[wname][:, oc * 128:(oc + 1) * 128].rearrange(
                               "(ic p) o -> p ic o", p=128), waits=wwait)
                for ng in range(2):
                    bk = (2 * oc + ng) % 4
                    mw = [("DG", wl), pb_read[bk]]
                    if oc == 0 and ng == 0:
                        mw.append(src_ready)
                    tm = mm_group(S, PB[bk][:, :],
                                  [(hwb[b][:, ic, :],
                                    src[:, ic, ng * 512:(ng + 1) * 512])
                                   for ic in range(32)], waits=mw)
                    ev = S.op("scalar", lambda sc, oc=oc, ng=ng, bk=bk, dst=dst, scol=scol: sc.activation(
                        out=dst[:, oc, ng * 512:(ng + 1) * 512],
                        in_=PB[bk][:, :], func=AF.Gelu,
                        scale=W[scol][:, oc:oc + 1]),
                        waits=[("T", tm)], inc="A")
                    pb_read[bk] = ("A", ev)
                    ev2[oc] = ev
            src_ready = ("A", S.cnt["A"])
        hC_ready = src_ready

        PB8 = PB[:7] + [PBT.bitcast(F32)[:, 0:512]]
        w3_hist = {}
        for tg in range(2):
            for hc in range(32):
                b = hc % 2
                if (tg, hc) in ((0, 0), (0, 1)):
                    wwait = S.guard()
                else:
                    prev = w3_hist.get((tg, hc - 2), w3_hist.get((tg - 1,
                                                                  30 + hc)))
                    wwait = [("T", prev)]
                wl = S.dma("gpsimd", w3b[b],
                           W["w3T"][hc * 128:(hc + 1) * 128, :], waits=wwait)

                def mm5_fn(t, hc=hc, b=b, tg=tg):
                    last = None
                    for ti in range(4):
                        tloc = tg * 4 + ti
                        for dg in range(2):
                            last = t.matmul(
                                out=PB8[ti * 2 + dg][:, :],
                                lhsT=hA[:, hc, tloc * 128:(tloc + 1) * 128],
                                rhs=w3b[b][:, dg * 512:(dg + 1) * 512],
                                start=(hc == 0), stop=(hc == 31))
                    return last
                mw = [("DG", wl)]
                if hc == 0:
                    mw += [hC_ready] + [pb_read[k] for k in range(8)]
                w3_hist[(tg, hc)] = S.op("tensor", mm5_fn, waits=mw, inc="T")
            mm_done = w3_hist[(tg, 31)]
            for ti in range(4):
                tloc = tg * 4 + ti
                tl = tile0 + tloc
                xl = S.dma("sync", xtile[ti % 2],
                           in_ap[tl * 128:(tl + 1) * 128, :],
                           waits=[("DS", last_store), aouts[7]])
                evs = None
                for dg in range(2):
                    bk = ti * 2 + dg
                    S.op("vector", lambda v, bk=bk, dg=dg: v.tensor_tensor(
                        out=t1fa, in0=PB8[bk][:, :],
                        in1=w3s_rep[:, dg * 512:(dg + 1) * 512], op=ALU.mult),
                        waits=[("T", mm_done), ("DS", xl)])
                    evs = S.op("vector", lambda v, ti=ti, dg=dg: v.tensor_add(
                        out=xres[ti % 2][:, dg * 512:(dg + 1) * 512],
                        in0=t1fa,
                        in1=xtile[ti % 2][:, dg * 512:(dg + 1) * 512]),
                        inc="V")
                    pb_read[bk] = ("V", evs)
                last_store = S.dma(
                    "sync", out_ap[tl * 128:(tl + 1) * 128, :],
                    xres[ti % 2], waits=[("V", evs)])
    ar.pop()
    return last_store



C0 = float(1.0 / np.sqrt(1.0 + LN_EPS))


def emit_l0attn(S, nc, ar, cst, W, in_ap, out_ap, PB, PBT, dbg=None):
    S.tensor_fence()

    def dump(name, ap, waits):
        if dbg is not None and name in dbg:
            dv = S.dma("sync", dbg[name], ap, waits=waits)
            S.op("vector", lambda v: v.engine_nop(), waits=[("DS", dv)],
                 inc="V")
            S.op("scalar", lambda sc: sc.nop(), waits=[("DS", dv)],
                 inc="A")
            S.op("gpsimd", lambda g: g.engine_nop(), waits=[("DS", dv)],
                 inc="P")
    ar.push()
    Wg = ar.alloc((128, 8, 4096), BF16)
    Sst = ar.alloc((128, 8, D), F32)
    Sbf = ar.alloc((128, 8, D), BF16)
    tT = ar.alloc((128, 8, 128), BF16)
    qpT = ar.alloc((128, 8, 128), BF16)
    kpT = ar.alloc((128, 8, 128), BF16)
    kg_f = ar.alloc((128, D), F32)
    gate_f = ar.alloc((128, D), F32)
    qp_bf = ar.alloc((128, D), BF16)
    kp_bf = ar.alloc((128, D), BF16)
    v_bf = ar.alloc((128, D), BF16)
    A_bf = ar.alloc((128, 128), BF16)
    t_bf = ar.alloc((128, D), BF16)
    xtile = [ar.alloc((128, D), F32) for _ in range(2)]
    xout_sb = [ar.alloc((128, D), F32) for _ in range(2)]
    scr0 = ar.alloc((128, 512), F32)
    scr1 = ar.alloc((128, 512), F32)
    scrD = ar.alloc((128, D), F32)
    scrLN = ar.alloc((128, D), F32)
    o1f = ar.alloc((128, D), F32)
    den = ar.alloc((128, 1), F32)
    den2 = ar.alloc((128, 1), F32)
    rden = ar.alloc((128, 1), F32)
    rowpool = ar.alloc((128, 6656), F32)
    rowbf = rowpool.bitcast(BF16)
    bvec = rowpool[0:1, 0:4096]
    bvec_bf = rowbf[0:1, 8192:12288]
    ksum_bf = rowbf[0:1, 12288:13312]
    brow_sb = rowpool[32:33, 0:4096]
    ksum = rowpool[64:65, 0:1024]
    lng = ar.alloc((128, 8), F32)
    lnb_bf = ar.alloc((128, 8), BF16)
    scrsm = ar.alloc((128, 32), F32)
    st2 = [make_stats(scrsm, j) for j in range(2)]

    wl = S.dma("gpsimd", Wg,
               W["l0a_w"].rearrange("(ic p) o -> p ic o", p=128),
               waits=S.guard())
    lnbl = S.dma("gpsimd", lnb_bf, W["l0a_lnb"])
    lngl = S.dma("sync", lng, W["l0a_lng"], waits=S.guard())
    browl = S.dma("sync", brow_sb, W["l0a_brow"])

    vb = None
    for og in range(8):
        tm = mm_group(S, PB[0][0:1, :],
                      [(lnb_bf[:, ic:ic + 1],
                        Wg[:, ic, og * 512:(og + 1) * 512])
                       for ic in range(8)],
                      waits=[("DG", max(wl, lnbl)), ("DS", browl),
                             ("V", vb)] + (S.guard() if og == 0 else []))
        vb = S.op("vector", lambda v, og=og: v.tensor_add(
            out=bvec[:, og * 512:(og + 1) * 512], in0=PB[0][0:1, :],
            in1=brow_sb[:, og * 512:(og + 1) * 512]), waits=[("T", tm)],
            inc="V")
    vbf = S.op("vector", lambda v: v.tensor_copy(out=bvec_bf, in_=bvec),
               inc="V")
    wsc = None
    bvec_T = S.cnt["T"]
    for ic in range(8):
        wsc = S.op("vector", lambda v, ic=ic: v.tensor_scalar(
            out=Wg[:, ic, :], in0=Wg[:, ic, :], scalar1=lng[:, ic:ic + 1],
            scalar2=C0, op0=ALU.mult, op1=ALU.mult),
            waits=([("T", bvec_T), ("DS", lngl)] if ic == 0 else []),
            inc="V")
    S.op("vector", lambda v: v.memset(Sst[:, :, :], 0.0))
    S.op("vector", lambda v: v.memset(ksum, 0.0))
    vz = S.op("vector", lambda v: v.memset(ksum_bf, 0.0), inc="V")

    pb_read = {i: ("V", S.cnt["V"]) for i in range(8)}
    last_store = None
    prev_store = None
    sbf_done = None
    prev_exp = prev_stt = None

    def tr_make(src_bf, dst, ready):
        def trf(t):
            last = None
            for ic in range(8):
                last = t.transpose(
                    out=PBT[:, ic * 128:(ic + 1) * 128],
                    in_=src_bf[:, ic * 128:(ic + 1) * 128],
                    identity=cst["ident_bf"])
            return last
        tm = S.op("tensor", trf, waits=[ready, pb_read[7],
                                        ("DG", cst["loaded_g"])], inc="T")
        pv = S.op("vector", lambda v: v.tensor_copy(
            out=dst[:, :, :],
            in_=PBT[:, :].rearrange("p (ic q) -> p ic q", ic=8)),
            waits=[("T", tm)], inc="V")
        pb_read[7] = ("V", pv)
        return ("V", pv)

    for c in range(NTILE):
        par = c % 2
        ld = S.dma("sync", xtile[par], in_ap[c * 128:(c + 1) * 128, :],
                   waits=[("DS", prev_store)] + (S.guard() if c < 2 else []))
        a = ln_tile(S, xtile[par], t_bf, scrLN, st2[par],
                    pre_waits=[("DS", ld)])
        tT_ok = tr_make(t_bf, tT, a)

        og_order = [0, 1, 6, 7, 2, 3, 4, 5]
        gate_ready = {}
        kmul = {}
        qp_done = kp_done = v_done = None
        for idx, og in enumerate(og_order):
            bk = idx % 4
            mw = [tT_ok, pb_read[bk]]
            if c == 0 and idx == 0:
                mw += [("V", vbf), ("V", wsc), ("V", vz)]
            tm = mm_group(S, PB[bk][:, :],
                          [(cst["ones_bf"][0:1, :],
                            bvec_bf[:, og * 512:(og + 1) * 512])] +
                          [(tT[:, ic, :], Wg[:, ic, og * 512:(og + 1) * 512])
                           for ic in range(8)], waits=mw)
            if og in (0, 1):
                vm = S.op("vector", lambda v, bk=bk: v.tensor_scalar_min(
                    out=scr0, in0=PB[bk][:, :], scalar1=0.0),
                    waits=[("T", tm), ("A", prev_exp)], inc="V")
                a1 = S.op("scalar", lambda sc: sc.activation(
                    out=scr1, in_=scr0, func=AF.Exp),
                    waits=[("V", vm), ("V", prev_stt)], inc="A")
                prev_exp = a1
                qp_done = S.op("vector", lambda v, bk=bk, og=og: v.scalar_tensor_tensor(
                    out=qp_bf[:, og * 512:(og + 1) * 512], in0=PB[bk][:, :],
                    scalar=0.0, in1=scr1, op0=ALU.max, op1=ALU.add),
                    waits=[("A", a1)], inc="V")
                prev_stt = qp_done
                pb_read[bk] = ("V", qp_done)
            elif og in (6, 7):
                g1 = S.op("scalar", lambda sc, bk=bk, og=og: sc.activation(
                    out=gate_f[:, (og - 6) * 512:(og - 5) * 512],
                    in_=PB[bk][:, :], func=AF.Sigmoid),
                    waits=[("T", tm), ("V", kmul.get(og - 6))], inc="A")
                gate_ready[og - 6] = g1
                pb_read[bk] = ("A", g1)
            elif og in (2, 3):
                blk = og - 2
                km = S.op("vector", lambda v, bk=bk, blk=blk: v.tensor_tensor(
                    out=kg_f[:, blk * 512:(blk + 1) * 512], in0=PB[bk][:, :],
                    in1=gate_f[:, blk * 512:(blk + 1) * 512], op=ALU.mult),
                    waits=[("T", tm), ("A", gate_ready[blk])], inc="V")
                kmul[blk] = km
                pb_read[bk] = ("V", km)
                vm = S.op("vector", lambda v, blk=blk: v.tensor_scalar_min(
                    out=scr0, in0=kg_f[:, blk * 512:(blk + 1) * 512],
                    scalar1=0.0), waits=[("A", prev_exp)], inc="V")
                a1 = S.op("scalar", lambda sc: sc.activation(
                    out=scr1, in_=scr0, func=AF.Exp),
                    waits=[("V", vm), ("V", prev_stt)], inc="A")
                prev_exp = a1
                kp_done = S.op("vector", lambda v, blk=blk: v.scalar_tensor_tensor(
                    out=kp_bf[:, blk * 512:(blk + 1) * 512],
                    in0=kg_f[:, blk * 512:(blk + 1) * 512],
                    scalar=0.0, in1=scr1, op0=ALU.max, op1=ALU.add),
                    waits=[("A", a1)], inc="V")
                prev_stt = kp_done
            else:
                v_done = S.op("vector", lambda v, bk=bk, og=og: v.tensor_copy(
                    out=v_bf[:, (og - 4) * 512:(og - 3) * 512],
                    in_=PB[bk][:, :]), waits=[("T", tm)], inc="V")
                pb_read[bk] = ("V", v_done)

        if c == 0:
            dump("d_tbf", t_bf, [a])
            dump("d_qp", qp_bf, [("V", qp_done)])
            dump("d_kp", kp_bf, [("V", kp_done)])
            dump("d_gate", gate_f, [("A", gate_ready[1])])
            dump("d_v", v_bf, [("V", v_done)])
        qpT_ok = tr_make(qp_bf, qpT, ("V", qp_done))
        kpT_ok = tr_make(kp_bf, kpT, ("V", kp_done))

        kc = {}
        for dg in range(2):
            pairs = [(cst["tri_bf"], kp_bf[:, dg * 512:(dg + 1) * 512])]
            if c > 0:
                pairs = [(cst["ones_bf"][0:1, :],
                          ksum_bf[:, dg * 512:(dg + 1) * 512])] + pairs
            kc[dg] = mm_group(S, PB[4 + dg][:, :], pairs,
                              waits=[("V", kp_done), pb_read[4 + dg]])
        ks = {}
        for dg in range(2):
            ks[dg] = mm_group(S, PB[6][0:1, :],
                              [(cst["ones_bf"][:, 0:1],
                                kp_bf[:, dg * 512:(dg + 1) * 512])],
                              waits=[pb_read[6], ("T", kc[1])])
            kv = S.op("vector", lambda v, dg=dg: v.tensor_tensor(
                out=ksum[:, dg * 512:(dg + 1) * 512], in0=PB[6][0:1, :],
                in1=ksum[:, dg * 512:(dg + 1) * 512], op=ALU.add),
                waits=[("T", ks[dg])], inc="V")
            pb_read[6] = ("V", kv)
        kbf = S.op("vector", lambda v: v.tensor_copy(out=ksum_bf, in_=ksum),
                   inc="V")

        dm = None
        for dg in range(2):
            dm = S.op("vector", lambda v, dg=dg: v.tensor_tensor(
                out=scrD[:, dg * 512:(dg + 1) * 512], in0=PB[4 + dg][:, :],
                in1=qp_bf[:, dg * 512:(dg + 1) * 512], op=ALU.mult),
                waits=[("T", kc[dg])], inc="V")
            pb_read[4 + dg] = ("V", dm)
        vr = S.op("vector", lambda v: v.reduce_sum(
            out=den, in_=scrD, axis=AX.X), inc="V")
        ad = S.op("scalar", lambda sc: sc.add(out=den2, in_=den, add=1e-6),
                  waits=[("V", vr)], inc="A")
        rv = S.op("vector", lambda v: v.reciprocal(out=rden, in_=den2),
                  waits=[("A", ad)], inc="V")
        if c == 0:
            dump("d_kcum", scrD, [("V", dm)])
            dump("d_den", den2, [("A", ad)])
        if c == 1:
            dump("e_qp", qp_bf, [("V", qp_done)])
            dump("e_kp", kp_bf, [("V", kp_done)])
            dump("e_kcum", scrD, [("V", dm)])
            dump("e_den", den2, [("A", ad)])

        am = mm_group(S, PB[2][:, 0:128],
                      [(kpT[:, ic, :], qpT[:, ic, :]) for ic in range(8)],
                      waits=[kpT_ok, qpT_ok, pb_read[2]])
        ab = S.op("vector", lambda v: v.tensor_tensor(
            out=A_bf, in0=PB[2][:, 0:128], in1=cst["tri_f"], op=ALU.mult),
            waits=[("T", am)], inc="V")
        pb_read[2] = ("V", ab)

        om = {}
        for dg in range(2):
            pairs = []
            if c > 0:
                pairs += [(qpT[:, ic, :], Sbf[:, ic, dg * 512:(dg + 1) * 512])
                          for ic in range(8)]
            pairs += [(A_bf, v_bf[:, dg * 512:(dg + 1) * 512])]
            mw = [("V", ab), ("V", v_done), pb_read[dg]]
            if c > 0:
                mw.append(("A", sbf_done))
            om[dg] = mm_group(S, PB[dg][:, :], pairs, waits=mw)
        ev = None
        for dg in range(2):
            o1 = S.op("scalar", lambda sc, dg=dg: sc.mul(
                out=o1f[:, dg * 512:(dg + 1) * 512], in_=PB[dg][:, :],
                mul=rden), waits=[("T", om[dg]), ("V", rv)], inc="A")
            ev = S.op("vector", lambda v, dg=dg, par=par: v.tensor_add(
                out=xout_sb[par][:, dg * 512:(dg + 1) * 512],
                in0=o1f[:, dg * 512:(dg + 1) * 512],
                in1=xtile[par][:, dg * 512:(dg + 1) * 512]),
                waits=[("A", o1), ("DS", prev_store)], inc="V")
            pb_read[dg] = ("A", o1)
        if c == 0:
            dump("d_abf", A_bf, [("V", ab)])
            dump("d_o1", o1f, [("A", pb_read[1][1])])
        if c == 1:
            dump("e_o1", o1f, [("A", pb_read[1][1])])
        if c == 2:
            dump("f_sbf", Sbf[:, :, :].rearrange("p a b -> p (a b)"),
                 [("P", sbf_done)])
        prev_store = last_store
        last_store = S.dma("sync", out_ap[c * 128:(c + 1) * 128, :],
                           xout_sb[par], waits=[("V", ev)])

        sadds = []
        for ic in range(8):
            for dg in range(2):
                bk = 3 + (2 * ic + dg) % 3
                sm = mm_group(S, PB[bk][:, :],
                              [(kp_bf[:, ic * 128:(ic + 1) * 128],
                                v_bf[:, dg * 512:(dg + 1) * 512])],
                              waits=[pb_read[bk], ("V", kp_done),
                                     ("V", v_done)])
                sv = S.op("vector", lambda v, ic=ic, dg=dg, bk=bk: v.tensor_tensor(
                    out=Sst[:, ic, dg * 512:(dg + 1) * 512], in0=PB[bk][:, :],
                    in1=Sst[:, ic, dg * 512:(dg + 1) * 512], op=ALU.add),
                    waits=[("T", sm)], inc="V")
                pb_read[bk] = ("V", sv)
            sadds.append(sv)
            sbf_done = S.op("scalar", lambda sc, ic=ic: sc.copy(
                out=Sbf[:, ic, :], in_=Sst[:, ic, :]),
                waits=[("V", sadds[ic])], inc="A")
    ar.pop()
    return last_store


def l0attn_host_arrays(inputs):
    w = np.concatenate([inputs["L0_attn_qkv_w"], inputs["L0_attn_gate_w"]],
                       axis=1)
    brow = np.concatenate([inputs["L0_attn_qkv_b"],
                           inputs["L0_attn_gate_b"]]).reshape(1, 4096)
    return {
        "l0a_w": np.ascontiguousarray(w),
        "l0a_brow": np.ascontiguousarray(brow),
        "l0a_lng": np.ascontiguousarray(
            inputs["L0_attn_ln_g"].reshape(8, 128).T),
        "l0a_lnb": np.ascontiguousarray(
            inputs["L0_attn_ln_b"].reshape(8, 128).T),
    }


def declare_l0attn(nc):
    return {
        "l0a_w": nc.dram_tensor("l0a_w", [D, 4096], F32,
                                kind="ExternalInput").ap(),
        "l0a_brow": nc.dram_tensor("l0a_brow", [1, 4096], F32,
                                   kind="ExternalInput").ap(),
        "l0a_lng": nc.dram_tensor("l0a_lng", [128, 8], F32,
                                  kind="ExternalInput").ap(),
        "l0a_lnb": nc.dram_tensor("l0a_lnb", [128, 8], F32,
                                  kind="ExternalInput").ap(),
    }


def build_l0attn_program(dbg_names=()):
    nc = bass.Bass("TRN2", target_bir_lowering=False, debug=False)
    xin = nc.dram_tensor("xin", [T, D], F32, kind="ExternalInput").ap()
    xout = nc.dram_tensor("xout", [T, D], F32, kind="ExternalOutput").ap()
    dbg = {}
    for nm in dbg_names:
        dt = BF16 if nm in ("d_tbf", "d_qp", "d_kp", "d_v", "d_abf",
                            "e_qp", "e_kp", "f_sbf") else F32
        shp = [128, 128] if nm == "d_abf" else (
            [128, 1] if nm in ("d_den", "e_den") else (
                [128, 8 * D] if nm == "f_sbf" else [128, D]))
        dbg[nm] = nc.dram_tensor(nm, shp, dt, kind="ExternalOutput").ap()
    S = Sched(nc)
    with contextlib.ExitStack() as stk:
        ar, PB, PBT = _alloc_common(stk, nc, S)
        cst = declare_consts(S, nc, ar)
        W = declare_l0attn(nc)
        emit_l0attn(S, nc, ar, cst, W, xin, xout, PB, PBT, dbg=dbg or None)
        S.op("sync", lambda s: s.wait_ge(S.sems["DS"], S.cnt["DS"]))
        block = stk.enter_context(nc.Block())
        S.emit(block)
    return nc



DT_SSM = 0.01
SQRT_DT = float(np.sqrt(np.float32(DT_SSM)))
TWO_PI = float(2.0 * np.pi)
HALF_PI = float(0.5 * np.pi)


def emit_l1attn(S, nc, ar, cst, W, in_ap, out_ap, PB, PBT, dbg=None):
    ar.push()
    S.tensor_fence()
    uT = ar.alloc((128, 8, T), BF16)
    bu_r = ar.alloc((128, T), F32)
    bu_i = ar.alloc((128, T), F32)
    cosb = ar.alloc((128, T), F32)
    sinb = ar.alloc((128, T), F32)
    rtb = ar.alloc((128, T), F32)
    rmtb = ar.alloc((128, T), F32)
    hrb = ar.alloc((128, T), F32)
    hib = ar.alloc((128, T), F32)
    c1 = ar.alloc((128, T), F32)
    c2 = ar.alloc((128, T), F32)
    ibuf = ar.alloc((128, T), I32)
    ibuf2 = ar.alloc((128, T), I32)
    onesb = ar.alloc((128, T), F32)
    trow = ar.alloc((128, T), F32)
    h_bf = ar.alloc((128, T), BF16)
    u_bf = ar.alloc((128, D), BF16)
    xtile = [ar.alloc((128, D), F32) for _ in range(2)]
    xout_sb = [ar.alloc((128, D), F32) for _ in range(2)]
    BTb = ar.alloc((128, 8, 128), BF16)
    CTb = ar.alloc((128, D), BF16)
    scrLN = ar.alloc((128, D), F32)
    cols = ar.alloc((128, 64), F32)
    cn = {}
    for i, k in enumerate(
        ("ar0", "ai0", "denr", "deni", "numr", "numi", "m1", "d2", "rinv",
         "rinv2", "invr", "invi", "adr", "adi", "t1", "t2", "r2", "lnr2",
         "lr", "nlr", "ratio", "th", "ath", "sgn", "sivr", "sivi")):
        cn[k] = cols[:, i:i + 1]
    scrsm = ar.alloc((128, 32), F32)
    st2 = [make_stats(scrsm, j) for j in range(2)]

    btl = S.dma("gpsimd", BTb, W["l1a_BT"].rearrange("(ic p) n -> p ic n",
                                                     p=128), waits=S.guard())
    ctl = S.dma("gpsimd", CTb, W["l1a_CT"])
    arl = S.dma("sync", cn["ar0"], W["l1a_Ar"], waits=S.guard())
    ail = S.dma("sync", cn["ai0"], W["l1a_Ai"])
    trl = S.dma("sync", trow, W["c_trow"])

    hdt = DT_SSM / 2.0
    S.op("scalar", lambda sc: sc.activation(
        out=cn["denr"], in_=cn["ar0"], func=AF.Copy, scale=-hdt, bias=1.0),
        waits=[("DS", ail)])
    S.op("scalar", lambda sc: sc.mul(out=cn["deni"], in_=cn["ai0"],
                                     mul=-hdt))
    S.op("scalar", lambda sc: sc.activation(
        out=cn["numr"], in_=cn["ar0"], func=AF.Copy, scale=hdt, bias=1.0))
    S.op("scalar", lambda sc: sc.mul(out=cn["numi"], in_=cn["ai0"],
                                     mul=hdt))
    S.op("scalar", lambda sc: sc.activation(out=cn["m1"], in_=cn["denr"],
                                            func=AF.Square))
    aq = S.op("scalar", lambda sc: sc.activation(out=cn["t1"], in_=cn["deni"],
                                                 func=AF.Square), inc="A")
    vd2 = S.op("vector", lambda v: v.tensor_add(
        out=cn["d2"], in0=cn["m1"], in1=cn["t1"]), waits=[("A", aq)],
        inc="V")
    ad2 = S.op("scalar", lambda sc: sc.copy(out=cn["t2"], in_=cn["d2"]),
               waits=[("V", vd2)], inc="A")
    rv = S.op("vector", lambda v: v.reciprocal(out=cn["rinv"], in_=cn["t2"]),
              waits=[("A", ad2)], inc="V")
    S.op("scalar", lambda sc: sc.mul(out=cn["invr"], in_=cn["denr"],
                                     mul=cn["rinv"]), waits=[("V", rv)])
    S.op("scalar", lambda sc: sc.mul(out=cn["t1"], in_=cn["deni"],
                                     mul=cn["rinv"]))
    S.op("scalar", lambda sc: sc.mul(out=cn["invi"], in_=cn["t1"],
                                     mul=-1.0))
    S.op("scalar", lambda sc: sc.mul(out=cn["t1"], in_=cn["numr"],
                                     mul=cn["invr"]))
    aq = S.op("scalar", lambda sc: sc.mul(out=cn["t2"], in_=cn["numi"],
                                          mul=cn["invi"]), inc="A")
    va = S.op("vector", lambda v: v.tensor_sub(
        out=cn["adr"], in0=cn["t1"], in1=cn["t2"]), waits=[("A", aq)],
        inc="V")
    S.op("scalar", lambda sc: sc.mul(out=cn["t1"], in_=cn["numr"],
                                     mul=cn["invi"]), waits=[("V", va)])
    aq = S.op("scalar", lambda sc: sc.mul(out=cn["t2"], in_=cn["numi"],
                                          mul=cn["invr"]), inc="A")
    vb = S.op("vector", lambda v: v.tensor_add(
        out=cn["adi"], in0=cn["t1"], in1=cn["t2"]), waits=[("A", aq)],
        inc="V")
    S.op("scalar", lambda sc: sc.activation(out=cn["t1"], in_=cn["adr"],
                                            func=AF.Square),
         waits=[("V", vb)])
    aq = S.op("scalar", lambda sc: sc.activation(out=cn["t2"], in_=cn["adi"],
                                                 func=AF.Square), inc="A")
    vr2 = S.op("vector", lambda v: v.tensor_add(
        out=cn["r2"], in0=cn["t1"], in1=cn["t2"]), waits=[("A", aq)],
        inc="V")
    S.op("scalar", lambda sc: sc.activation(out=cn["lnr2"], in_=cn["r2"],
                                            func=AF.Ln), waits=[("V", vr2)])
    S.op("scalar", lambda sc: sc.mul(out=cn["lr"], in_=cn["lnr2"], mul=0.5))
    S.op("scalar", lambda sc: sc.mul(out=cn["nlr"], in_=cn["lr"], mul=-1.0))
    aadr = S.op("scalar", lambda sc: sc.copy(out=cn["t1"], in_=cn["adr"]),
                inc="A")
    rv2 = S.op("vector", lambda v: v.reciprocal(out=cn["t2"], in_=cn["t1"]),
               waits=[("A", aadr)], inc="V")
    S.op("scalar", lambda sc: sc.mul(out=cn["ratio"], in_=cn["adi"],
                                     mul=cn["t2"]), waits=[("V", rv2)])
    S.op("scalar", lambda sc: sc.activation(out=cn["th"], in_=cn["ratio"],
                                            func=AF.Arctan))
    S.op("scalar", lambda sc: sc.activation(out=cn["ath"], in_=cn["th"],
                                            func=AF.Abs))
    a3 = S.op("scalar", lambda sc: sc.activation(out=cn["sgn"], in_=cn["th"],
                                                 func=AF.Sign), inc="A")
    S.op("scalar", lambda sc: sc.mul(out=cn["sivr"], in_=cn["invr"],
                                     mul=SQRT_DT))
    sv2 = S.op("scalar", lambda sc: sc.mul(out=cn["sivi"], in_=cn["invi"],
                                           mul=SQRT_DT), inc="A")

    S.op("vector", lambda v: v.memset(onesb, 1.0))
    a1 = S.op("scalar", lambda sc: sc.mul(
        out=c1, in_=trow, mul=cn["ath"]),
        waits=[("DS", trl), ("A", sv2)])
    a2 = S.op("scalar", lambda sc: sc.activation(
        out=c2, in_=trow, func=AF.Identity, scale=cn["ath"],
        bias=float(0.5 * np.pi)), inc="A")
    S.op("vector", lambda v: v.tensor_scalar_mul(
        out=hrb, in0=c1, scalar1=float(1.0 / TWO_PI)), waits=[("A", a2)])
    S.op("vector", lambda v: v.tensor_scalar_mul(
        out=hib, in0=c2, scalar1=float(1.0 / TWO_PI)))
    S.op("vector", lambda v: v.tensor_copy(out=ibuf, in_=hrb))
    S.op("vector", lambda v: v.tensor_copy(out=ibuf2, in_=hib))
    S.op("vector", lambda v: v.tensor_copy(out=hrb, in_=ibuf))
    S.op("vector", lambda v: v.tensor_copy(out=hib, in_=ibuf2))
    S.op("vector", lambda v: v.scalar_tensor_tensor(
        out=hrb, in0=hrb, scalar=-TWO_PI, in1=c1, op0=ALU.mult,
        op1=ALU.add))
    vm = S.op("vector", lambda v: v.scalar_tensor_tensor(
        out=hib, in0=hib, scalar=-TWO_PI, in1=c2, op0=ALU.mult,
        op1=ALU.add), inc="V")
    art = S.op("scalar", lambda sc: sc.activation(
        out=rtb, in_=trow, func=AF.Exp, scale=cn["lr"]),
        waits=[("V", vm)], inc="A")
    armt = S.op("scalar", lambda sc: sc.activation(
        out=rmtb, in_=trow, func=AF.Exp, scale=cn["nlr"]), inc="A")
    S.op("scalar", lambda sc: sc.activation(
        out=sinb, in_=hrb, func=AF.Sin))
    am2 = S.op("scalar", lambda sc: sc.activation(
        out=cosb, in_=hib, func=AF.Sin), inc="A")
    sgn_done = S.op("scalar", lambda sc: sc.mul(
        out=sinb, in_=sinb, mul=cn["sgn"]), inc="A")

    pb_read = {i: ("V", S.cnt["V"]) for i in range(8)}
    aouts = {}
    last_store = None
    for i in range(NTILE):
        ld = S.dma("sync", xtile[i % 2], in_ap[i * 128:(i + 1) * 128, :],
                   waits=([aouts[i - 2]] if i >= 2 else S.guard()))
        pw = [("DS", ld)]
        if i >= 1:
            pw.append(aouts[i - 1])
        a = ln_tile(S, xtile[i % 2], u_bf, scrLN, st2[i % 2], pre_waits=pw)
        aouts[i] = a

        def trf(t, i=i):
            last = None
            for ic in range(8):
                last = t.transpose(
                    out=PBT[:, ic * 128:(ic + 1) * 128],
                    in_=u_bf[:, ic * 128:(ic + 1) * 128],
                    identity=cst["ident_bf"])
            return last
        tm = S.op("tensor", trf, waits=[a, pb_read[7],
                                        ("DG", cst["loaded_g"])], inc="T")
        pv = S.op("vector", lambda v, i=i: v.tensor_copy(
            out=uT[:, :, i * 128:(i + 1) * 128],
            in_=PBT[:, :].rearrange("p (ic q) -> p ic q", ic=8)),
            waits=[("T", tm)], inc="V")
        pb_read[7] = ("V", pv)
    uT_ok = S.cnt["V"]

    bu_done = None
    for tg in range(4):
        bk = tg % 2
        tm = mm_group(S, PB[bk][:, :],
                      [(BTb[:, ic, :], uT[:, ic, tg * 512:(tg + 1) * 512])
                       for ic in range(8)],
                      waits=[("V", uT_ok), ("DG", btl), pb_read[bk]])
        S.op("scalar", lambda sc, bk=bk, tg=tg: sc.mul(
            out=bu_r[:, tg * 512:(tg + 1) * 512], in_=PB[bk][:, :],
            mul=cn["sivr"]), waits=[("T", tm), ("A", sv2)])
        bu_done = S.op("scalar", lambda sc, bk=bk, tg=tg: sc.mul(
            out=bu_i[:, tg * 512:(tg + 1) * 512], in_=PB[bk][:, :],
            mul=cn["sivi"]), inc="A")
        pb_read[bk] = ("A", bu_done)

    wv = [("A", bu_done), ("A", sgn_done), ("A", am2), ("A", armt)]
    S.op("vector", lambda v: v.tensor_mul(out=c1, in0=cosb, in1=rmtb),
         waits=wv)
    S.op("vector", lambda v: v.tensor_mul(out=c2, in0=sinb, in1=rmtb))
    S.op("vector", lambda v: v.tensor_mul(out=hrb, in0=bu_r, in1=c1))
    S.op("vector", lambda v: v.tensor_mul(out=ibuf.bitcast(F32), in0=bu_i,
                                          in1=c2))
    S.op("vector", lambda v: v.tensor_add(out=hrb, in0=hrb,
                                          in1=ibuf.bitcast(F32)))
    S.op("vector", lambda v: v.tensor_mul(out=hib, in0=bu_i, in1=c1))
    S.op("vector", lambda v: v.tensor_mul(out=ibuf.bitcast(F32), in0=bu_r,
                                          in1=c2))
    S.op("vector", lambda v: v.tensor_sub(out=hib, in0=hib,
                                          in1=ibuf.bitcast(F32)))
    S.op("vector", lambda v: v.tensor_tensor_scan(
        out=bu_r, data0=onesb, data1=hrb, initial=0.0, op0=ALU.mult,
        op1=ALU.add))
    S.op("vector", lambda v: v.tensor_tensor_scan(
        out=bu_i, data0=onesb, data1=hib, initial=0.0, op0=ALU.mult,
        op1=ALU.add))
    S.op("vector", lambda v: v.tensor_mul(out=c1, in0=cosb, in1=rtb))
    S.op("vector", lambda v: v.tensor_mul(out=c2, in0=sinb, in1=rtb))
    S.op("vector", lambda v: v.tensor_mul(out=hrb, in0=bu_r, in1=c1))
    S.op("vector", lambda v: v.tensor_mul(out=hib, in0=bu_i, in1=c2))
    S.op("vector", lambda v: v.tensor_sub(out=hrb, in0=hrb, in1=hib))
    hv = S.op("vector", lambda v: v.tensor_copy(out=h_bf, in_=hrb), inc="V")
    if dbg:
        for nm, ap in (("g_cols", cols), ("g_rmt", rmtb), ("g_rt", rtb),
                       ("g_cos", cosb), ("g_sin", sinb), ("g_bur", bu_r),
                       ("g_hr", hrb)):
            if nm in dbg:
                S.dma("sync", dbg[nm], ap, waits=[("V", hv)])

    ev = None
    for i in range(NTILE):
        bks = (2 + (i % 2) * 2, 3 + (i % 2) * 2)
        tm = {}
        for dg in range(2):
            tm[dg] = mm_group(S, PB[bks[dg]][:, :],
                              [(h_bf[:, i * 128:(i + 1) * 128],
                                CTb[:, dg * 512:(dg + 1) * 512])],
                              waits=[("V", hv), ("DG", ctl),
                                     pb_read[bks[dg]]])
        ld = S.dma("sync", xtile[i % 2], in_ap[i * 128:(i + 1) * 128, :],
                   waits=[("DS", last_store), aouts[NTILE - 1]])
        for dg in range(2):
            ev = S.op("vector", lambda v, i=i, dg=dg, bks=bks: v.tensor_add(
                out=xout_sb[i % 2][:, dg * 512:(dg + 1) * 512],
                in0=PB[bks[dg]][:, :],
                in1=xtile[i % 2][:, dg * 512:(dg + 1) * 512]),
                waits=[("T", tm[dg]), ("DS", ld)], inc="V")
            pb_read[bks[dg]] = ("V", ev)
        last_store = S.dma("sync", out_ap[i * 128:(i + 1) * 128, :],
                           xout_sb[i % 2], waits=[("V", ev)])
    ar.pop()
    return last_store


def l1attn_host_arrays(inputs):
    return {
        "l1a_BT": np.ascontiguousarray(inputs["L1_attn_B"].T),
        "l1a_CT": np.ascontiguousarray(inputs["L1_attn_C"].T),
        "l1a_Ar": np.ascontiguousarray(
            inputs["L1_attn_A_real"].reshape(128, 1)),
        "l1a_Ai": np.ascontiguousarray(
            inputs["L1_attn_A_imag"].reshape(128, 1)),
    }


def declare_l1attn(nc):
    return {
        "l1a_BT": nc.dram_tensor("l1a_BT", [D, 128], F32,
                                 kind="ExternalInput").ap(),
        "l1a_CT": nc.dram_tensor("l1a_CT", [128, D], F32,
                                 kind="ExternalInput").ap(),
        "l1a_Ar": nc.dram_tensor("l1a_Ar", [128, 1], F32,
                                 kind="ExternalInput").ap(),
        "l1a_Ai": nc.dram_tensor("l1a_Ai", [128, 1], F32,
                                 kind="ExternalInput").ap(),
        "c_trow": nc.dram_tensor("c_trow", [128, T], F32,
                                 kind="ExternalInput").ap(),
    }


def build_l1attn_program(dbg_names=()):
    nc = bass.Bass("TRN2", target_bir_lowering=False, debug=False)
    xin = nc.dram_tensor("xin", [T, D], F32, kind="ExternalInput").ap()
    xout = nc.dram_tensor("xout", [T, D], F32, kind="ExternalOutput").ap()
    dbg = {}
    for nm in dbg_names:
        shp = [128, 64] if nm == "g_cols" else [128, T]
        dbg[nm] = nc.dram_tensor(nm, shp, F32, kind="ExternalOutput").ap()
    S = Sched(nc)
    with contextlib.ExitStack() as stk:
        ar, PB, PBT = _alloc_common(stk, nc, S)
        cst = declare_consts(S, nc, ar)
        W = declare_l1attn(nc)
        emit_l1attn(S, nc, ar, cst, W, xin, xout, PB, PBT, dbg=dbg or None)
        S.op("sync", lambda s: s.wait_ge(S.sems["DS"], S.cnt["DS"]))
        block = stk.enter_context(nc.Block())
        S.emit(block)
    return nc



INV_SQRT32 = float(1.0 / np.sqrt(32.0))


def emit_l1ffn(S, nc, ar, cst, W, in_ap, out_ap, xb2, PB, PBT, upto=None,
               dbg=None):
    def bail(extra=()):
        vz = S.op("vector", lambda v: v.memset(xtile[0], 0.0),
                  waits=S.guard(), inc="V")
        st = S.dma("sync", out_ap[0:128, :], xtile[0], waits=[("V", vz)])
        if dbg:
            for nm, ap in extra:
                if nm in dbg:
                    st = S.dma("sync", dbg[nm], ap, waits=[("V", vz)])
        return st
    S.tensor_fence()
    ar.push()
    tTreg = ar.alloc((128, 8, T), BF16)
    hT = tTreg
    tT = tTreg
    xtile = [ar.alloc((128, D), F32) for _ in range(2)]
    gate_all = ar.alloc((128, 16, 8), F32)
    ar.push()
    xlnb = ar.alloc((128, D), F32)
    t0f = ar.alloc((128, D), F32)
    h_bf = ar.alloc((128, D), BF16)
    scrLN = ar.alloc((128, D), F32)
    g2rep = ar.alloc((128, D), F32)
    qss = ar.alloc((128, 16, 16), F32)
    kss = ar.alloc((128, 16, 16), F32)
    biascols = ar.alloc((128, 16), F32)
    mrow = ar.alloc((128, 64), F32)
    rowp = ar.alloc((128, 1056), F32)
    rowbf = rowp.bitcast(BF16)
    g2row = rowp[0:1, 0:1024]
    nbrow_bf = rowbf[0:1, 2048:2080]
    scrsm = ar.alloc((128, 32), F32)
    st2 = [make_stats(scrsm, j) for j in range(2)]
    gsc = ar.alloc((128, 64), F32)
    e_sb = gsc[:, 0:8]
    ism = gsc[:, 8:16]
    em_s = gsc[:, 16:24]
    ind_s = gsc[:, 24:32]
    ge_s = gsc[:, 32:40]
    nmx8 = gsc[:, 40:41]
    m1c = gsc[:, 41:42]
    m2c = gsc[:, 42:43]
    s12c = gsc[:, 43:44]
    rsumc = gsc[:, 44:45]
    m1g = gsc[:, 45:46]
    m2g = gsc[:, 46:47]
    s12g = gsc[:, 47:48]
    rsg = gsc[:, 48:49]
    m1v8 = gsc[:, 56:64]
    m2v8 = mrow[:, 32:40]
    rsv8 = mrow[:, 40:48]

    wdsb = ar.alloc((128, 8, 512), BF16)
    wqsb = ar.alloc((128, 4, 512), BF16)
    wksb = ar.alloc((128, 4, 512), BF16)
    wvsb = ar.alloc((128, 4, 512), BF16)
    wupsb = ar.alloc((128, 4, D), BF16)
    rsb = ar.alloc((128, 8, 8), BF16)
    g1m = ar.alloc((128, 8), F32)
    wdl = S.dma("gpsimd", wdsb,
                W["l1f_wdown"].rearrange("(ic p) l -> p ic l", p=128),
                waits=S.guard())
    S.dma("gpsimd", wqsb, W["l1f_wq"].rearrange("(lc p) o -> p lc o", p=128))
    S.dma("gpsimd", wksb, W["l1f_wk"].rearrange("(lc p) o -> p lc o", p=128))
    wvl = S.dma("gpsimd", wvsb,
                W["l1f_wv"].rearrange("(lc p) o -> p lc o", p=128))
    wul = S.dma("gpsimd", wupsb,
                W["l1f_wup"].rearrange("(lc p) d -> p lc d", p=128))
    rl = S.dma("gpsimd", rsb, W["l1f_router"].rearrange(
        "(ic p) e -> p ic e", p=128))
    g1l = S.dma("sync", g1m, W["l1f_g1m"], waits=S.guard())
    g2l = S.dma("sync", g2row, W["l1f_g2row"])

    wsc = None
    for ic in range(8):
        wsc = S.op("vector", lambda v, ic=ic: v.tensor_scalar(
            out=wdsb[:, ic, :], in0=wdsb[:, ic, :],
            scalar1=g1m[:, ic:ic + 1], scalar2=C0, op0=ALU.mult,
            op1=ALU.mult),
            waits=[("DG", wdl)] + ([("DS", g1l)] if ic == 0 else []),
            inc="V")

    pb_read = {i: ("V", S.cnt["V"]) for i in range(8)}
    for dg in range(2):
        tm = mm_group(S, PB[6][:, 0:512],
                      [(cst["ones_f"][0:1, :],
                        g2row[0:1, dg * 512:(dg + 1) * 512])],
                      waits=[("DS", g2l), ("DG", cst["loaded_g"]),
                             pb_read[6]])
        g2d = S.op("vector", lambda v, dg=dg: v.tensor_copy(
            out=g2rep[:, dg * 512:(dg + 1) * 512], in0=PB[6][:, 0:512])
            if False else v.tensor_scalar_mul(
            out=g2rep[:, dg * 512:(dg + 1) * 512], in0=PB[6][:, 0:512],
            scalar1=1.0), waits=[("T", tm)], inc="V")
        pb_read[6] = ("V", g2d)

    aouts = {}
    last_st2 = None
    for i in range(NTILE):
        ld = S.dma("sync", xtile[i % 2], in_ap[i * 128:(i + 1) * 128, :],
                   waits=([aouts[i - 2]] if i >= 2 else S.guard()))
        pw = [("DS", ld)]
        if i >= 1:
            pw.append(aouts[i - 1])
        a = ln_tile(S, xtile[i % 2], xlnb, scrLN, st2[i % 2], pre_waits=pw,
                    out_waits=[("DS", last_st2)])
        aouts[i] = a
        last_st2 = S.dma("sync", xb2[i * 128:(i + 1) * 128, :], xlnb,
                         waits=[a])
        hv = S.op("vector", lambda v: v.tensor_copy(out=h_bf, in_=xlnb),
                  waits=[a], inc="V")

        def trf(t, i=i):
            last = None
            for ic in range(8):
                last = t.transpose(
                    out=PBT[:, ic * 128:(ic + 1) * 128],
                    in_=h_bf[:, ic * 128:(ic + 1) * 128],
                    identity=cst["ident_bf"])
            return last
        tm = S.op("tensor", trf, waits=[("V", hv), pb_read[7]], inc="T")
        pv = S.op("vector", lambda v, i=i: v.tensor_copy(
            out=hT[:, :, i * 128:(i + 1) * 128],
            in_=PBT[:, :].rearrange("p (ic q) -> p ic q", ic=8)),
            waits=[("T", tm)], inc="V")
        pb_read[7] = ("V", pv)
        aouts[i] = ("V", pv)
    hT_ok = S.cnt["V"]
    if upto == "p1":
        return bail()

    cTreg = ar.alloc((128, 4, T), BF16)
    cT = cTreg
    oT = cTreg
    qT = ar.alloc((128, 4, T), BF16)
    kT = ar.alloc((128, 4, T), BF16)
    v_tok = ar.alloc((128, 16, 512), BF16)
    qk_bf = ar.alloc((128, 512), BF16)
    rsc = [ar.alloc((128, 256), F32) for _ in range(2)]
    costb = [ar.alloc((128, 256), F32) for _ in range(2)]
    sintb = [ar.alloc((128, 256), F32) for _ in range(2)]
    Abuf = ar.alloc((128, 4, 512), BF16)
    rl_sb = ar.alloc((128, 512), F32)
    rl2_sb = ar.alloc((128, 512), F32)

    last_ev = None
    for lc in range(4):
        for tg in range(4):
            bk = (lc * 4 + tg) % 3
            mw = [pb_read[bk], ("V", wsc)]
            if lc == 0 and tg == 0:
                mw.append(("V", hT_ok))
            tm = mm_group(S, PB[bk][:, :],
                          [(wdsb[:, ic, lc * 128:(lc + 1) * 128],
                            hT[:, ic, tg * 512:(tg + 1) * 512])
                           for ic in range(8)], waits=mw)
            last_ev = S.op("vector", lambda v, lc=lc, tg=tg, bk=bk: v.tensor_copy(
                out=cT[:, lc, tg * 512:(tg + 1) * 512], in_=PB[bk][:, :]),
                waits=[("T", tm)], inc="V")
            pb_read[bk] = ("V", last_ev)
    cT_ok = S.cnt["V"]
    if upto == "p2":
        return bail([("d_cT", cT[:, :, :].rearrange("p a b -> p (a b)")), ("d_hT", hT[:, :, :].rearrange("p a b -> p (a b)")), ("d_st", scrsm)])

    prev_tr = {}
    vvv = None
    for i in range(NTILE):
        cl = S.dma("sync", costb[i % 2], W["c_cos"][:, i * 256:(i + 1) * 256],
                   waits=[("V", prev_tr.get(("k", i - 2)))])
        sl = S.dma("sync", sintb[i % 2], W["c_sin"][:, i * 256:(i + 1) * 256])
        for which, wsb, dstT in (("q", wqsb, qT), ("k", wksb, kT)):
            bk = (3 if which == "q" else 4) if i % 2 == 0 else                 (0 if which == "q" else 1)
            mw = [pb_read[bk], ("DG", wvl)]
            if i == 0 and which == "q":
                mw.append(("V", cT_ok))
            tm = mm_group(S, PB[bk][:, :],
                          [(cT[:, lc, i * 128:(i + 1) * 128],
                            wsb[:, lc, :]) for lc in range(4)], waits=mw)
            ps = PB[bk][:, :]
            S.op("vector", lambda v, ps=ps, i=i: v.tensor_tensor(
                out=rsc[0], in0=ps.rearrange("p (c two) -> p c two", two=2
                                             )[:, :, 0],
                in1=costb[i % 2], op=ALU.mult),
                waits=[("T", tm), ("DS", sl)])
            S.op("vector", lambda v, ps=ps, i=i: v.tensor_tensor(
                out=rsc[1], in0=ps.rearrange("p (c two) -> p c two", two=2
                                             )[:, :, 1],
                in1=sintb[i % 2], op=ALU.mult))
            S.op("vector", lambda v: v.tensor_sub(
                out=qk_bf.rearrange("p (c two) -> p c two", two=2)[:, :, 0],
                in0=rsc[0], in1=rsc[1]))
            S.op("vector", lambda v, ps=ps, i=i: v.tensor_tensor(
                out=rsc[0], in0=ps.rearrange("p (c two) -> p c two", two=2
                                             )[:, :, 0],
                in1=sintb[i % 2], op=ALU.mult))
            S.op("vector", lambda v, ps=ps, i=i: v.tensor_tensor(
                out=rsc[1], in0=ps.rearrange("p (c two) -> p c two", two=2
                                             )[:, :, 1],
                in1=costb[i % 2], op=ALU.mult))
            rdone = S.op("vector", lambda v: v.tensor_add(
                out=qk_bf.rearrange("p (c two) -> p c two", two=2)[:, :, 1],
                in0=rsc[0], in1=rsc[1]), inc="V")
            pb_read[bk] = ("V", rdone)
            dst_ss = qss if which == "q" else kss
            S.op("vector", lambda v, dst_ss=dst_ss, i=i: v.reduce_sum(
                out=dst_ss[:, :, i:i + 1],
                in_=qk_bf.rearrange("p (h hd) -> p h hd", h=16), axis=AX.X,
                apply_absolute_value=False, negate=False))

            def trf2(t, i=i, dstT=dstT):
                last = None
                for lc in range(4):
                    last = t.transpose(
                        out=PBT[:, lc * 128:(lc + 1) * 128],
                        in_=qk_bf[:, lc * 128:(lc + 1) * 128],
                        identity=cst["ident_bf"])
                return last
            tr = S.op("tensor", trf2, waits=[("V", rdone), pb_read[7]],
                      inc="T")
            pv = S.op("vector", lambda v, i=i, dstT=dstT: v.tensor_copy(
                out=dstT[:, :, i * 128:(i + 1) * 128],
                in_=PBT[:, 0:512].rearrange("p (lc q) -> p lc q", lc=4)),
                waits=[("T", tr)], inc="V")
            pb_read[7] = ("V", pv)
            prev_tr[(which, i)] = pv
        bkv = 5 if i % 2 == 0 else 2
        tm = mm_group(S, PB[bkv][:, :],
                      [(cT[:, lc, i * 128:(i + 1) * 128],
                        wvsb[:, lc, :]) for lc in range(4)],
                      waits=[pb_read[bkv]])
        vvv = S.op("vector", lambda v, i=i, bkv=bkv: v.tensor_copy(
            out=v_tok[:, i, :], in_=PB[bkv][:, :]), waits=[("T", tm)],
            inc="V")
        pb_read[bkv] = ("V", vvv)
    qkv_ok = S.cnt["V"]
    if upto == "p3":
        return bail([("d_qT", qT[:, :, :].rearrange("p a b -> p (a b)")), ("d_kT", kT[:, :, :].rearrange("p a b -> p (a b)")), ("d_v", v_tok[:, :, :].rearrange("p a b -> p (a b)"))])

    vq = S.op("vector", lambda v: v.reduce_max(
        out=mrow[:, 0:16].rearrange("p (h one) -> p h one", one=1),
        in_=qss[:, :, :], axis=AX.X), waits=[("V", qkv_ok)])
    vk = S.op("vector", lambda v: v.reduce_max(
        out=mrow[:, 16:32].rearrange("p (h one) -> p h one", one=1),
        in_=kss[:, :, :], axis=AX.X), inc="V")
    mbf = mrow.bitcast(BF16)
    ab1 = S.op("scalar", lambda sc: sc.copy(out=mbf[:, 64:96],
                                            in_=mrow[:, 0:32]),
               waits=[("V", vk)], inc="A")
    trm = S.op("tensor", lambda t: t.transpose(
        out=PBT[0:16, 0:128], in_=mbf[:, 64:80],
        identity=cst["ident_bf"]), waits=[("A", ab1), pb_read[7]])
    trm = S.op("tensor", lambda t: t.transpose(
        out=PBT[0:16, 128:256], in_=mbf[:, 80:96],
        identity=cst["ident_bf"]), inc="T")
    vgq = S.op("vector", lambda v: v.reduce_max(
        out=gsc[0:16, 49:50], in_=PBT[0:16, 0:128], axis=AX.X),
        waits=[("T", trm)])
    vgk = S.op("vector", lambda v: v.reduce_max(
        out=gsc[0:16, 50:51], in_=PBT[0:16, 128:256], axis=AX.X), inc="V")
    pb_read[7] = ("V", vgk)
    aq = S.op("scalar", lambda sc: sc.copy(out=gsc[0:16, 51:52],
                                           in_=gsc[0:16, 49:50]),
              waits=[("V", vgk)])
    aq = S.op("scalar", lambda sc: sc.copy(out=gsc[0:16, 52:53],
                                           in_=gsc[0:16, 50:51]), inc="A")
    vm2 = S.op("vector", lambda v: v.tensor_mul(
        out=gsc[0:16, 53:54], in0=gsc[0:16, 51:52], in1=gsc[0:16, 52:53]),
        waits=[("A", aq)], inc="V")
    S.op("scalar", lambda sc: sc.activation(
        out=gsc[0:16, 54:55], in_=gsc[0:16, 53:54], func=AF.Sqrt),
        waits=[("V", vm2)])
    S.op("scalar", lambda sc: sc.mul(out=gsc[0:16, 55:56],
                                     in_=gsc[0:16, 54:55], mul=INV_SQRT32))
    S.op("scalar", lambda sc: sc.mul(out=gsc[0:16, 56:57],
                                     in_=gsc[0:16, 55:56], mul=-1.0))
    anb = S.op("scalar", lambda sc: sc.copy(
        out=gsc.bitcast(BF16)[0:16, 120:121], in_=gsc[0:16, 56:57]),
        inc="A")
    trn = S.op("tensor", lambda t: t.transpose(
        out=PBT[0:1, 0:16], in_=gsc.bitcast(BF16)[0:16, 120:121],
        identity=cst["ident_bf"][0:16, 0:16]), waits=[("A", anb), pb_read[7]], inc="T")
    vnb = S.op("vector", lambda v: v.tensor_copy(
        out=nbrow_bf[0:1, 0:16], in_=PBT[0:1, 0:16]), waits=[("T", trn)],
        inc="V")
    pb_read[7] = ("V", vnb)
    def bc_fn(t):
        last = None
        for h in range(16):
            last = t.matmul(out=PB[6][:, h:h + 1],
                            lhsT=cst["ones_bf"][0:1, :],
                            rhs=nbrow_bf[0:1, h:h + 1],
                            start=True, stop=True)
        return last
    tmb = S.op("tensor", bc_fn, waits=[("V", vnb), pb_read[6]], inc="T")
    vbc = S.op("vector", lambda v: v.tensor_copy(
        out=biascols, in_=PB[6][:, 0:16]), waits=[("T", tmb)], inc="V")
    pb_read[6] = ("V", vbc)
    if upto == "p4":
        return bail([("d_bias", biascols)])

    for hg in range(4):
        for tg in range(4):
            nchunks = 4 * tg + 4
            av_done = None
            for j in range(nchunks):
                r = j - 4 * tg
                t_lo = max(r, 0) * 128
                n_t = 512 - t_lo
                def smm(t, hg=hg, tg=tg, j=j, t_lo=t_lo, n_t=n_t):
                    last = None
                    for a in range(4):
                        last = t.matmul(
                            out=PB[a][:, t_lo:512],
                            lhsT=kT[32 * a:32 * a + 32, hg,
                                    j * 128:(j + 1) * 128],
                            rhs=qT[32 * a:32 * a + 32, hg,
                                   tg * 512 + t_lo:(tg + 1) * 512],
                            start=True, stop=True,
                            tile_position=(32 * a, 0))
                    return last
                mw = [pb_read[a] for a in range(4)]
                if hg == 0 and tg == 0 and j == 0:
                    mw.append(("V", qkv_ok))
                tms = S.op("tensor", smm, waits=mw, inc="T")
                aev = {}
                for a in range(4):
                    aev[a] = S.op("scalar", lambda sc, a=a, hg=hg, t_lo=t_lo: sc.activation(
                        out=Abuf[:, a, t_lo:512], in_=PB[a][:, t_lo:512],
                        func=AF.Exp, scale=INV_SQRT32,
                        bias=biascols[:, 4 * hg + a:4 * hg + a + 1]),
                        waits=[("T", tms), ("V", vbc)] if a == 0
                        else [], inc="A")
                    pb_read[a] = ("A", aev[a])
                msk = None
                if r >= 0:
                    msk = None
                    for a in range(4):
                        msk = S.op("vector", lambda v, a=a, t_lo=t_lo: v.tensor_tensor(
                            out=Abuf[:, a, t_lo:t_lo + 128],
                            in0=Abuf[:, a, t_lo:t_lo + 128],
                            in1=cst["tri_f"], op=ALU.mult),
                            waits=[("A", aev[3])], inc="V")
                def avmm(t, j=j, hg=hg, t_lo=t_lo):
                    last = None
                    for a in range(4):
                        last = t.matmul(
                            out=PB[4][32 * a:32 * a + 32, t_lo:512],
                            lhsT=v_tok[:, j, (4 * hg + a) * 32:
                                       (4 * hg + a) * 32 + 32],
                            rhs=Abuf[:, a, t_lo:512],
                            start=(j == 0), stop=False,
                            tile_position=(0, 32 * a))
                    for a in range(4):
                        last = t.matmul(
                            out=PB[5][32 * a:32 * a + 32, t_lo:512],
                            lhsT=cst["ones_bf"][:, 0:32],
                            rhs=Abuf[:, a, t_lo:512],
                            start=(j == 0), stop=False,
                            tile_position=(0, 32 * a))
                    return last
                avw = [("A", aev[3])]
                if msk is not None:
                    avw.append(("V", msk))
                if j == 0:
                    avw += [pb_read[4], pb_read[5]]
                av_done = S.op("tensor", avmm, waits=avw, inc="T")
            vrl = S.op("vector", lambda v: v.reciprocal(
                out=rl_sb, in_=PB[5][:, :]), waits=[("T", av_done)],
                inc="V")
            pb_read[5] = ("V", vrl)
            arl2 = S.op("scalar", lambda sc: sc.copy(out=rl2_sb, in_=rl_sb),
                        waits=[("V", vrl)], inc="A")
            vo = S.op("vector", lambda v, hg=hg, tg=tg: v.tensor_tensor(
                out=oT[:, hg, tg * 512:(tg + 1) * 512], in0=PB[4][:, :],
                in1=rl2_sb, op=ALU.mult), waits=[("A", arl2)], inc="V")
            pb_read[4] = ("V", vo)
    oT_ok = S.cnt["V"]
    if upto == "p5":
        return bail([("d_oT", oT[:, :, :].rearrange("p a b -> p (a b)"))])

    xmid_sb = ar.alloc((128, D), F32)
    tbf2 = ar.alloc((128, D), BF16)
    xt6 = [ar.alloc((128, D), F32) for _ in range(2)]
    last_xm = None
    rt_done = {}
    for i in range(NTILE):
        mw = [pb_read[6], ("DG", wul)]
        if i == 0:
            mw.append(("V", oT_ok))
        tm6 = mm_group(S, PB[6][:, :],
                       [(oT[:, hg, i * 128:(i + 1) * 128],
                         wupsb[:, hg, 0:512]) for hg in range(4)], waits=mw)
        tm0 = mm_group(S, PB[0][:, :],
                       [(oT[:, hg, i * 128:(i + 1) * 128],
                         wupsb[:, hg, 512:1024]) for hg in range(4)],
                       waits=[pb_read[0]])
        xl = S.dma("sync", xt6[i % 2], xb2[i * 128:(i + 1) * 128, :],
                   waits=[("DS", last_xm), rt_done.get(i - 2)])
        S.op("vector", lambda v, i=i: v.tensor_add(
            out=xmid_sb[:, 0:512], in0=PB[6][:, :],
            in1=xt6[i % 2][:, 0:512]),
            waits=[("T", tm0), ("DS", xl)])
        xme = S.op("vector", lambda v, i=i: v.tensor_add(
            out=xmid_sb[:, 512:1024], in0=PB[0][:, :],
            in1=xt6[i % 2][:, 512:1024]), inc="V")
        pb_read[6] = ("V", xme)
        pb_read[0] = ("V", xme)
        last_xm = S.dma("sync", xb2[i * 128:(i + 1) * 128, :], xmid_sb,
                        waits=[("V", xme)])
        a2 = ln_tile(S, xmid_sb, t0f, scrLN, st2[i % 2], rms_only=True,
                     pre_waits=[("V", xme)], out_waits=[("DS", last_xm)])
        vt = S.op("vector", lambda v: v.tensor_mul(
            out=tbf2, in0=t0f, in1=g2rep), waits=[a2], inc="V")

        def trf3(t, i=i):
            last = None
            for ic in range(8):
                last = t.transpose(
                    out=PBT[:, ic * 128:(ic + 1) * 128],
                    in_=tbf2[:, ic * 128:(ic + 1) * 128],
                    identity=cst["ident_bf"])
            return last
        tr = S.op("tensor", trf3, waits=[("V", vt), pb_read[7]], inc="T")
        pv = S.op("vector", lambda v, i=i: v.tensor_copy(
            out=tT[:, :, i * 128:(i + 1) * 128],
            in_=PBT[:, :].rearrange("p (ic q) -> p ic q", ic=8)),
            waits=[("T", tr)], inc="V")
        pb_read[7] = ("V", pv)
        rt_done[i] = ("V", pv)
        tmr = mm_group(S, PB[1][:, 0:8],
                       [(tT[:, ic, i * 128:(i + 1) * 128], rsb[:, ic, :])
                        for ic in range(8)],
                       waits=[pb_read[1], ("V", pv), ("DG", rl)])
        v1 = S.op("vector", lambda v: v.reduce_max(
            out=nmx8, in_=PB[1][:, 0:8], axis=AX.X, negate=True),
            waits=[("T", tmr)], inc="V")
        ae = S.op("scalar", lambda sc: sc.activation(
            out=e_sb, in_=PB[1][:, 0:8], func=AF.Exp, bias=nmx8),
            waits=[("V", v1)], inc="A")
        pb_read[1] = ("A", ae)
        vm1 = S.op("vector", lambda v: v.reduce_max(
            out=m1c, in_=e_sb, axis=AX.X), waits=[("A", ae)], inc="V")
        am1b = S.op("scalar", lambda sc: sc.mul(
            out=m1v8, in_=cst["ones_f"][:, 0:8], mul=m1c),
            waits=[("V", vm1)], inc="A")
        vi = S.op("vector", lambda v: v.tensor_tensor(
            out=ism, in0=e_sb, in1=m1v8, op=ALU.is_ge),
            waits=[("A", am1b)], inc="V")
        hb = S.op("scalar", lambda sc: sc.nop(), waits=[("V", vi)], inc="A")
        vi = S.op("vector", lambda v: v.tensor_scalar(
            out=ism, in0=ism, scalar1=-1.0, scalar2=1.0, op0=ALU.mult,
            op1=ALU.add), waits=[("A", hb)], inc="V")
        hb = S.op("scalar", lambda sc: sc.nop(), waits=[("V", vi)], inc="A")
        vem = S.op("vector", lambda v: v.tensor_mul(
            out=em_s, in0=e_sb, in1=ism), waits=[("A", hb)], inc="V")
        hb = S.op("scalar", lambda sc: sc.nop(), waits=[("V", vem)],
                  inc="A")
        vm2b = S.op("vector", lambda v: v.reduce_max(
            out=m2c, in_=em_s, axis=AX.X), waits=[("A", hb)], inc="V")
        am2b = S.op("scalar", lambda sc: sc.mul(
            out=m2v8, in_=cst["ones_f"][:, 0:8], mul=m2c),
            waits=[("V", vm2b)], inc="A")
        vi = S.op("vector", lambda v: v.tensor_tensor(
            out=ind_s, in0=e_sb, in1=m2v8, op=ALU.is_ge),
            waits=[("A", am2b)], inc="V")
        hb = S.op("scalar", lambda sc: sc.nop(), waits=[("V", vi)], inc="A")
        vge = S.op("vector", lambda v: v.tensor_mul(
            out=ge_s, in0=e_sb, in1=ind_s), waits=[("A", hb)], inc="V")
        hb = S.op("scalar", lambda sc: sc.nop(), waits=[("V", vge)],
                  inc="A")
        vs12 = S.op("vector", lambda v: v.reduce_sum(
            out=s12c, in_=ge_s, axis=AX.X), waits=[("A", hb)], inc="V")
        as12 = S.op("scalar", lambda sc: sc.copy(out=s12g, in_=s12c),
                    waits=[("V", vs12)], inc="A")
        vrs = S.op("vector", lambda v: v.reciprocal(out=rsg, in_=s12g),
                   waits=[("A", as12)], inc="V")
        arsb = S.op("scalar", lambda sc: sc.mul(
            out=rsv8, in_=cst["ones_f"][:, 0:8], mul=rsg),
            waits=[("V", vrs)], inc="A")
        vg = S.op("vector", lambda v, i=i: v.tensor_tensor(
            out=gate_all[:, i, :], in0=ge_s, in1=rsv8, op=ALU.mult),
            waits=[("A", arsb)], inc="V")
    gate_ok = S.cnt["V"]
    tT_ok2 = S.cnt["V"]
    if upto == "p6":
        return bail([("d_gate", gate_all[:, :, :].rearrange("p a b -> p (a b)")), ("d_tT", tT[:, :, :].rearrange("p a b -> p (a b)")), ("d_oT", oT[:, :, :].rearrange("p a b -> p (a b)")), ("d_gsc", gsc)])
    ar.pop()

    w1sb = ar.alloc((128, 8, 2048), BF16)
    w3sb = ar.alloc((128, 8, 2048), BF16)
    w2sb = ar.alloc((128, 16, 1024), BF16)
    hid = ar.alloc((128, 16, 512), BF16)
    sil = ar.alloc((128, 512), BF16)
    y_sb = ar.alloc((128, 8, D), F32)
    xm2 = ar.alloc((128, D), F32)
    hid_T = {"hid": None, "y": None, "w13": None, "w2": None}
    for th in range(2):
        ym = S.op("vector", lambda v: v.memset(y_sb[:, :, :], 0.0),
                  waits=S.guard() if th == 0 else [], inc="V")
        y_last = {}
        for e in range(8):
            w13w = S.guard() if (th == 0 and e == 0) else                 [("T", hid_T["w13"])]
            w1l = S.dma("gpsimd", w1sb,
                        W["l1f_w1"][e].rearrange("(ic p) f -> p ic f",
                                                 p=128), waits=w13w)
            w3l = S.dma("gpsimd", w3sb,
                        W["l1f_w3"][e].rearrange("(ic p) f -> p ic f",
                                                 p=128))
            w2w = S.guard() if (th == 0 and e == 0) else                 [("T", hid_T["w2"])]
            w2l = S.dma("gpsimd", w2sb,
                        W["l1f_w2"][e].rearrange("(fc p) d -> p fc d",
                                                 p=128), waits=w2w)
            last_hid_T = None
            for tg in range(2):
                if upto == "p7t0e" and th == 0 and e == 0 and tg == 1:
                    return bail()
                hid_evs = None
                for fc in range(16):
                    bkA = (fc % 2) * 2
                    bkB = bkA + 1
                    mwA = [pb_read[bkA], ("DG", w3l)]
                    if th == 0 and e == 0 and tg == 0 and fc == 0:
                        mwA.append(("V", tT_ok2))
                    tcol = th * 1024 + tg * 512
                    tmA = mm_group(S, PB[bkA][:, :],
                                   [(w1sb[:, ic, fc * 128:(fc + 1) * 128],
                                     tT[:, ic, tcol:tcol + 512])
                                    for ic in range(8)], waits=mwA)
                    tmB = mm_group(S, PB[bkB][:, :],
                                   [(w3sb[:, ic, fc * 128:(fc + 1) * 128],
                                     tT[:, ic, tcol:tcol + 512])
                                    for ic in range(8)],
                                   waits=[pb_read[bkB]])
                    sA = S.op("scalar", lambda sc, bkA=bkA: sc.activation(
                        out=sil, in_=PB[bkA][:, :], func=AF.Silu),
                        waits=[("T", tmA), ("V", hid_evs)], inc="A")
                    hid_evs = S.op("vector", lambda v, bkB=bkB, fc=fc: v.tensor_tensor(
                        out=hid[:, fc, :], in0=PB[bkB][:, :], in1=sil,
                        op=ALU.mult), waits=[("A", sA), ("T", tmB)],
                        inc="V")
                    pb_read[bkA] = ("A", sA)
                    pb_read[bkB] = ("V", hid_evs)
                last_hidmm_T = tmB
                if upto == "p7hid" and th == 0 and e == 0 and tg == 0:
                    return bail()
                for tl in range(4):
                    tile = th * 8 + tg * 4 + tl
                    for dg in range(2):
                        bk = 4 + dg
                        tmy = mm_group(
                            S, PB[bk][:, :],
                            [(hid[:, fc, tl * 128:(tl + 1) * 128],
                              w2sb[:, fc, dg * 512:(dg + 1) * 512])
                             for fc in range(16)],
                            waits=[pb_read[bk], ("V", hid_evs),
                                   ("DG", w2l)])
                        yv = S.op("vector", lambda v, tg=tg, tl=tl, dg=dg, bk=bk, tile=tile, e=e: v.scalar_tensor_tensor(
                            out=y_sb[:, tg * 4 + tl,
                                     dg * 512:(dg + 1) * 512],
                            in0=PB[bk][:, :],
                            scalar=gate_all[:, tile, e:e + 1],
                            in1=y_sb[:, tg * 4 + tl,
                                     dg * 512:(dg + 1) * 512],
                            op0=ALU.mult, op1=ALU.add),
                            waits=[("T", tmy), ("V", ym), ("V", gate_ok),
                                   ("V", y_last.get((tg, tl, dg)))],
                            inc="V")
                        pb_read[bk] = ("V", yv)
                        y_last[(tg, tl, dg)] = yv
                if upto == "p7y" and th == 0 and e == 0 and tg == 0:
                    return bail()
                last_hid_T = tmy
                if upto == "p7n" and th == 0 and e == 0 and tg == 0:
                    return bail()
            hid_T["w13"] = last_hidmm_T
            hid_T["w2"] = last_hid_T
            if upto == "p7a" and th == 0 and e == 0:
                return bail()
        if upto == "p7b" and th == 0:
            return bail()
        for tl8 in range(8):
            tile = th * 8 + tl8
            xl = S.dma("sync", xtile[tl8 % 2],
                       in_ap[tile * 128:(tile + 1) * 128, :],
                       waits=[("DS", last_xm)] if tl8 < 2 else
                       [("DS", last_xm)])
            xm = S.dma("sync", xm2, xb2[tile * 128:(tile + 1) * 128, :],
                       waits=[("V", y_last.get((1, 3, 1)))] if tl8 == 0
                       else [])
            vv1 = S.op("vector", lambda v, tl8=tl8: v.tensor_add(
                out=xm2, in0=xm2, in1=y_sb[:, tl8, :]
                if False else y_sb[:, tl8, :]),
                waits=[("DS", xm), ("V", y_last.get((1, 3, 1)))])
            vv2 = S.op("vector", lambda v, tl8=tl8: v.scalar_tensor_tensor(
                out=xm2, in0=xm2, scalar=0.5,
                in1=xtile[tl8 % 2], op0=ALU.mult, op1=ALU.add),
                waits=[("DS", xl)], inc="V")
            last_xm = S.dma("sync", out_ap[tile * 128:(tile + 1) * 128, :],
                            xm2, waits=[("V", vv2)])
        if upto == "p7c" and th == 0:
            return bail()
        if dbg and "d_y" in dbg and th == 1:
            S.dma("sync", dbg["d_y"],
                  y_sb[:, :, :].rearrange("p a b -> p (a b)"),
                  waits=[("V", y_last.get((1, 3, 1)))])
        if dbg and "d_hid" in dbg and th == 1:
            S.dma("sync", dbg["d_hid"],
                  hid[:, :, :].rearrange("p a b -> p (a b)"),
                  waits=[("V", y_last.get((1, 3, 1)))])
    ar.pop()
    return last_xm


def l1ffn_host_arrays(inputs):
    HD = 32
    inv_freq = 1.0 / (10000.0 ** (np.arange(0, HD, 2, dtype=np.float64)
                                  / HD))
    pos = np.arange(T, dtype=np.float64)
    cos = np.zeros((128, NTILE * 256), np.float32)
    sin = np.zeros((128, NTILE * 256), np.float32)
    for i in range(NTILE):
        p = pos[i * 128:(i + 1) * 128][:, None]
        ang = p * inv_freq[None, :]
        cos_t = np.cos(ang).astype(np.float32)
        sin_t = np.sin(ang).astype(np.float32)
        cos[:, i * 256:(i + 1) * 256] = np.tile(cos_t, (1, 16))
        sin[:, i * 256:(i + 1) * 256] = np.tile(sin_t, (1, 16))
    return {
        "l1f_wdown": np.ascontiguousarray(inputs["L1_ffn_w_down"]),
        "l1f_wq": np.ascontiguousarray(inputs["L1_ffn_wq"]),
        "l1f_wk": np.ascontiguousarray(inputs["L1_ffn_wk"]),
        "l1f_wv": np.ascontiguousarray(inputs["L1_ffn_wv"]),
        "l1f_wup": np.ascontiguousarray(inputs["L1_ffn_w_up"]),
        "l1f_router": np.ascontiguousarray(inputs["L1_ffn_router_w"]),
        "l1f_w1": np.ascontiguousarray(inputs["L1_ffn_w1"]),
        "l1f_w3": np.ascontiguousarray(inputs["L1_ffn_w3"]),
        "l1f_w2": np.ascontiguousarray(inputs["L1_ffn_w2"]),
        "l1f_g1m": np.ascontiguousarray(
            inputs["L1_ffn_g1"].reshape(8, 128).T),
        "l1f_g2row": np.ascontiguousarray(
            inputs["L1_ffn_g2"].reshape(1, D)),
        "c_cos": cos,
        "c_sin": sin,
    }


def declare_l1ffn(nc):
    def dt_(name, shape):
        return nc.dram_tensor(name, shape, F32, kind="ExternalInput").ap()
    return {
        "l1f_wdown": dt_("l1f_wdown", [D, 512]),
        "l1f_wq": dt_("l1f_wq", [512, 512]),
        "l1f_wk": dt_("l1f_wk", [512, 512]),
        "l1f_wv": dt_("l1f_wv", [512, 512]),
        "l1f_wup": dt_("l1f_wup", [512, D]),
        "l1f_router": dt_("l1f_router", [D, 8]),
        "l1f_w1": dt_("l1f_w1", [8, D, 2048]),
        "l1f_w3": dt_("l1f_w3", [8, D, 2048]),
        "l1f_w2": dt_("l1f_w2", [8, 2048, D]),
        "l1f_g1m": dt_("l1f_g1m", [128, 8]),
        "l1f_g2row": dt_("l1f_g2row", [1, D]),
        "c_cos": dt_("c_cos", [128, NTILE * 256]),
        "c_sin": dt_("c_sin", [128, NTILE * 256]),
    }


L1F_DBG_SHAPES = {
    "d_cT": ([128, 4 * T], BF16), "d_hT": ([128, 8 * T], BF16),
    "d_qT": ([128, 4 * T], BF16), "d_kT": ([128, 4 * T], BF16),
    "d_v": ([128, 16 * 512], BF16), "d_bias": ([128, 16], F32),
    "d_oT": ([128, 4 * T], BF16), "d_gate": ([128, 16 * 8], F32),
    "d_tT": ([128, 8 * T], BF16),
    "d_st": ([128, 32], F32),
    "d_gsc": ([128, 64], F32),
    "d_y": ([128, 8 * D], F32),
    "d_hid": ([128, 16 * 512], BF16),
}


def build_l1ffn_program(upto=None, dbg_names=(), xb2_out=False):
    nc = bass.Bass("TRN2", target_bir_lowering=False, debug=False)
    xin = nc.dram_tensor("xin", [T, D], F32, kind="ExternalInput").ap()
    xout = nc.dram_tensor("xout", [T, D], F32, kind="ExternalOutput").ap()
    xb2 = nc.dram_tensor("xb2", [T, D], F32,
                         kind="ExternalOutput" if xb2_out else
                         "Internal").ap()
    dbg = {}
    for nm in dbg_names:
        shp, dt = L1F_DBG_SHAPES[nm]
        dbg[nm] = nc.dram_tensor(nm, shp, dt, kind="ExternalOutput").ap()
    S = Sched(nc)
    with contextlib.ExitStack() as stk:
        ar, PB, PBT = _alloc_common(stk, nc, S)
        cst = declare_consts(S, nc, ar)
        W = declare_l1ffn(nc)
        emit_l1ffn(S, nc, ar, cst, W, xin, xout, xb2, PB, PBT, upto=upto,
                   dbg=dbg or None)
        S.op("sync", lambda s: s.wait_ge(S.sems["DS"], S.cnt["DS"]))
        block = stk.enter_context(nc.Block())
        S.emit(block)
    return nc





ACT_CONSTS = [1.0 / D, -0.5, LN_EPS, 1e-6, float(1.0 / np.sqrt(32.0)),
              0.5, 2.0, 1.0 / 512.0, 1.0, -1.0, 0.005, -0.005,
              float(np.sqrt(np.float32(0.01))), float(0.5 * np.pi)]


def make_const_arrays():
    return {
        "c_ident": np.eye(128, dtype=np.float32),
        "c_tri": np.triu(np.ones((128, 128), np.float32)),
        "c_ones": np.ones((128, 128), np.float32),
        "c_actc": np.tile(np.asarray(ACT_CONSTS, np.float32)[None, :],
                          (128, 1)),
        "c_trow": np.tile(np.arange(T, dtype=np.float32)[None, :], (128, 1)),
    }


def declare_consts(S, nc, ar):
    cst = {}
    d_ident = nc.dram_tensor("c_ident", [128, 128], F32,
                             kind="ExternalInput").ap()
    d_tri = nc.dram_tensor("c_tri", [128, 128], F32,
                           kind="ExternalInput").ap()
    d_ones = nc.dram_tensor("c_ones", [128, 128], F32,
                            kind="ExternalInput").ap()
    cst["ident_bf"] = ar.alloc((128, 128), BF16)
    cst["tri_f"] = ar.alloc((128, 128), F32)
    cst["tri_bf"] = ar.alloc((128, 128), BF16)
    cst["ones_f"] = ar.alloc((128, 128), F32)
    cst["ones_bf"] = ar.alloc((128, 128), BF16)
    d_actc = nc.dram_tensor("c_actc", [128, len(ACT_CONSTS)], F32,
                            kind="ExternalInput").ap()
    actc = ar.alloc((128, len(ACT_CONSTS)), F32)
    for i, v in enumerate(ACT_CONSTS):
        nc.const_aps.register(F32, float(v), actc[:, i:i + 1])
    S.dma("gpsimd", cst["ident_bf"], d_ident)
    S.dma("gpsimd", cst["tri_bf"], d_tri)
    S.dma("gpsimd", cst["ones_bf"], d_ones)
    cst["loaded_g"] = S.cnt["DG"]
    S.dma("sync", cst["tri_f"], d_tri)
    S.dma("sync", cst["ones_f"], d_ones)
    ld = S.dma("sync", actc, d_actc)
    cst["loaded"] = ld
    return cst


def l0ffn_host_arrays(inputs):
    out = {
        "l0_w1T": np.ascontiguousarray(inputs["L0_ffn_w1_w"].T),
        "l0_w2T": np.ascontiguousarray(inputs["L0_ffn_w2_w"].T),
        "l0_h1T": np.ascontiguousarray(inputs["L0_ffn_h1_w"].T),
        "l0_h2T": np.ascontiguousarray(inputs["L0_ffn_h2_w"].T),
        "l0_w3T": np.ascontiguousarray(inputs["L0_ffn_w3_w"].T),
        "l0_w3s_row": np.ascontiguousarray(
            inputs["L0_ffn_w3_s"].reshape(1, D)),
    }
    for nm, key in (("s1m", "L0_ffn_w1_s"), ("s2m", "L0_ffn_w2_s"),
                    ("h1m", "L0_ffn_h1_s"), ("h2m", "L0_ffn_h2_s")):
        out["l0_" + nm] = np.ascontiguousarray(
            inputs[key].reshape(32, 128).T)
    return out


def declare_l0ffn(S, nc, ar, cst):
    W = {}
    W["w1T"] = nc.dram_tensor("l0_w1T", [D, DHID], F32,
                              kind="ExternalInput").ap()
    W["w2T"] = nc.dram_tensor("l0_w2T", [D, DHID], F32,
                              kind="ExternalInput").ap()
    W["h1T"] = nc.dram_tensor("l0_h1T", [DHID, DHID], F32,
                              kind="ExternalInput").ap()
    W["h2T"] = nc.dram_tensor("l0_h2T", [DHID, DHID], F32,
                              kind="ExternalInput").ap()
    W["w3T"] = nc.dram_tensor("l0_w3T", [DHID, D], F32,
                              kind="ExternalInput").ap()
    for nm in ("s1m", "s2m", "h1m", "h2m"):
        d = nc.dram_tensor("l0_" + nm, [128, 32], F32,
                           kind="ExternalInput").ap()
        W[nm] = ar.alloc((128, 32), F32)
        S.dma("sync", W[nm], d)
    d = nc.dram_tensor("l0_w3s_row", [1, D], F32, kind="ExternalInput").ap()
    w3row_full = ar.alloc((128, D), F32)[0:1, :]
    ld = S.dma("sync", w3row_full, d)
    cst["w3s_row"] = w3row_full
    cst["loaded"] = ld
    return W



def _alloc_common(stk, nc, S):
    big = stk.enter_context(nc.sbuf_tensor("arena", [128, ARENA_WORDS], F32))
    PB = [stk.enter_context(nc.psum_tensor(f"pb{i}", [128, 512], F32))
          for i in range(7)]
    PBT = stk.enter_context(nc.psum_tensor("pbt", [128, 1024], BF16))
    for name in ("T", "V", "A", "P", "DS", "DG"):
        S.sems[name] = stk.enter_context(nc.semaphore(name))
    return Arena(big, ARENA_WORDS), PB, PBT


def build_l0ffn_program():
    nc = bass.Bass("TRN2", target_bir_lowering=False, debug=False)
    xin = nc.dram_tensor("xin", [T, D], F32, kind="ExternalInput").ap()
    xout = nc.dram_tensor("xout", [T, D], F32, kind="ExternalOutput").ap()
    S = Sched(nc)
    with contextlib.ExitStack() as stk:
        ar, PB, PBT = _alloc_common(stk, nc, S)
        cst = declare_consts(S, nc, ar)
        W = declare_l0ffn(S, nc, ar, cst)
        emit_l0ffn(S, nc, ar, cst, W, xin, xout, PB, PBT)
        S.op("sync", lambda s: s.wait_ge(S.sems["DS"], S.cnt["DS"]))
        block = stk.enter_context(nc.Block())
        S.emit(block)
    return nc


def build_fused_program():
    nc = bass.Bass("TRN2", target_bir_lowering=False, debug=False)
    xin = nc.dram_tensor("xin", [T, D], F32, kind="ExternalInput").ap()
    xout = nc.dram_tensor("xout", [T, D], F32, kind="ExternalOutput").ap()
    xbA = nc.dram_tensor("xbA", [T, D], F32, kind="Internal").ap()
    xbB = nc.dram_tensor("xbB", [T, D], F32, kind="Internal").ap()
    xb2 = nc.dram_tensor("xb2", [T, D], F32, kind="Internal").ap()
    S = Sched(nc)
    with contextlib.ExitStack() as stk:
        ar, PB, PBT = _alloc_common(stk, nc, S)
        cst = declare_consts(S, nc, ar)
        W0f = declare_l0ffn(S, nc, ar, cst)
        W0a = declare_l0attn(nc)
        W1a = declare_l1attn(nc)
        W1f = declare_l1ffn(nc)
        emit_l0ffn(S, nc, ar, cst, W0f, xin, xbA, PB, PBT)
        emit_l0attn(S, nc, ar, cst, W0a, xbA, xbB, PB, PBT)
        emit_l0ffn(S, nc, ar, cst, W0f, xbB, xbA, PB, PBT)
        emit_l1ffn(S, nc, ar, cst, W1f, xbA, xbB, xb2, PB, PBT)
        emit_l1attn(S, nc, ar, cst, W1a, xbB, xbA, PB, PBT)
        emit_l1ffn(S, nc, ar, cst, W1f, xbA, xout, xb2, PB, PBT)
        S.op("sync", lambda s: s.wait_ge(S.sems["DS"], S.cnt["DS"]))
        block = stk.enter_context(nc.Block())
        S.emit(block)
    return nc


_PROGRAMS = {}


def _get_program(name):
    if name not in _PROGRAMS:
        if name == "fused":
            _PROGRAMS[name] = build_fused_program()
        elif name == "l0ffn":
            _PROGRAMS[name] = build_l0ffn_program()
        elif name == "l0attn":
            _PROGRAMS[name] = build_l0attn_program()
        elif name == "l1ffn":
            _PROGRAMS[name] = build_l1ffn_program()
        elif name == "l1attn":
            _PROGRAMS[name] = build_l1attn_program()
    return _PROGRAMS[name]


def kernel(**inputs):
    inputs = {k: np.asarray(v) for k, v in inputs.items()}
    n = 8
    x = np.ascontiguousarray(inputs["L0_attn_x"], dtype=np.float32)

    base = dict(make_const_arrays())
    base.update(l0ffn_host_arrays(inputs))
    base.update(l0attn_host_arrays(inputs))
    base.update(l1attn_host_arrays(inputs))
    base.update(l1ffn_host_arrays(inputs))

    def run(name, xs):
        nc = _get_program(name)
        names = {t for t in nc.to_json()["tensors"]} if False else None
        in_maps = []
        for c in range(n):
            m = dict(base)
            m["xin"] = np.ascontiguousarray(xs[c])
            in_maps.append(m)
        res = run_bass_kernel_spmd(nc, in_maps, core_ids=list(range(n)))
        return [res.results[c]["xout"] for c in range(n)]

    xs = [x[c] for c in range(n)]
    import os
    if os.environ.get("MK_UNFUSED"):
        for stage in ("l0ffn", "l0attn", "l0ffn", "l1ffn", "l1attn",
                      "l1ffn"):
            xs = run(stage, xs)
    else:
        xs = run("fused", xs)
    out = np.stack([np.asarray(s) for s in xs], axis=0).astype(np.float32)
    return out


# version 06 (final)
# speedup vs baseline: 1.0767x; 1.0529x over previous
import contextlib
import numpy as np
import ml_dtypes
import concourse.bass as bass
import concourse.mybir as mybir
from concourse.bass_utils import run_bass_kernel_spmd

F32 = mybir.dt.float32
BF16 = mybir.dt.bfloat16
I32 = mybir.dt.int32
AF = mybir.ActivationFunctionType
ALU = mybir.AluOpType
AX = mybir.AxisListType

T = 2048
D = 1024
NTILE = T // 128
HALF = T // 2
DHID = 4096
LN_EPS = 1e-5
ARENA_WORDS = 52000

ENGINES = ("tensor", "vector", "scalar", "gpsimd", "sync")


class Sched:
    def __init__(self, nc):
        self.nc = nc
        self.ops = {e: [] for e in ENGINES}
        self.sems = {}
        self.cnt = {"T": 0, "V": 0, "A": 0, "P": 0, "DS": 0, "DG": 0}

    def op(self, eng, fn, waits=(), inc=None, inc_amt=1):
        val = None
        if inc is not None:
            self.cnt[inc] += inc_amt
            val = self.cnt[inc]
        ws = tuple(w for w in waits if w is not None and w[1] is not None)
        self.ops[eng].append((fn, ws, inc, val))
        return val

    def dma(self, queue, out, in_, waits=(), allow_slow=False):
        sem = "DS" if queue == "sync" else "DG"

        def fn(eng):
            return eng.dma_start(out=out, in_=in_,
                                 allow_slow_non_contiguous=allow_slow)
        return self.op(queue, fn, waits=waits, inc=sem, inc_amt=16)

    def guard(self):
        return [("A", self.cnt["A"] or None), ("V", self.cnt["V"] or None),
                ("T", self.cnt["T"] or None)]

    def tensor_fence(self):
        return self.op("tensor", lambda t: t.nop(), waits=self.guard(),
                       inc="T")

    def emit(self, block):
        def run(eng_obj, eng_name):
            for fn, waits, inc, val in self.ops[eng_name]:
                for (s, v) in waits:
                    eng_obj.wait_ge(self.sems[s], v)
                inst = fn(eng_obj)
                if inc is not None:
                    amt = 16 if inc in ("DS", "DG") else 1
                    inst.then_inc(self.sems[inc], amt)

        @block.tensor
        def _(t):
            run(t, "tensor")

        @block.vector
        def _(v):
            run(v, "vector")

        @block.scalar
        def _(sc):
            run(sc, "scalar")

        @block.gpsimd
        def _(g):
            run(g, "gpsimd")

        @block.sync
        def _(s):
            run(s, "sync")


class Arena:
    def __init__(self, tensor, words):
        self.t = tensor
        self.total = words
        self.off = 0
        self.marks = []

    def push(self):
        self.marks.append(self.off)

    def pop(self):
        self.off = self.marks.pop()

    def alloc(self, shape, dtype):
        free = int(np.prod(shape[1:]))
        words = free if dtype in (F32, I32) else (free + 1) // 2
        words = (words + 7) & ~7
        assert self.off + words <= self.total, \
            f"arena overflow {self.off}+{words}>{self.total}"
        view = self.t[:, self.off:self.off + words]
        self.off += words
        if dtype == BF16:
            view = view.bitcast(BF16)
        elif dtype == I32:
            view = view.bitcast(I32)
        total_free = int(np.prod(shape[1:]))
        view = view[:, :total_free]
        if len(shape) > 2:
            names = " ".join(f"d{i}" for i in range(len(shape) - 1))
            kw = {f"d{i}": int(shape[i + 1]) for i in range(len(shape) - 1)}
            view = view.rearrange(f"p ({names}) -> p {names}", **kw)
        return view


def mm_group(S, psum_ap, pairs, waits=(), start=True, stop=True):
    def fn(t):
        last = None
        n = len(pairs)
        for i, (lh, rh) in enumerate(pairs):
            last = t.matmul(out=psum_ap, lhsT=lh, rhs=rh,
                            start=(start and i == 0),
                            stop=(stop and i == n - 1))
        return last
    return S.op("tensor", fn, waits=waits, inc="T")




STAT_KEYS = ("nsum", "ssq", "nmean", "msq", "veps", "lnl", "rstd", "mrs",
             "rstd_v", "mrs_v")


def make_stats(scr, j):
    base = 16 * j
    st = {k: scr[:, base + i:base + i + 1]
          for i, k in enumerate(STAT_KEYS)}
    st["psrc"] = scr[:, base + 6:base + 8]
    st["pdst"] = scr[:, base + 8:base + 10]
    return st

def ln_tile(S, x_sb, xn_bf, scrap, st, pre_waits=(), out_waits=(),
            rms_only=False):
    if rms_only:
        S.op("scalar", lambda sc: sc.activation(
            out=scrap, in_=x_sb, func=AF.Square, accum_out=st["ssq"]),
            waits=pre_waits)
        S.op("scalar", lambda sc: sc.activation(
            out=st["lnl"], in_=st["ssq"], func=AF.Ln, scale=1.0 / D,
            bias=LN_EPS))
        ar1 = S.op("scalar", lambda sc: sc.activation(
            out=st["rstd"], in_=st["lnl"], func=AF.Exp, scale=-0.5),
            inc="A")
        b1 = S.op("vector", lambda v: v.engine_nop(), waits=[("A", ar1)],
                  inc="V")
        vo = S.op("scalar", lambda sc: sc.mul(
            out=xn_bf, in_=x_sb, mul=st["rstd"]),
            waits=list(out_waits) + [("V", b1)], inc="A")
        return ("A", vo)
    vsum = S.op("vector", lambda v: v.reduce_sum(
        out=st["nsum"], in_=x_sb, axis=AX.X, negate=True),
        waits=pre_waits, inc="V")
    S.op("scalar", lambda sc: sc.activation(
        out=scrap, in_=x_sb, func=AF.Square, accum_out=st["ssq"]),
        waits=pre_waits)
    S.op("scalar", lambda sc: sc.mul(
        out=st["nmean"], in_=st["nsum"], mul=1.0 / D),
        waits=[("V", vsum)])
    aml = S.op("scalar", lambda sc: sc.activation(
        out=st["msq"], in_=st["nmean"], func=AF.Square), inc="A")
    vv = S.op("vector", lambda v: v.scalar_tensor_tensor(
        out=st["veps"], in0=st["ssq"], scalar=1.0 / D, in1=st["msq"],
        op0=ALU.mult, op1=ALU.subtract), waits=[("A", aml)], inc="V")
    S.op("scalar", lambda sc: sc.activation(
        out=st["lnl"], in_=st["veps"], func=AF.Ln, bias=LN_EPS),
        waits=[("V", vv)])
    ar1 = S.op("scalar", lambda sc: sc.activation(
        out=st["rstd"], in_=st["lnl"], func=AF.Exp, scale=-0.5), inc="A")
    b1 = S.op("vector", lambda v: v.engine_nop(), waits=[("A", ar1)],
              inc="V")
    am = S.op("scalar", lambda sc: sc.mul(
        out=st["mrs"], in_=st["nmean"], mul=st["rstd"]),
        waits=[("V", b1)], inc="A")
    b2 = S.op("vector", lambda v: v.engine_nop(), waits=[("A", am)],
              inc="V")
    vo = S.op("scalar", lambda sc: sc.activation(
        out=xn_bf, in_=x_sb, func=AF.Identity, scale=st["rstd"],
        bias=st["mrs"]), waits=list(out_waits) + [("V", b2)], inc="A")
    return ("A", vo)



def emit_l0ffn(S, nc, ar, cst, W, in_ap, out_ap, PB, PBT):
    ar.push()
    S.tensor_fence()
    xnT = ar.alloc((128, 8, HALF), BF16)
    hA = ar.alloc((128, 32, HALF), BF16)
    hB = ar.alloc((128, 32, HALF), BF16)
    w12 = [[ar.alloc((128, 8, 128), BF16) for _ in range(2)] for _ in range(2)]
    hwb = [ar.alloc((128, 32, 128), BF16) for _ in range(2)]
    w3b = [ar.alloc((128, D), BF16) for _ in range(2)]
    xtile = [ar.alloc((128, D), F32) for _ in range(2)]
    scr = ar.alloc((128, 2592), F32)
    w3s_rep = ar.alloc((128, D), F32)
    t1fa = ar.alloc((128, 512), F32)
    scrap = scr[:, 0:1024]
    xn_bf = [scr[:, 1024 + i * 512:1536 + i * 512].bitcast(BF16)
             for i in range(2)]
    stats = [make_stats(scr[:, 2048:2080], j) for j in range(2)]
    tmpS = [scr[:, 2080 + 256 * i:2336 + 256 * i].bitcast(BF16)
            for i in range(2)]
    xres = [scr[:, 0:1024], scr[:, 1024:2048]]

    for dg in range(2):
        tm = mm_group(S, PB[6][:, 0:512],
                      [(cst["ones_f"][0:1, :],
                        cst["w3s_row"][0:1, dg * 512:(dg + 1) * 512])],
                      waits=([("DS", cst["loaded"]), ("DG", cst["loaded_g"])]
                             if dg == 0 else []) + S.guard())
        S.op("vector", lambda v, dg=dg: v.tensor_scalar_mul(
            out=w3s_rep[:, dg * 512:(dg + 1) * 512], in0=PB[6][:, 0:512],
            scalar1=0.5), waits=[("T", tm)], inc="V")

    pb_read = {i: ("V", S.cnt["V"]) for i in range(8)}
    last_store = None

    for half in range(2):
        tile0 = half * 8
        aouts = {}
        tdone = {}
        for i in range(8):
            tl = tile0 + i
            ld = S.dma("sync", xtile[i % 2],
                       in_ap[tl * 128:(tl + 1) * 128, :],
                       waits=([aouts[i - 2]] if i >= 2 else
                              S.guard() + [("DS", last_store)]))
            pw = [("DS", ld)]
            if i >= 1:
                pw.append(aouts[i - 1])
            ow = [("T", tdone[i - 2])] if i >= 2 else []
            a = ln_tile(S, xtile[i % 2], xn_bf[i % 2], scrap, stats[i % 2],
                        pre_waits=pw, out_waits=ow)
            aouts[i] = a

            def tr_fn(t, i=i):
                last = None
                for ic in range(8):
                    last = t.transpose(
                        out=PBT[:, ic * 128:(ic + 1) * 128],
                        in_=xn_bf[i % 2][:, ic * 128:(ic + 1) * 128],
                        identity=cst["ident_bf"])
                return last
            tdone[i] = S.op("tensor", tr_fn, waits=[a, pb_read[7]],
                            inc="T")
            pv = S.op("vector", lambda v, i=i: v.tensor_copy(
                out=xnT[:, :, i * 128:(i + 1) * 128],
                in_=PBT[:, :].rearrange("p (ic q) -> p ic q", ic=8)),
                waits=[("T", tdone[i])], inc="V")
            pb_read[7] = ("V", pv)
        xnT_ready = S.cnt["V"]

        ev_hist = {}
        for oc in range(32):
            b = oc % 2
            wwait = ([("V", ev_hist.get((oc - 2, 1)))] if oc >= 2
                     else S.guard())
            S.dma("gpsimd", w12[b][0],
                  W["w1T"][:, oc * 128:(oc + 1) * 128].rearrange(
                      "(ic p) o -> p ic o", p=128), waits=wwait)
            w2l = S.dma("gpsimd", w12[b][1],
                        W["w2T"][:, oc * 128:(oc + 1) * 128].rearrange(
                            "(ic p) o -> p ic o", p=128))
            for ng in range(2):
                bkA, bkB = 2 * ng, 2 * ng + 1
                mw = [("DG", w2l), pb_read[bkA], pb_read[bkB]]
                if oc == 0 and ng == 0:
                    mw.append(("V", xnT_ready))
                tmA = mm_group(S, PB[bkA][:, :],
                               [(w12[b][0][:, ic, :],
                                 xnT[:, ic, ng * 512:(ng + 1) * 512])
                                for ic in range(8)], waits=mw)
                tmB = mm_group(S, PB[bkB][:, :],
                               [(w12[b][1][:, ic, :],
                                 xnT[:, ic, ng * 512:(ng + 1) * 512])
                                for ic in range(8)])
                sA = S.op("scalar", lambda sc, oc=oc, ng=ng, bkA=bkA: sc.activation(
                    out=tmpS[ng], in_=PB[bkA][:, :], func=AF.Silu,
                    scale=W["s1m"][:, oc:oc + 1]),
                    waits=[("T", tmA), ("V", ev_hist.get((oc - 1, ng)))],
                    inc="A")
                ev = S.op("vector", lambda v, oc=oc, ng=ng, bkB=bkB: v.scalar_tensor_tensor(
                    out=hA[:, oc, ng * 512:(ng + 1) * 512], in0=PB[bkB][:, :],
                    scalar=W["s2m"][:, oc:oc + 1], in1=tmpS[ng],
                    op0=ALU.mult, op1=ALU.mult),
                    waits=[("A", sA), ("T", tmB)], inc="V")
                pb_read[bkA] = ("A", sA)
                pb_read[bkB] = ("V", ev)
                ev_hist[(oc, ng)] = ev
        src_ready = ("V", S.cnt["V"])

        for (wname, scol, src, dst) in (("h1T", "h1m", hA, hB),
                                        ("h2T", "h2m", hB, hA)):
            ev2 = {}
            for oc in range(32):
                b = oc % 2
                wwait = ([("A", ev2.get(oc - 2))] if oc >= 2 else S.guard())
                wl = S.dma("gpsimd", hwb[b],
                           W[wname][:, oc * 128:(oc + 1) * 128].rearrange(
                               "(ic p) o -> p ic o", p=128), waits=wwait)
                for ng in range(2):
                    bk = (2 * oc + ng) % 4
                    mw = [("DG", wl), pb_read[bk]]
                    if oc == 0 and ng == 0:
                        mw.append(src_ready)
                    tm = mm_group(S, PB[bk][:, :],
                                  [(hwb[b][:, ic, :],
                                    src[:, ic, ng * 512:(ng + 1) * 512])
                                   for ic in range(32)], waits=mw)
                    ev = S.op("scalar", lambda sc, oc=oc, ng=ng, bk=bk, dst=dst, scol=scol: sc.activation(
                        out=dst[:, oc, ng * 512:(ng + 1) * 512],
                        in_=PB[bk][:, :], func=AF.Gelu,
                        scale=W[scol][:, oc:oc + 1]),
                        waits=[("T", tm)], inc="A")
                    pb_read[bk] = ("A", ev)
                    ev2[oc] = ev
            src_ready = ("A", S.cnt["A"])
        hC_ready = src_ready

        PB8 = PB[:7] + [PBT.bitcast(F32)[:, 0:512]]
        w3_hist = {}
        for tg in range(2):
            for hc in range(32):
                b = hc % 2
                if (tg, hc) in ((0, 0), (0, 1)):
                    wwait = S.guard()
                else:
                    prev = w3_hist.get((tg, hc - 2), w3_hist.get((tg - 1,
                                                                  30 + hc)))
                    wwait = [("T", prev)]
                wl = S.dma("gpsimd", w3b[b],
                           W["w3T"][hc * 128:(hc + 1) * 128, :], waits=wwait)

                def mm5_fn(t, hc=hc, b=b, tg=tg):
                    last = None
                    for ti in range(4):
                        tloc = tg * 4 + ti
                        for dg in range(2):
                            last = t.matmul(
                                out=PB8[ti * 2 + dg][:, :],
                                lhsT=hA[:, hc, tloc * 128:(tloc + 1) * 128],
                                rhs=w3b[b][:, dg * 512:(dg + 1) * 512],
                                start=(hc == 0), stop=(hc == 31))
                    return last
                mw = [("DG", wl)]
                if hc == 0:
                    mw += [hC_ready] + [pb_read[k] for k in range(8)]
                w3_hist[(tg, hc)] = S.op("tensor", mm5_fn, waits=mw, inc="T")
            mm_done = w3_hist[(tg, 31)]
            for ti in range(4):
                tloc = tg * 4 + ti
                tl = tile0 + tloc
                xl = S.dma("sync", xtile[ti % 2],
                           in_ap[tl * 128:(tl + 1) * 128, :],
                           waits=[("DS", last_store), aouts[7]])
                evs = None
                for dg in range(2):
                    bk = ti * 2 + dg
                    S.op("vector", lambda v, bk=bk, dg=dg: v.tensor_tensor(
                        out=t1fa, in0=PB8[bk][:, :],
                        in1=w3s_rep[:, dg * 512:(dg + 1) * 512], op=ALU.mult),
                        waits=[("T", mm_done), ("DS", xl)])
                    evs = S.op("vector", lambda v, ti=ti, dg=dg: v.tensor_add(
                        out=xres[ti % 2][:, dg * 512:(dg + 1) * 512],
                        in0=t1fa,
                        in1=xtile[ti % 2][:, dg * 512:(dg + 1) * 512]),
                        inc="V")
                    pb_read[bk] = ("V", evs)
                last_store = S.dma(
                    "sync", out_ap[tl * 128:(tl + 1) * 128, :],
                    xres[ti % 2], waits=[("V", evs)])
    ar.pop()
    return last_store



C0 = float(1.0 / np.sqrt(1.0 + LN_EPS))


def emit_l0attn(S, nc, ar, cst, W, in_ap, out_ap, PB, PBT, dbg=None):
    S.tensor_fence()

    def dump(name, ap, waits):
        if dbg is not None and name in dbg:
            dv = S.dma("sync", dbg[name], ap, waits=waits)
            S.op("vector", lambda v: v.engine_nop(), waits=[("DS", dv)],
                 inc="V")
            S.op("scalar", lambda sc: sc.nop(), waits=[("DS", dv)],
                 inc="A")
            S.op("gpsimd", lambda g: g.engine_nop(), waits=[("DS", dv)],
                 inc="P")
    ar.push()
    Wg = ar.alloc((128, 8, 4096), BF16)
    Sst = ar.alloc((128, 8, D), F32)
    Sbf = ar.alloc((128, 8, D), BF16)
    tT = ar.alloc((128, 8, 128), BF16)
    qpT = ar.alloc((128, 8, 128), BF16)
    kpT = ar.alloc((128, 8, 128), BF16)
    kg_f = ar.alloc((128, D), F32)
    gate_f = ar.alloc((128, D), F32)
    qp_bf = ar.alloc((128, D), BF16)
    kp_bf = ar.alloc((128, D), BF16)
    v_bf = ar.alloc((128, D), BF16)
    A_bf = ar.alloc((128, 128), BF16)
    t_bf = ar.alloc((128, D), BF16)
    xtile = [ar.alloc((128, D), F32) for _ in range(2)]
    xout_sb = [ar.alloc((128, D), F32) for _ in range(2)]
    scr0 = ar.alloc((128, 512), F32)
    scr1 = ar.alloc((128, 512), F32)
    scrD = ar.alloc((128, D), F32)
    scrLN = ar.alloc((128, D), F32)
    o1f = ar.alloc((128, D), F32)
    den = ar.alloc((128, 1), F32)
    den2 = ar.alloc((128, 1), F32)
    rden = ar.alloc((128, 1), F32)
    rowpool = ar.alloc((128, 6656), F32)
    rowbf = rowpool.bitcast(BF16)
    bvec = rowpool[0:1, 0:4096]
    bvec_bf = rowbf[0:1, 8192:12288]
    ksum_bf = rowbf[0:1, 12288:13312]
    brow_sb = rowpool[32:33, 0:4096]
    ksum = rowpool[64:65, 0:1024]
    lng = ar.alloc((128, 8), F32)
    lnb_bf = ar.alloc((128, 8), BF16)
    scrsm = ar.alloc((128, 32), F32)
    st2 = [make_stats(scrsm, j) for j in range(2)]

    wl = S.dma("gpsimd", Wg,
               W["l0a_w"].rearrange("(ic p) o -> p ic o", p=128),
               waits=S.guard())
    lnbl = S.dma("gpsimd", lnb_bf, W["l0a_lnb"])
    lngl = S.dma("sync", lng, W["l0a_lng"], waits=S.guard())
    browl = S.dma("sync", brow_sb, W["l0a_brow"])

    vb = None
    for og in range(8):
        tm = mm_group(S, PB[0][0:1, :],
                      [(lnb_bf[:, ic:ic + 1],
                        Wg[:, ic, og * 512:(og + 1) * 512])
                       for ic in range(8)],
                      waits=[("DG", max(wl, lnbl)), ("DS", browl),
                             ("V", vb)] + (S.guard() if og == 0 else []))
        vb = S.op("vector", lambda v, og=og: v.tensor_add(
            out=bvec[:, og * 512:(og + 1) * 512], in0=PB[0][0:1, :],
            in1=brow_sb[:, og * 512:(og + 1) * 512]), waits=[("T", tm)],
            inc="V")
    vbf = S.op("vector", lambda v: v.tensor_copy(out=bvec_bf, in_=bvec),
               inc="V")
    wsc = None
    bvec_T = S.cnt["T"]
    for ic in range(8):
        wsc = S.op("vector", lambda v, ic=ic: v.tensor_scalar(
            out=Wg[:, ic, :], in0=Wg[:, ic, :], scalar1=lng[:, ic:ic + 1],
            scalar2=C0, op0=ALU.mult, op1=ALU.mult),
            waits=([("T", bvec_T), ("DS", lngl)] if ic == 0 else []),
            inc="V")
    S.op("vector", lambda v: v.memset(Sst[:, :, :], 0.0))
    S.op("vector", lambda v: v.memset(ksum, 0.0))
    vz = S.op("vector", lambda v: v.memset(ksum_bf, 0.0), inc="V")

    pb_read = {i: ("V", S.cnt["V"]) for i in range(8)}
    last_store = None
    prev_store = None
    sbf_done = None
    prev_exp = prev_stt = None

    def tr_make(src_bf, dst, ready):
        def trf(t):
            last = None
            for ic in range(8):
                last = t.transpose(
                    out=PBT[:, ic * 128:(ic + 1) * 128],
                    in_=src_bf[:, ic * 128:(ic + 1) * 128],
                    identity=cst["ident_bf"])
            return last
        tm = S.op("tensor", trf, waits=[ready, pb_read[7],
                                        ("DG", cst["loaded_g"])], inc="T")
        pv = S.op("vector", lambda v: v.tensor_copy(
            out=dst[:, :, :],
            in_=PBT[:, :].rearrange("p (ic q) -> p ic q", ic=8)),
            waits=[("T", tm)], inc="V")
        pb_read[7] = ("V", pv)
        return ("V", pv)

    for c in range(NTILE):
        par = c % 2
        ld = S.dma("sync", xtile[par], in_ap[c * 128:(c + 1) * 128, :],
                   waits=[("DS", prev_store)] + (S.guard() if c < 2 else []))
        a = ln_tile(S, xtile[par], t_bf, scrLN, st2[par],
                    pre_waits=[("DS", ld)])
        tT_ok = tr_make(t_bf, tT, a)

        og_order = [0, 1, 6, 7, 2, 3, 4, 5]
        gate_ready = {}
        kmul = {}
        qp_done = kp_done = v_done = None
        for idx, og in enumerate(og_order):
            bk = idx % 4
            mw = [tT_ok, pb_read[bk]]
            if c == 0 and idx == 0:
                mw += [("V", vbf), ("V", wsc), ("V", vz)]
            tm = mm_group(S, PB[bk][:, :],
                          [(cst["ones_bf"][0:1, :],
                            bvec_bf[:, og * 512:(og + 1) * 512])] +
                          [(tT[:, ic, :], Wg[:, ic, og * 512:(og + 1) * 512])
                           for ic in range(8)], waits=mw)
            if og in (0, 1):
                vm = S.op("vector", lambda v, bk=bk: v.tensor_scalar_min(
                    out=scr0, in0=PB[bk][:, :], scalar1=0.0),
                    waits=[("T", tm), ("A", prev_exp)], inc="V")
                a1 = S.op("scalar", lambda sc: sc.activation(
                    out=scr1, in_=scr0, func=AF.Exp),
                    waits=[("V", vm), ("V", prev_stt)], inc="A")
                prev_exp = a1
                qp_done = S.op("vector", lambda v, bk=bk, og=og: v.scalar_tensor_tensor(
                    out=qp_bf[:, og * 512:(og + 1) * 512], in0=PB[bk][:, :],
                    scalar=0.0, in1=scr1, op0=ALU.max, op1=ALU.add),
                    waits=[("A", a1)], inc="V")
                prev_stt = qp_done
                pb_read[bk] = ("V", qp_done)
            elif og in (6, 7):
                g1 = S.op("scalar", lambda sc, bk=bk, og=og: sc.activation(
                    out=gate_f[:, (og - 6) * 512:(og - 5) * 512],
                    in_=PB[bk][:, :], func=AF.Sigmoid),
                    waits=[("T", tm), ("V", kmul.get(og - 6))], inc="A")
                gate_ready[og - 6] = g1
                pb_read[bk] = ("A", g1)
            elif og in (2, 3):
                blk = og - 2
                km = S.op("vector", lambda v, bk=bk, blk=blk: v.tensor_tensor(
                    out=kg_f[:, blk * 512:(blk + 1) * 512], in0=PB[bk][:, :],
                    in1=gate_f[:, blk * 512:(blk + 1) * 512], op=ALU.mult),
                    waits=[("T", tm), ("A", gate_ready[blk])], inc="V")
                kmul[blk] = km
                pb_read[bk] = ("V", km)
                vm = S.op("vector", lambda v, blk=blk: v.tensor_scalar_min(
                    out=scr0, in0=kg_f[:, blk * 512:(blk + 1) * 512],
                    scalar1=0.0), waits=[("A", prev_exp)], inc="V")
                a1 = S.op("scalar", lambda sc: sc.activation(
                    out=scr1, in_=scr0, func=AF.Exp),
                    waits=[("V", vm), ("V", prev_stt)], inc="A")
                prev_exp = a1
                kp_done = S.op("vector", lambda v, blk=blk: v.scalar_tensor_tensor(
                    out=kp_bf[:, blk * 512:(blk + 1) * 512],
                    in0=kg_f[:, blk * 512:(blk + 1) * 512],
                    scalar=0.0, in1=scr1, op0=ALU.max, op1=ALU.add),
                    waits=[("A", a1)], inc="V")
                prev_stt = kp_done
            else:
                v_done = S.op("vector", lambda v, bk=bk, og=og: v.tensor_copy(
                    out=v_bf[:, (og - 4) * 512:(og - 3) * 512],
                    in_=PB[bk][:, :]), waits=[("T", tm)], inc="V")
                pb_read[bk] = ("V", v_done)

        if c == 0:
            dump("d_tbf", t_bf, [a])
            dump("d_qp", qp_bf, [("V", qp_done)])
            dump("d_kp", kp_bf, [("V", kp_done)])
            dump("d_gate", gate_f, [("A", gate_ready[1])])
            dump("d_v", v_bf, [("V", v_done)])
        qpT_ok = tr_make(qp_bf, qpT, ("V", qp_done))
        kpT_ok = tr_make(kp_bf, kpT, ("V", kp_done))

        kc = {}
        for dg in range(2):
            pairs = [(cst["tri_bf"], kp_bf[:, dg * 512:(dg + 1) * 512])]
            if c > 0:
                pairs = [(cst["ones_bf"][0:1, :],
                          ksum_bf[:, dg * 512:(dg + 1) * 512])] + pairs
            kc[dg] = mm_group(S, PB[4 + dg][:, :], pairs,
                              waits=[("V", kp_done), pb_read[4 + dg]])
        ks = {}
        for dg in range(2):
            ks[dg] = mm_group(S, PB[6][0:1, :],
                              [(cst["ones_bf"][:, 0:1],
                                kp_bf[:, dg * 512:(dg + 1) * 512])],
                              waits=[pb_read[6], ("T", kc[1])])
            kv = S.op("vector", lambda v, dg=dg: v.tensor_tensor(
                out=ksum[:, dg * 512:(dg + 1) * 512], in0=PB[6][0:1, :],
                in1=ksum[:, dg * 512:(dg + 1) * 512], op=ALU.add),
                waits=[("T", ks[dg])], inc="V")
            pb_read[6] = ("V", kv)
        kbf = S.op("vector", lambda v: v.tensor_copy(out=ksum_bf, in_=ksum),
                   inc="V")

        dm = None
        for dg in range(2):
            dm = S.op("vector", lambda v, dg=dg: v.tensor_tensor(
                out=scrD[:, dg * 512:(dg + 1) * 512], in0=PB[4 + dg][:, :],
                in1=qp_bf[:, dg * 512:(dg + 1) * 512], op=ALU.mult),
                waits=[("T", kc[dg])], inc="V")
            pb_read[4 + dg] = ("V", dm)
        vr = S.op("vector", lambda v: v.reduce_sum(
            out=den, in_=scrD, axis=AX.X), inc="V")
        ad = S.op("scalar", lambda sc: sc.add(out=den2, in_=den, add=1e-6),
                  waits=[("V", vr)], inc="A")
        rv = S.op("vector", lambda v: v.reciprocal(out=rden, in_=den2),
                  waits=[("A", ad)], inc="V")
        if c == 0:
            dump("d_kcum", scrD, [("V", dm)])
            dump("d_den", den2, [("A", ad)])
        if c == 1:
            dump("e_qp", qp_bf, [("V", qp_done)])
            dump("e_kp", kp_bf, [("V", kp_done)])
            dump("e_kcum", scrD, [("V", dm)])
            dump("e_den", den2, [("A", ad)])

        am = mm_group(S, PB[2][:, 0:128],
                      [(kpT[:, ic, :], qpT[:, ic, :]) for ic in range(8)],
                      waits=[kpT_ok, qpT_ok, pb_read[2]])
        ab = S.op("vector", lambda v: v.tensor_tensor(
            out=A_bf, in0=PB[2][:, 0:128], in1=cst["tri_f"], op=ALU.mult),
            waits=[("T", am)], inc="V")
        pb_read[2] = ("V", ab)

        om = {}
        for dg in range(2):
            pairs = []
            if c > 0:
                pairs += [(qpT[:, ic, :], Sbf[:, ic, dg * 512:(dg + 1) * 512])
                          for ic in range(8)]
            pairs += [(A_bf, v_bf[:, dg * 512:(dg + 1) * 512])]
            mw = [("V", ab), ("V", v_done), pb_read[dg]]
            if c > 0:
                mw.append(("A", sbf_done))
            om[dg] = mm_group(S, PB[dg][:, :], pairs, waits=mw)
        ev = None
        for dg in range(2):
            o1 = S.op("scalar", lambda sc, dg=dg: sc.mul(
                out=o1f[:, dg * 512:(dg + 1) * 512], in_=PB[dg][:, :],
                mul=rden), waits=[("T", om[dg]), ("V", rv)], inc="A")
            ev = S.op("vector", lambda v, dg=dg, par=par: v.tensor_add(
                out=xout_sb[par][:, dg * 512:(dg + 1) * 512],
                in0=o1f[:, dg * 512:(dg + 1) * 512],
                in1=xtile[par][:, dg * 512:(dg + 1) * 512]),
                waits=[("A", o1), ("DS", prev_store)], inc="V")
            pb_read[dg] = ("A", o1)
        if c == 0:
            dump("d_abf", A_bf, [("V", ab)])
            dump("d_o1", o1f, [("A", pb_read[1][1])])
        if c == 1:
            dump("e_o1", o1f, [("A", pb_read[1][1])])
        if c == 2:
            dump("f_sbf", Sbf[:, :, :].rearrange("p a b -> p (a b)"),
                 [("P", sbf_done)])
        prev_store = last_store
        last_store = S.dma("sync", out_ap[c * 128:(c + 1) * 128, :],
                           xout_sb[par], waits=[("V", ev)])

        sadds = []
        for ic in range(8):
            for dg in range(2):
                bk = 3 + (2 * ic + dg) % 3
                sm = mm_group(S, PB[bk][:, :],
                              [(kp_bf[:, ic * 128:(ic + 1) * 128],
                                v_bf[:, dg * 512:(dg + 1) * 512])],
                              waits=[pb_read[bk], ("V", kp_done),
                                     ("V", v_done)])
                sv = S.op("vector", lambda v, ic=ic, dg=dg, bk=bk: v.tensor_tensor(
                    out=Sst[:, ic, dg * 512:(dg + 1) * 512], in0=PB[bk][:, :],
                    in1=Sst[:, ic, dg * 512:(dg + 1) * 512], op=ALU.add),
                    waits=[("T", sm)], inc="V")
                pb_read[bk] = ("V", sv)
            sadds.append(sv)
            sbf_done = S.op("scalar", lambda sc, ic=ic: sc.copy(
                out=Sbf[:, ic, :], in_=Sst[:, ic, :]),
                waits=[("V", sadds[ic])], inc="A")
    ar.pop()
    return last_store


def l0attn_host_arrays(inputs):
    w = np.concatenate([inputs["L0_attn_qkv_w"], inputs["L0_attn_gate_w"]],
                       axis=1)
    brow = np.concatenate([inputs["L0_attn_qkv_b"],
                           inputs["L0_attn_gate_b"]]).reshape(1, 4096)
    return {
        "l0a_w": np.ascontiguousarray(w),
        "l0a_brow": np.ascontiguousarray(brow),
        "l0a_lng": np.ascontiguousarray(
            inputs["L0_attn_ln_g"].reshape(8, 128).T),
        "l0a_lnb": np.ascontiguousarray(
            inputs["L0_attn_ln_b"].reshape(8, 128).T),
    }


def declare_l0attn(nc):
    return {
        "l0a_w": nc.dram_tensor("l0a_w", [D, 4096], F32,
                                kind="ExternalInput").ap(),
        "l0a_brow": nc.dram_tensor("l0a_brow", [1, 4096], F32,
                                   kind="ExternalInput").ap(),
        "l0a_lng": nc.dram_tensor("l0a_lng", [128, 8], F32,
                                  kind="ExternalInput").ap(),
        "l0a_lnb": nc.dram_tensor("l0a_lnb", [128, 8], F32,
                                  kind="ExternalInput").ap(),
    }


def build_l0attn_program(dbg_names=()):
    nc = bass.Bass("TRN2", target_bir_lowering=False, debug=False)
    xin = nc.dram_tensor("xin", [T, D], F32, kind="ExternalInput").ap()
    xout = nc.dram_tensor("xout", [T, D], F32, kind="ExternalOutput").ap()
    dbg = {}
    for nm in dbg_names:
        dt = BF16 if nm in ("d_tbf", "d_qp", "d_kp", "d_v", "d_abf",
                            "e_qp", "e_kp", "f_sbf") else F32
        shp = [128, 128] if nm == "d_abf" else (
            [128, 1] if nm in ("d_den", "e_den") else (
                [128, 8 * D] if nm == "f_sbf" else [128, D]))
        dbg[nm] = nc.dram_tensor(nm, shp, dt, kind="ExternalOutput").ap()
    S = Sched(nc)
    with contextlib.ExitStack() as stk:
        ar, PB, PBT = _alloc_common(stk, nc, S)
        cst = declare_consts(S, nc, ar)
        W = declare_l0attn(nc)
        emit_l0attn(S, nc, ar, cst, W, xin, xout, PB, PBT, dbg=dbg or None)
        S.op("sync", lambda s: s.wait_ge(S.sems["DS"], S.cnt["DS"]))
        block = stk.enter_context(nc.Block())
        S.emit(block)
    return nc



DT_SSM = 0.01
SQRT_DT = float(np.sqrt(np.float32(DT_SSM)))
TWO_PI = float(2.0 * np.pi)
HALF_PI = float(0.5 * np.pi)


def emit_l1attn(S, nc, ar, cst, W, in_ap, out_ap, PB, PBT, dbg=None):
    ar.push()
    S.tensor_fence()
    uT = ar.alloc((128, 8, T), BF16)
    bu_r = ar.alloc((128, T), F32)
    bu_i = ar.alloc((128, T), F32)
    cosb = ar.alloc((128, T), F32)
    sinb = ar.alloc((128, T), F32)
    rtb = ar.alloc((128, T), F32)
    rmtb = ar.alloc((128, T), F32)
    hrb = ar.alloc((128, T), F32)
    hib = ar.alloc((128, T), F32)
    c1 = ar.alloc((128, T), F32)
    c2 = ar.alloc((128, T), F32)
    ibuf = ar.alloc((128, T), I32)
    ibuf2 = ar.alloc((128, T), I32)
    onesb = ar.alloc((128, T), F32)
    trow = ar.alloc((128, T), F32)
    h_bf = ar.alloc((128, T), BF16)
    u_bf = ar.alloc((128, D), BF16)
    xtile = [ar.alloc((128, D), F32) for _ in range(2)]
    xout_sb = [ar.alloc((128, D), F32) for _ in range(2)]
    BTb = ar.alloc((128, 8, 128), BF16)
    CTb = ar.alloc((128, D), BF16)
    scrLN = ar.alloc((128, D), F32)
    cols = ar.alloc((128, 64), F32)
    cn = {}
    for i, k in enumerate(
        ("ar0", "ai0", "denr", "deni", "numr", "numi", "m1", "d2", "rinv",
         "rinv2", "invr", "invi", "adr", "adi", "t1", "t2", "r2", "lnr2",
         "lr", "nlr", "ratio", "th", "ath", "sgn", "sivr", "sivi")):
        cn[k] = cols[:, i:i + 1]
    scrsm = ar.alloc((128, 32), F32)
    st2 = [make_stats(scrsm, j) for j in range(2)]

    btl = S.dma("gpsimd", BTb, W["l1a_BT"].rearrange("(ic p) n -> p ic n",
                                                     p=128), waits=S.guard())
    ctl = S.dma("gpsimd", CTb, W["l1a_CT"])
    arl = S.dma("sync", cn["ar0"], W["l1a_Ar"], waits=S.guard())
    ail = S.dma("sync", cn["ai0"], W["l1a_Ai"])
    trl = S.dma("sync", trow, W["c_trow"])

    pb_read = {i: ("V", S.cnt["V"]) for i in range(8)}
    aouts = {}
    last_store = None
    for i in range(NTILE):
        ld = S.dma("sync", xtile[i % 2], in_ap[i * 128:(i + 1) * 128, :],
                   waits=([aouts[i - 2]] if i >= 2 else S.guard()))
        pw = [("DS", ld)]
        if i >= 1:
            pw.append(aouts[i - 1])
        a = ln_tile(S, xtile[i % 2], u_bf, scrLN, st2[i % 2], pre_waits=pw)
        aouts[i] = a

        def trf(t, i=i):
            last = None
            for ic in range(8):
                last = t.transpose(
                    out=PBT[:, ic * 128:(ic + 1) * 128],
                    in_=u_bf[:, ic * 128:(ic + 1) * 128],
                    identity=cst["ident_bf"])
            return last
        tm = S.op("tensor", trf, waits=[a, pb_read[7],
                                        ("DG", cst["loaded_g"])], inc="T")
        pv = S.op("vector", lambda v, i=i: v.tensor_copy(
            out=uT[:, :, i * 128:(i + 1) * 128],
            in_=PBT[:, :].rearrange("p (ic q) -> p ic q", ic=8)),
            waits=[("T", tm)], inc="V")
        pb_read[7] = ("V", pv)
    uT_ok = S.cnt["V"]

    hdt = DT_SSM / 2.0
    S.op("scalar", lambda sc: sc.activation(
        out=cn["denr"], in_=cn["ar0"], func=AF.Copy, scale=-hdt, bias=1.0),
        waits=[("DS", ail)])
    S.op("scalar", lambda sc: sc.mul(out=cn["deni"], in_=cn["ai0"],
                                     mul=-hdt))
    S.op("scalar", lambda sc: sc.activation(
        out=cn["numr"], in_=cn["ar0"], func=AF.Copy, scale=hdt, bias=1.0))
    S.op("scalar", lambda sc: sc.mul(out=cn["numi"], in_=cn["ai0"],
                                     mul=hdt))
    S.op("scalar", lambda sc: sc.activation(out=cn["m1"], in_=cn["denr"],
                                            func=AF.Square))
    aq = S.op("scalar", lambda sc: sc.activation(out=cn["t1"], in_=cn["deni"],
                                                 func=AF.Square), inc="A")
    vd2 = S.op("vector", lambda v: v.tensor_add(
        out=cn["d2"], in0=cn["m1"], in1=cn["t1"]), waits=[("A", aq)],
        inc="V")
    ad2 = S.op("scalar", lambda sc: sc.copy(out=cn["t2"], in_=cn["d2"]),
               waits=[("V", vd2)], inc="A")
    rv = S.op("vector", lambda v: v.reciprocal(out=cn["rinv"], in_=cn["t2"]),
              waits=[("A", ad2)], inc="V")
    S.op("scalar", lambda sc: sc.mul(out=cn["invr"], in_=cn["denr"],
                                     mul=cn["rinv"]), waits=[("V", rv)])
    S.op("scalar", lambda sc: sc.mul(out=cn["t1"], in_=cn["deni"],
                                     mul=cn["rinv"]))
    S.op("scalar", lambda sc: sc.mul(out=cn["invi"], in_=cn["t1"],
                                     mul=-1.0))
    S.op("scalar", lambda sc: sc.mul(out=cn["t1"], in_=cn["numr"],
                                     mul=cn["invr"]))
    aq = S.op("scalar", lambda sc: sc.mul(out=cn["t2"], in_=cn["numi"],
                                          mul=cn["invi"]), inc="A")
    va = S.op("vector", lambda v: v.tensor_sub(
        out=cn["adr"], in0=cn["t1"], in1=cn["t2"]), waits=[("A", aq)],
        inc="V")
    S.op("scalar", lambda sc: sc.mul(out=cn["t1"], in_=cn["numr"],
                                     mul=cn["invi"]), waits=[("V", va)])
    aq = S.op("scalar", lambda sc: sc.mul(out=cn["t2"], in_=cn["numi"],
                                          mul=cn["invr"]), inc="A")
    vb = S.op("vector", lambda v: v.tensor_add(
        out=cn["adi"], in0=cn["t1"], in1=cn["t2"]), waits=[("A", aq)],
        inc="V")
    S.op("scalar", lambda sc: sc.activation(out=cn["t1"], in_=cn["adr"],
                                            func=AF.Square),
         waits=[("V", vb)])
    aq = S.op("scalar", lambda sc: sc.activation(out=cn["t2"], in_=cn["adi"],
                                                 func=AF.Square), inc="A")
    vr2 = S.op("vector", lambda v: v.tensor_add(
        out=cn["r2"], in0=cn["t1"], in1=cn["t2"]), waits=[("A", aq)],
        inc="V")
    S.op("scalar", lambda sc: sc.activation(out=cn["lnr2"], in_=cn["r2"],
                                            func=AF.Ln), waits=[("V", vr2)])
    S.op("scalar", lambda sc: sc.mul(out=cn["lr"], in_=cn["lnr2"], mul=0.5))
    S.op("scalar", lambda sc: sc.mul(out=cn["nlr"], in_=cn["lr"], mul=-1.0))
    aadr = S.op("scalar", lambda sc: sc.copy(out=cn["t1"], in_=cn["adr"]),
                inc="A")
    rv2 = S.op("vector", lambda v: v.reciprocal(out=cn["t2"], in_=cn["t1"]),
               waits=[("A", aadr)], inc="V")
    S.op("scalar", lambda sc: sc.mul(out=cn["ratio"], in_=cn["adi"],
                                     mul=cn["t2"]), waits=[("V", rv2)])
    S.op("scalar", lambda sc: sc.activation(out=cn["th"], in_=cn["ratio"],
                                            func=AF.Arctan))
    S.op("scalar", lambda sc: sc.activation(out=cn["ath"], in_=cn["th"],
                                            func=AF.Abs))
    a3 = S.op("scalar", lambda sc: sc.activation(out=cn["sgn"], in_=cn["th"],
                                                 func=AF.Sign), inc="A")
    S.op("scalar", lambda sc: sc.mul(out=cn["sivr"], in_=cn["invr"],
                                     mul=SQRT_DT))
    sv2 = S.op("scalar", lambda sc: sc.mul(out=cn["sivi"], in_=cn["invi"],
                                           mul=SQRT_DT), inc="A")

    S.op("vector", lambda v: v.memset(onesb, 1.0))
    a1 = S.op("scalar", lambda sc: sc.mul(
        out=c1, in_=trow, mul=cn["ath"]),
        waits=[("DS", trl), ("A", sv2)])
    a2 = S.op("scalar", lambda sc: sc.activation(
        out=c2, in_=trow, func=AF.Identity, scale=cn["ath"],
        bias=float(0.5 * np.pi)), inc="A")
    S.op("vector", lambda v: v.tensor_scalar_mul(
        out=hrb, in0=c1, scalar1=float(1.0 / TWO_PI)), waits=[("A", a2)])
    S.op("vector", lambda v: v.tensor_scalar_mul(
        out=hib, in0=c2, scalar1=float(1.0 / TWO_PI)))
    S.op("vector", lambda v: v.tensor_copy(out=ibuf, in_=hrb))
    S.op("vector", lambda v: v.tensor_copy(out=ibuf2, in_=hib))
    S.op("vector", lambda v: v.tensor_copy(out=hrb, in_=ibuf))
    S.op("vector", lambda v: v.tensor_copy(out=hib, in_=ibuf2))
    S.op("vector", lambda v: v.scalar_tensor_tensor(
        out=hrb, in0=hrb, scalar=-TWO_PI, in1=c1, op0=ALU.mult,
        op1=ALU.add))
    vm = S.op("vector", lambda v: v.scalar_tensor_tensor(
        out=hib, in0=hib, scalar=-TWO_PI, in1=c2, op0=ALU.mult,
        op1=ALU.add), inc="V")
    art = S.op("scalar", lambda sc: sc.activation(
        out=rtb, in_=trow, func=AF.Exp, scale=cn["lr"]),
        waits=[("V", vm)], inc="A")
    armt = S.op("scalar", lambda sc: sc.activation(
        out=rmtb, in_=trow, func=AF.Exp, scale=cn["nlr"]), inc="A")
    S.op("scalar", lambda sc: sc.activation(
        out=sinb, in_=hrb, func=AF.Sin))
    am2 = S.op("scalar", lambda sc: sc.activation(
        out=cosb, in_=hib, func=AF.Sin), inc="A")
    sgn_done = S.op("scalar", lambda sc: sc.mul(
        out=sinb, in_=sinb, mul=cn["sgn"]), inc="A")

    bu_done = None
    for tg in range(4):
        bk = tg % 2
        tm = mm_group(S, PB[bk][:, :],
                      [(BTb[:, ic, :], uT[:, ic, tg * 512:(tg + 1) * 512])
                       for ic in range(8)],
                      waits=[("V", uT_ok), ("DG", btl), pb_read[bk]])
        S.op("scalar", lambda sc, bk=bk, tg=tg: sc.mul(
            out=bu_r[:, tg * 512:(tg + 1) * 512], in_=PB[bk][:, :],
            mul=cn["sivr"]), waits=[("T", tm), ("A", sv2)])
        bu_done = S.op("scalar", lambda sc, bk=bk, tg=tg: sc.mul(
            out=bu_i[:, tg * 512:(tg + 1) * 512], in_=PB[bk][:, :],
            mul=cn["sivi"]), inc="A")
        pb_read[bk] = ("A", bu_done)

    wv = [("A", bu_done), ("A", sgn_done), ("A", am2), ("A", armt)]
    S.op("vector", lambda v: v.tensor_mul(out=c1, in0=cosb, in1=rmtb),
         waits=wv)
    S.op("vector", lambda v: v.tensor_mul(out=c2, in0=sinb, in1=rmtb))
    S.op("vector", lambda v: v.tensor_mul(out=hrb, in0=bu_r, in1=c1))
    S.op("vector", lambda v: v.tensor_mul(out=ibuf.bitcast(F32), in0=bu_i,
                                          in1=c2))
    S.op("vector", lambda v: v.tensor_add(out=hrb, in0=hrb,
                                          in1=ibuf.bitcast(F32)))
    S.op("vector", lambda v: v.tensor_mul(out=hib, in0=bu_i, in1=c1))
    S.op("vector", lambda v: v.tensor_mul(out=ibuf.bitcast(F32), in0=bu_r,
                                          in1=c2))
    S.op("vector", lambda v: v.tensor_sub(out=hib, in0=hib,
                                          in1=ibuf.bitcast(F32)))
    S.op("vector", lambda v: v.tensor_tensor_scan(
        out=bu_r, data0=onesb, data1=hrb, initial=0.0, op0=ALU.mult,
        op1=ALU.add))
    S.op("vector", lambda v: v.tensor_tensor_scan(
        out=bu_i, data0=onesb, data1=hib, initial=0.0, op0=ALU.mult,
        op1=ALU.add))
    S.op("vector", lambda v: v.tensor_mul(out=c1, in0=cosb, in1=rtb))
    S.op("vector", lambda v: v.tensor_mul(out=c2, in0=sinb, in1=rtb))
    S.op("vector", lambda v: v.tensor_mul(out=hrb, in0=bu_r, in1=c1))
    S.op("vector", lambda v: v.tensor_mul(out=hib, in0=bu_i, in1=c2))
    S.op("vector", lambda v: v.tensor_sub(out=hrb, in0=hrb, in1=hib))
    hv = S.op("vector", lambda v: v.tensor_copy(out=h_bf, in_=hrb), inc="V")
    if dbg:
        for nm, ap in (("g_cols", cols), ("g_rmt", rmtb), ("g_rt", rtb),
                       ("g_cos", cosb), ("g_sin", sinb), ("g_bur", bu_r),
                       ("g_hr", hrb)):
            if nm in dbg:
                S.dma("sync", dbg[nm], ap, waits=[("V", hv)])

    ev = None
    for i in range(NTILE):
        bks = (2 + (i % 2) * 2, 3 + (i % 2) * 2)
        tm = {}
        for dg in range(2):
            tm[dg] = mm_group(S, PB[bks[dg]][:, :],
                              [(h_bf[:, i * 128:(i + 1) * 128],
                                CTb[:, dg * 512:(dg + 1) * 512])],
                              waits=[("V", hv), ("DG", ctl),
                                     pb_read[bks[dg]]])
        ld = S.dma("sync", xtile[i % 2], in_ap[i * 128:(i + 1) * 128, :],
                   waits=[("DS", last_store), aouts[NTILE - 1]])
        for dg in range(2):
            ev = S.op("vector", lambda v, i=i, dg=dg, bks=bks: v.tensor_add(
                out=xout_sb[i % 2][:, dg * 512:(dg + 1) * 512],
                in0=PB[bks[dg]][:, :],
                in1=xtile[i % 2][:, dg * 512:(dg + 1) * 512]),
                waits=[("T", tm[dg]), ("DS", ld)], inc="V")
            pb_read[bks[dg]] = ("V", ev)
        last_store = S.dma("sync", out_ap[i * 128:(i + 1) * 128, :],
                           xout_sb[i % 2], waits=[("V", ev)])
    ar.pop()
    return last_store


def l1attn_host_arrays(inputs):
    return {
        "l1a_BT": np.ascontiguousarray(inputs["L1_attn_B"].T),
        "l1a_CT": np.ascontiguousarray(inputs["L1_attn_C"].T),
        "l1a_Ar": np.ascontiguousarray(
            inputs["L1_attn_A_real"].reshape(128, 1)),
        "l1a_Ai": np.ascontiguousarray(
            inputs["L1_attn_A_imag"].reshape(128, 1)),
    }


def declare_l1attn(nc):
    return {
        "l1a_BT": nc.dram_tensor("l1a_BT", [D, 128], F32,
                                 kind="ExternalInput").ap(),
        "l1a_CT": nc.dram_tensor("l1a_CT", [128, D], F32,
                                 kind="ExternalInput").ap(),
        "l1a_Ar": nc.dram_tensor("l1a_Ar", [128, 1], F32,
                                 kind="ExternalInput").ap(),
        "l1a_Ai": nc.dram_tensor("l1a_Ai", [128, 1], F32,
                                 kind="ExternalInput").ap(),
        "c_trow": nc.dram_tensor("c_trow", [128, T], F32,
                                 kind="ExternalInput").ap(),
    }


def build_l1attn_program(dbg_names=()):
    nc = bass.Bass("TRN2", target_bir_lowering=False, debug=False)
    xin = nc.dram_tensor("xin", [T, D], F32, kind="ExternalInput").ap()
    xout = nc.dram_tensor("xout", [T, D], F32, kind="ExternalOutput").ap()
    dbg = {}
    for nm in dbg_names:
        shp = [128, 64] if nm == "g_cols" else [128, T]
        dbg[nm] = nc.dram_tensor(nm, shp, F32, kind="ExternalOutput").ap()
    S = Sched(nc)
    with contextlib.ExitStack() as stk:
        ar, PB, PBT = _alloc_common(stk, nc, S)
        cst = declare_consts(S, nc, ar)
        W = declare_l1attn(nc)
        emit_l1attn(S, nc, ar, cst, W, xin, xout, PB, PBT, dbg=dbg or None)
        S.op("sync", lambda s: s.wait_ge(S.sems["DS"], S.cnt["DS"]))
        block = stk.enter_context(nc.Block())
        S.emit(block)
    return nc



INV_SQRT32 = float(1.0 / np.sqrt(32.0))


def emit_l1ffn(S, nc, ar, cst, W, in_ap, out_ap, xb2, PB, PBT, upto=None,
               dbg=None):
    def bail(extra=()):
        vz = S.op("vector", lambda v: v.memset(xtile[0], 0.0),
                  waits=S.guard(), inc="V")
        st = S.dma("sync", out_ap[0:128, :], xtile[0], waits=[("V", vz)])
        if dbg:
            for nm, ap in extra:
                if nm in dbg:
                    st = S.dma("sync", dbg[nm], ap, waits=[("V", vz)])
        return st
    S.tensor_fence()
    ar.push()
    tTreg = ar.alloc((128, 8, T), BF16)
    hT = tTreg
    tT = tTreg
    xtile = [ar.alloc((128, D), F32) for _ in range(2)]
    gate_all = ar.alloc((128, 16, 8), F32)
    ar.push()
    xlnb = ar.alloc((128, D), F32)
    t0f = ar.alloc((128, D), F32)
    h_bf = ar.alloc((128, D), BF16)
    scrLN = ar.alloc((128, D), F32)
    g2rep = ar.alloc((128, D), F32)
    qss = ar.alloc((128, 16, 16), F32)
    kss = ar.alloc((128, 16, 16), F32)
    biascols = ar.alloc((128, 16), F32)
    mrow = ar.alloc((128, 64), F32)
    rowp = ar.alloc((128, 1056), F32)
    rowbf = rowp.bitcast(BF16)
    g2row = rowp[0:1, 0:1024]
    nbrow_bf = rowbf[0:1, 2048:2080]
    scrsm = ar.alloc((128, 32), F32)
    st2 = [make_stats(scrsm, j) for j in range(2)]
    gsc = ar.alloc((128, 64), F32)
    e_sb = gsc[:, 0:8]
    ism = gsc[:, 8:16]
    em_s = gsc[:, 16:24]
    ind_s = gsc[:, 24:32]
    ge_s = gsc[:, 32:40]
    nmx8 = gsc[:, 40:41]
    m1c = gsc[:, 41:42]
    m2c = gsc[:, 42:43]
    s12c = gsc[:, 43:44]
    rsumc = gsc[:, 44:45]
    m1g = gsc[:, 45:46]
    m2g = gsc[:, 46:47]
    s12g = gsc[:, 47:48]
    rsg = gsc[:, 48:49]
    m1v8 = gsc[:, 56:64]
    m2v8 = mrow[:, 32:40]
    rsv8 = mrow[:, 40:48]

    wdsb = ar.alloc((128, 8, 512), BF16)
    wqsb = ar.alloc((128, 4, 512), BF16)
    wksb = ar.alloc((128, 4, 512), BF16)
    wvsb = ar.alloc((128, 4, 512), BF16)
    wupsb = ar.alloc((128, 4, D), BF16)
    rsb = ar.alloc((128, 8, 8), BF16)
    g1m = ar.alloc((128, 8), F32)
    wdl = S.dma("gpsimd", wdsb,
                W["l1f_wdown"].rearrange("(ic p) l -> p ic l", p=128),
                waits=S.guard())
    S.dma("gpsimd", wqsb, W["l1f_wq"].rearrange("(lc p) o -> p lc o", p=128))
    S.dma("gpsimd", wksb, W["l1f_wk"].rearrange("(lc p) o -> p lc o", p=128))
    wvl = S.dma("gpsimd", wvsb,
                W["l1f_wv"].rearrange("(lc p) o -> p lc o", p=128))
    wul = S.dma("gpsimd", wupsb,
                W["l1f_wup"].rearrange("(lc p) d -> p lc d", p=128))
    rl = S.dma("gpsimd", rsb, W["l1f_router"].rearrange(
        "(ic p) e -> p ic e", p=128))
    g1l = S.dma("sync", g1m, W["l1f_g1m"], waits=S.guard())
    g2l = S.dma("sync", g2row, W["l1f_g2row"])

    wsc = None
    for ic in range(8):
        wsc = S.op("vector", lambda v, ic=ic: v.tensor_scalar(
            out=wdsb[:, ic, :], in0=wdsb[:, ic, :],
            scalar1=g1m[:, ic:ic + 1], scalar2=C0, op0=ALU.mult,
            op1=ALU.mult),
            waits=[("DG", wdl)] + ([("DS", g1l)] if ic == 0 else []),
            inc="V")

    pb_read = {i: ("V", S.cnt["V"]) for i in range(8)}
    for dg in range(2):
        tm = mm_group(S, PB[6][:, 0:512],
                      [(cst["ones_f"][0:1, :],
                        g2row[0:1, dg * 512:(dg + 1) * 512])],
                      waits=[("DS", g2l), ("DG", cst["loaded_g"]),
                             pb_read[6]])
        g2d = S.op("vector", lambda v, dg=dg: v.tensor_copy(
            out=g2rep[:, dg * 512:(dg + 1) * 512], in0=PB[6][:, 0:512])
            if False else v.tensor_scalar_mul(
            out=g2rep[:, dg * 512:(dg + 1) * 512], in0=PB[6][:, 0:512],
            scalar1=1.0), waits=[("T", tm)], inc="V")
        pb_read[6] = ("V", g2d)

    aouts = {}
    last_st2 = None
    for i in range(NTILE):
        ld = S.dma("sync", xtile[i % 2], in_ap[i * 128:(i + 1) * 128, :],
                   waits=([aouts[i - 2]] if i >= 2 else S.guard()))
        pw = [("DS", ld)]
        if i >= 1:
            pw.append(aouts[i - 1])
        a = ln_tile(S, xtile[i % 2], xlnb, scrLN, st2[i % 2], pre_waits=pw,
                    out_waits=[("DS", last_st2)])
        aouts[i] = a
        last_st2 = S.dma("sync", xb2[i * 128:(i + 1) * 128, :], xlnb,
                         waits=[a])
        hv = S.op("vector", lambda v: v.tensor_copy(out=h_bf, in_=xlnb),
                  waits=[a], inc="V")

        def trf(t, i=i):
            last = None
            for ic in range(8):
                last = t.transpose(
                    out=PBT[:, ic * 128:(ic + 1) * 128],
                    in_=h_bf[:, ic * 128:(ic + 1) * 128],
                    identity=cst["ident_bf"])
            return last
        tm = S.op("tensor", trf, waits=[("V", hv), pb_read[7]], inc="T")
        pv = S.op("vector", lambda v, i=i: v.tensor_copy(
            out=hT[:, :, i * 128:(i + 1) * 128],
            in_=PBT[:, :].rearrange("p (ic q) -> p ic q", ic=8)),
            waits=[("T", tm)], inc="V")
        pb_read[7] = ("V", pv)
        aouts[i] = ("V", pv)
    hT_ok = S.cnt["V"]
    if upto == "p1":
        return bail()

    cTreg = ar.alloc((128, 4, T), BF16)
    cT = cTreg
    oT = cTreg
    qT = ar.alloc((128, 4, T), BF16)
    kT = ar.alloc((128, 4, T), BF16)
    v_tok = ar.alloc((128, 16, 512), BF16)
    qk_bf = ar.alloc((128, 512), BF16)
    rsc = [ar.alloc((128, 256), F32) for _ in range(2)]
    costb = [ar.alloc((128, 256), F32) for _ in range(2)]
    sintb = [ar.alloc((128, 256), F32) for _ in range(2)]
    Abuf = ar.alloc((128, 4, 512), BF16)
    rl_sb = ar.alloc((128, 512), F32)
    rl2_sb = ar.alloc((128, 512), F32)

    last_ev = None
    for lc in range(4):
        for tg in range(4):
            bk = (lc * 4 + tg) % 3
            mw = [pb_read[bk], ("V", wsc)]
            if lc == 0 and tg == 0:
                mw.append(("V", hT_ok))
            tm = mm_group(S, PB[bk][:, :],
                          [(wdsb[:, ic, lc * 128:(lc + 1) * 128],
                            hT[:, ic, tg * 512:(tg + 1) * 512])
                           for ic in range(8)], waits=mw)
            last_ev = S.op("vector", lambda v, lc=lc, tg=tg, bk=bk: v.tensor_copy(
                out=cT[:, lc, tg * 512:(tg + 1) * 512], in_=PB[bk][:, :]),
                waits=[("T", tm)], inc="V")
            pb_read[bk] = ("V", last_ev)
    cT_ok = S.cnt["V"]
    if upto == "p2":
        return bail([("d_cT", cT[:, :, :].rearrange("p a b -> p (a b)")), ("d_hT", hT[:, :, :].rearrange("p a b -> p (a b)")), ("d_st", scrsm)])

    prev_tr = {}
    vvv = None
    for i in range(NTILE):
        cl = S.dma("sync", costb[i % 2], W["c_cos"][:, i * 256:(i + 1) * 256],
                   waits=[("V", prev_tr.get(("k", i - 2)))])
        sl = S.dma("sync", sintb[i % 2], W["c_sin"][:, i * 256:(i + 1) * 256])
        for which, wsb, dstT in (("q", wqsb, qT), ("k", wksb, kT)):
            bk = (3 if which == "q" else 4) if i % 2 == 0 else                 (0 if which == "q" else 1)
            mw = [pb_read[bk], ("DG", wvl)]
            if i == 0 and which == "q":
                mw.append(("V", cT_ok))
            tm = mm_group(S, PB[bk][:, :],
                          [(cT[:, lc, i * 128:(i + 1) * 128],
                            wsb[:, lc, :]) for lc in range(4)], waits=mw)
            ps = PB[bk][:, :]
            S.op("vector", lambda v, ps=ps, i=i: v.tensor_tensor(
                out=rsc[0], in0=ps.rearrange("p (c two) -> p c two", two=2
                                             )[:, :, 0],
                in1=costb[i % 2], op=ALU.mult),
                waits=[("T", tm), ("DS", sl)])
            S.op("vector", lambda v, ps=ps, i=i: v.tensor_tensor(
                out=rsc[1], in0=ps.rearrange("p (c two) -> p c two", two=2
                                             )[:, :, 1],
                in1=sintb[i % 2], op=ALU.mult))
            S.op("vector", lambda v: v.tensor_sub(
                out=qk_bf.rearrange("p (c two) -> p c two", two=2)[:, :, 0],
                in0=rsc[0], in1=rsc[1]))
            S.op("vector", lambda v, ps=ps, i=i: v.tensor_tensor(
                out=rsc[0], in0=ps.rearrange("p (c two) -> p c two", two=2
                                             )[:, :, 0],
                in1=sintb[i % 2], op=ALU.mult))
            S.op("vector", lambda v, ps=ps, i=i: v.tensor_tensor(
                out=rsc[1], in0=ps.rearrange("p (c two) -> p c two", two=2
                                             )[:, :, 1],
                in1=costb[i % 2], op=ALU.mult))
            rdone = S.op("vector", lambda v: v.tensor_add(
                out=qk_bf.rearrange("p (c two) -> p c two", two=2)[:, :, 1],
                in0=rsc[0], in1=rsc[1]), inc="V")
            pb_read[bk] = ("V", rdone)
            dst_ss = qss if which == "q" else kss
            S.op("vector", lambda v, dst_ss=dst_ss, i=i: v.reduce_sum(
                out=dst_ss[:, :, i:i + 1],
                in_=qk_bf.rearrange("p (h hd) -> p h hd", h=16), axis=AX.X,
                apply_absolute_value=False, negate=False))

            def trf2(t, i=i, dstT=dstT):
                last = None
                for lc in range(4):
                    last = t.transpose(
                        out=PBT[:, lc * 128:(lc + 1) * 128],
                        in_=qk_bf[:, lc * 128:(lc + 1) * 128],
                        identity=cst["ident_bf"])
                return last
            tr = S.op("tensor", trf2, waits=[("V", rdone), pb_read[7]],
                      inc="T")
            pv = S.op("vector", lambda v, i=i, dstT=dstT: v.tensor_copy(
                out=dstT[:, :, i * 128:(i + 1) * 128],
                in_=PBT[:, 0:512].rearrange("p (lc q) -> p lc q", lc=4)),
                waits=[("T", tr)], inc="V")
            pb_read[7] = ("V", pv)
            prev_tr[(which, i)] = pv
        bkv = 5 if i % 2 == 0 else 2
        tm = mm_group(S, PB[bkv][:, :],
                      [(cT[:, lc, i * 128:(i + 1) * 128],
                        wvsb[:, lc, :]) for lc in range(4)],
                      waits=[pb_read[bkv]])
        vvv = S.op("vector", lambda v, i=i, bkv=bkv: v.tensor_copy(
            out=v_tok[:, i, :], in_=PB[bkv][:, :]), waits=[("T", tm)],
            inc="V")
        pb_read[bkv] = ("V", vvv)
    qkv_ok = S.cnt["V"]
    if upto == "p3":
        return bail([("d_qT", qT[:, :, :].rearrange("p a b -> p (a b)")), ("d_kT", kT[:, :, :].rearrange("p a b -> p (a b)")), ("d_v", v_tok[:, :, :].rearrange("p a b -> p (a b)"))])

    vq = S.op("vector", lambda v: v.reduce_max(
        out=mrow[:, 0:16].rearrange("p (h one) -> p h one", one=1),
        in_=qss[:, :, :], axis=AX.X), waits=[("V", qkv_ok)])
    vk = S.op("vector", lambda v: v.reduce_max(
        out=mrow[:, 16:32].rearrange("p (h one) -> p h one", one=1),
        in_=kss[:, :, :], axis=AX.X), inc="V")
    mbf = mrow.bitcast(BF16)
    ab1 = S.op("scalar", lambda sc: sc.copy(out=mbf[:, 64:96],
                                            in_=mrow[:, 0:32]),
               waits=[("V", vk)], inc="A")
    trm = S.op("tensor", lambda t: t.transpose(
        out=PBT[0:16, 0:128], in_=mbf[:, 64:80],
        identity=cst["ident_bf"]), waits=[("A", ab1), pb_read[7]])
    trm = S.op("tensor", lambda t: t.transpose(
        out=PBT[0:16, 128:256], in_=mbf[:, 80:96],
        identity=cst["ident_bf"]), inc="T")
    vgq = S.op("vector", lambda v: v.reduce_max(
        out=gsc[0:16, 49:50], in_=PBT[0:16, 0:128], axis=AX.X),
        waits=[("T", trm)])
    vgk = S.op("vector", lambda v: v.reduce_max(
        out=gsc[0:16, 50:51], in_=PBT[0:16, 128:256], axis=AX.X), inc="V")
    pb_read[7] = ("V", vgk)
    aq = S.op("scalar", lambda sc: sc.copy(out=gsc[0:16, 51:52],
                                           in_=gsc[0:16, 49:50]),
              waits=[("V", vgk)])
    aq = S.op("scalar", lambda sc: sc.copy(out=gsc[0:16, 52:53],
                                           in_=gsc[0:16, 50:51]), inc="A")
    vm2 = S.op("vector", lambda v: v.tensor_mul(
        out=gsc[0:16, 53:54], in0=gsc[0:16, 51:52], in1=gsc[0:16, 52:53]),
        waits=[("A", aq)], inc="V")
    S.op("scalar", lambda sc: sc.activation(
        out=gsc[0:16, 54:55], in_=gsc[0:16, 53:54], func=AF.Sqrt),
        waits=[("V", vm2)])
    S.op("scalar", lambda sc: sc.mul(out=gsc[0:16, 55:56],
                                     in_=gsc[0:16, 54:55], mul=INV_SQRT32))
    S.op("scalar", lambda sc: sc.mul(out=gsc[0:16, 56:57],
                                     in_=gsc[0:16, 55:56], mul=-1.0))
    anb = S.op("scalar", lambda sc: sc.copy(
        out=gsc.bitcast(BF16)[0:16, 120:121], in_=gsc[0:16, 56:57]),
        inc="A")
    trn = S.op("tensor", lambda t: t.transpose(
        out=PBT[0:1, 0:16], in_=gsc.bitcast(BF16)[0:16, 120:121],
        identity=cst["ident_bf"][0:16, 0:16]), waits=[("A", anb), pb_read[7]], inc="T")
    vnb = S.op("vector", lambda v: v.tensor_copy(
        out=nbrow_bf[0:1, 0:16], in_=PBT[0:1, 0:16]), waits=[("T", trn)],
        inc="V")
    pb_read[7] = ("V", vnb)
    def bc_fn(t):
        last = None
        for h in range(16):
            last = t.matmul(out=PB[6][:, h:h + 1],
                            lhsT=cst["ones_bf"][0:1, :],
                            rhs=nbrow_bf[0:1, h:h + 1],
                            start=True, stop=True)
        return last
    tmb = S.op("tensor", bc_fn, waits=[("V", vnb), pb_read[6]], inc="T")
    vbc = S.op("vector", lambda v: v.tensor_copy(
        out=biascols, in_=PB[6][:, 0:16]), waits=[("T", tmb)], inc="V")
    pb_read[6] = ("V", vbc)
    if upto == "p4":
        return bail([("d_bias", biascols)])

    for hg in range(4):
        for tg in range(4):
            nchunks = 4 * tg + 4
            av_done = None
            for j in range(nchunks):
                r = j - 4 * tg
                t_lo = max(r, 0) * 128
                n_t = 512 - t_lo
                def smm(t, hg=hg, tg=tg, j=j, t_lo=t_lo, n_t=n_t):
                    last = None
                    for a in range(4):
                        last = t.matmul(
                            out=PB[a][:, t_lo:512],
                            lhsT=kT[32 * a:32 * a + 32, hg,
                                    j * 128:(j + 1) * 128],
                            rhs=qT[32 * a:32 * a + 32, hg,
                                   tg * 512 + t_lo:(tg + 1) * 512],
                            start=True, stop=True,
                            tile_position=(32 * a, 0))
                    return last
                mw = [pb_read[a] for a in range(4)]
                if hg == 0 and tg == 0 and j == 0:
                    mw.append(("V", qkv_ok))
                tms = S.op("tensor", smm, waits=mw, inc="T")
                aev = {}
                for a in range(4):
                    aev[a] = S.op("scalar", lambda sc, a=a, hg=hg, t_lo=t_lo: sc.activation(
                        out=Abuf[:, a, t_lo:512], in_=PB[a][:, t_lo:512],
                        func=AF.Exp, scale=INV_SQRT32,
                        bias=biascols[:, 4 * hg + a:4 * hg + a + 1]),
                        waits=[("T", tms), ("V", vbc)] if a == 0
                        else [], inc="A")
                    pb_read[a] = ("A", aev[a])
                msk = None
                if r >= 0:
                    msk = None
                    for a in range(4):
                        msk = S.op("vector", lambda v, a=a, t_lo=t_lo: v.tensor_tensor(
                            out=Abuf[:, a, t_lo:t_lo + 128],
                            in0=Abuf[:, a, t_lo:t_lo + 128],
                            in1=cst["tri_f"], op=ALU.mult),
                            waits=[("A", aev[3])], inc="V")
                def avmm(t, j=j, hg=hg, t_lo=t_lo):
                    last = None
                    for a in range(4):
                        last = t.matmul(
                            out=PB[4][32 * a:32 * a + 32, t_lo:512],
                            lhsT=v_tok[:, j, (4 * hg + a) * 32:
                                       (4 * hg + a) * 32 + 32],
                            rhs=Abuf[:, a, t_lo:512],
                            start=(j == 0), stop=False,
                            tile_position=(0, 32 * a))
                    for a in range(4):
                        last = t.matmul(
                            out=PB[5][32 * a:32 * a + 32, t_lo:512],
                            lhsT=cst["ones_bf"][:, 0:32],
                            rhs=Abuf[:, a, t_lo:512],
                            start=(j == 0), stop=False,
                            tile_position=(0, 32 * a))
                    return last
                avw = [("A", aev[3])]
                if msk is not None:
                    avw.append(("V", msk))
                if j == 0:
                    avw += [pb_read[4], pb_read[5]]
                av_done = S.op("tensor", avmm, waits=avw, inc="T")
            vrl = S.op("vector", lambda v: v.reciprocal(
                out=rl_sb, in_=PB[5][:, :]), waits=[("T", av_done)],
                inc="V")
            pb_read[5] = ("V", vrl)
            arl2 = S.op("scalar", lambda sc: sc.copy(out=rl2_sb, in_=rl_sb),
                        waits=[("V", vrl)], inc="A")
            vo = S.op("vector", lambda v, hg=hg, tg=tg: v.tensor_tensor(
                out=oT[:, hg, tg * 512:(tg + 1) * 512], in0=PB[4][:, :],
                in1=rl2_sb, op=ALU.mult), waits=[("A", arl2)], inc="V")
            pb_read[4] = ("V", vo)
    oT_ok = S.cnt["V"]
    if upto == "p5":
        return bail([("d_oT", oT[:, :, :].rearrange("p a b -> p (a b)"))])

    xmid_sb = ar.alloc((128, D), F32)
    tbf2 = ar.alloc((128, D), BF16)
    xt6 = [ar.alloc((128, D), F32) for _ in range(2)]
    last_xm = None
    rt_done = {}
    for i in range(NTILE):
        mw = [pb_read[6], ("DG", wul)]
        if i == 0:
            mw.append(("V", oT_ok))
        tm6 = mm_group(S, PB[6][:, :],
                       [(oT[:, hg, i * 128:(i + 1) * 128],
                         wupsb[:, hg, 0:512]) for hg in range(4)], waits=mw)
        tm0 = mm_group(S, PB[0][:, :],
                       [(oT[:, hg, i * 128:(i + 1) * 128],
                         wupsb[:, hg, 512:1024]) for hg in range(4)],
                       waits=[pb_read[0]])
        xl = S.dma("sync", xt6[i % 2], xb2[i * 128:(i + 1) * 128, :],
                   waits=[("DS", last_xm), rt_done.get(i - 2)])
        S.op("vector", lambda v, i=i: v.tensor_add(
            out=xmid_sb[:, 0:512], in0=PB[6][:, :],
            in1=xt6[i % 2][:, 0:512]),
            waits=[("T", tm0), ("DS", xl)])
        xme = S.op("vector", lambda v, i=i: v.tensor_add(
            out=xmid_sb[:, 512:1024], in0=PB[0][:, :],
            in1=xt6[i % 2][:, 512:1024]), inc="V")
        pb_read[6] = ("V", xme)
        pb_read[0] = ("V", xme)
        last_xm = S.dma("sync", xb2[i * 128:(i + 1) * 128, :], xmid_sb,
                        waits=[("V", xme)])
        a2 = ln_tile(S, xmid_sb, t0f, scrLN, st2[i % 2], rms_only=True,
                     pre_waits=[("V", xme)], out_waits=[("DS", last_xm)])
        vt = S.op("vector", lambda v: v.tensor_mul(
            out=tbf2, in0=t0f, in1=g2rep), waits=[a2], inc="V")

        def trf3(t, i=i):
            last = None
            for ic in range(8):
                last = t.transpose(
                    out=PBT[:, ic * 128:(ic + 1) * 128],
                    in_=tbf2[:, ic * 128:(ic + 1) * 128],
                    identity=cst["ident_bf"])
            return last
        tr = S.op("tensor", trf3, waits=[("V", vt), pb_read[7]], inc="T")
        pv = S.op("vector", lambda v, i=i: v.tensor_copy(
            out=tT[:, :, i * 128:(i + 1) * 128],
            in_=PBT[:, :].rearrange("p (ic q) -> p ic q", ic=8)),
            waits=[("T", tr)], inc="V")
        pb_read[7] = ("V", pv)
        rt_done[i] = ("V", pv)
        tmr = mm_group(S, PB[1][:, 0:8],
                       [(tT[:, ic, i * 128:(i + 1) * 128], rsb[:, ic, :])
                        for ic in range(8)],
                       waits=[pb_read[1], ("V", pv), ("DG", rl)])
        v1 = S.op("vector", lambda v: v.reduce_max(
            out=nmx8, in_=PB[1][:, 0:8], axis=AX.X, negate=True),
            waits=[("T", tmr)], inc="V")
        ae = S.op("scalar", lambda sc: sc.activation(
            out=e_sb, in_=PB[1][:, 0:8], func=AF.Exp, bias=nmx8),
            waits=[("V", v1)], inc="A")
        pb_read[1] = ("A", ae)
        vm1 = S.op("vector", lambda v: v.reduce_max(
            out=m1c, in_=e_sb, axis=AX.X), waits=[("A", ae)], inc="V")
        am1b = S.op("scalar", lambda sc: sc.mul(
            out=m1v8, in_=cst["ones_f"][:, 0:8], mul=m1c),
            waits=[("V", vm1)], inc="A")
        vi = S.op("vector", lambda v: v.tensor_tensor(
            out=ism, in0=e_sb, in1=m1v8, op=ALU.is_ge),
            waits=[("A", am1b)], inc="V")
        hb = S.op("scalar", lambda sc: sc.nop(), waits=[("V", vi)], inc="A")
        vi = S.op("vector", lambda v: v.tensor_scalar(
            out=ism, in0=ism, scalar1=-1.0, scalar2=1.0, op0=ALU.mult,
            op1=ALU.add), waits=[("A", hb)], inc="V")
        hb = S.op("scalar", lambda sc: sc.nop(), waits=[("V", vi)], inc="A")
        vem = S.op("vector", lambda v: v.tensor_mul(
            out=em_s, in0=e_sb, in1=ism), waits=[("A", hb)], inc="V")
        hb = S.op("scalar", lambda sc: sc.nop(), waits=[("V", vem)],
                  inc="A")
        vm2b = S.op("vector", lambda v: v.reduce_max(
            out=m2c, in_=em_s, axis=AX.X), waits=[("A", hb)], inc="V")
        am2b = S.op("scalar", lambda sc: sc.mul(
            out=m2v8, in_=cst["ones_f"][:, 0:8], mul=m2c),
            waits=[("V", vm2b)], inc="A")
        vi = S.op("vector", lambda v: v.tensor_tensor(
            out=ind_s, in0=e_sb, in1=m2v8, op=ALU.is_ge),
            waits=[("A", am2b)], inc="V")
        hb = S.op("scalar", lambda sc: sc.nop(), waits=[("V", vi)], inc="A")
        vge = S.op("vector", lambda v: v.tensor_mul(
            out=ge_s, in0=e_sb, in1=ind_s), waits=[("A", hb)], inc="V")
        hb = S.op("scalar", lambda sc: sc.nop(), waits=[("V", vge)],
                  inc="A")
        vs12 = S.op("vector", lambda v: v.reduce_sum(
            out=s12c, in_=ge_s, axis=AX.X), waits=[("A", hb)], inc="V")
        as12 = S.op("scalar", lambda sc: sc.copy(out=s12g, in_=s12c),
                    waits=[("V", vs12)], inc="A")
        vrs = S.op("vector", lambda v: v.reciprocal(out=rsg, in_=s12g),
                   waits=[("A", as12)], inc="V")
        arsb = S.op("scalar", lambda sc: sc.mul(
            out=rsv8, in_=cst["ones_f"][:, 0:8], mul=rsg),
            waits=[("V", vrs)], inc="A")
        vg = S.op("vector", lambda v, i=i: v.tensor_tensor(
            out=gate_all[:, i, :], in0=ge_s, in1=rsv8, op=ALU.mult),
            waits=[("A", arsb)], inc="V")
    gate_ok = S.cnt["V"]
    tT_ok2 = S.cnt["V"]
    if upto == "p6":
        return bail([("d_gate", gate_all[:, :, :].rearrange("p a b -> p (a b)")), ("d_tT", tT[:, :, :].rearrange("p a b -> p (a b)")), ("d_oT", oT[:, :, :].rearrange("p a b -> p (a b)")), ("d_gsc", gsc)])
    ar.pop()

    w1sb = ar.alloc((128, 8, 2048), BF16)
    w3sb = ar.alloc((128, 8, 2048), BF16)
    w2sb = ar.alloc((128, 16, 1024), BF16)
    hid = ar.alloc((128, 16, 512), BF16)
    sil = ar.alloc((128, 512), BF16)
    y_sb = ar.alloc((128, 16, D), BF16)
    xm2 = ar.alloc((128, D), F32)
    hid_T = {"w13": None, "w2": None}
    y_last = {}
    ym = S.op("vector", lambda v: v.memset(
        y_sb[:, :, :].rearrange("p a b -> p (a b)"), 0.0),
        waits=S.guard(), inc="V")
    for e in range(8):
        w13w = S.guard() if e == 0 else [("T", hid_T["w13"])]
        w1l = S.dma("gpsimd", w1sb,
                    W["l1f_w1"][e].rearrange("(ic p) f -> p ic f", p=128),
                    waits=w13w)
        w3l = S.dma("gpsimd", w3sb,
                    W["l1f_w3"][e].rearrange("(ic p) f -> p ic f", p=128))
        w2w = S.guard() if e == 0 else [("T", hid_T["w2"])]
        w2l = S.dma("gpsimd", w2sb,
                    W["l1f_w2"][e].rearrange("(fc p) d -> p fc d", p=128),
                    waits=w2w)
        last_hidmm_T = None
        tmy = None
        for th in range(2):
            for tg in range(2):
                hid_evs = None
                for fc in range(16):
                    bkA = (fc % 2) * 2
                    bkB = bkA + 1
                    mwA = [pb_read[bkA], ("DG", w3l)]
                    if e == 0 and th == 0 and tg == 0 and fc == 0:
                        mwA.append(("V", tT_ok2))
                    tcol = th * 1024 + tg * 512
                    tmA = mm_group(S, PB[bkA][:, :],
                                   [(w1sb[:, ic, fc * 128:(fc + 1) * 128],
                                     tT[:, ic, tcol:tcol + 512])
                                    for ic in range(8)], waits=mwA)
                    tmB = mm_group(S, PB[bkB][:, :],
                                   [(w3sb[:, ic, fc * 128:(fc + 1) * 128],
                                     tT[:, ic, tcol:tcol + 512])
                                    for ic in range(8)],
                                   waits=[pb_read[bkB]])
                    sA = S.op("scalar", lambda sc, bkA=bkA: sc.activation(
                        out=sil, in_=PB[bkA][:, :], func=AF.Silu),
                        waits=[("T", tmA), ("V", hid_evs)], inc="A")
                    hid_evs = S.op("vector", lambda v, bkB=bkB, fc=fc: v.tensor_tensor(
                        out=hid[:, fc, :], in0=PB[bkB][:, :], in1=sil,
                        op=ALU.mult), waits=[("A", sA), ("T", tmB)],
                        inc="V")
                    pb_read[bkA] = ("A", sA)
                    pb_read[bkB] = ("V", hid_evs)
                last_hidmm_T = tmB
                for tl in range(4):
                    tile = th * 8 + tg * 4 + tl
                    for dg in range(2):
                        bk = 4 + dg
                        tmy = mm_group(
                            S, PB[bk][:, :],
                            [(hid[:, fc, tl * 128:(tl + 1) * 128],
                              w2sb[:, fc, dg * 512:(dg + 1) * 512])
                             for fc in range(16)],
                            waits=[pb_read[bk], ("V", hid_evs),
                                   ("DG", w2l)])
                        yv = S.op("vector", lambda v, tile=tile, dg=dg, bk=bk, e=e: v.scalar_tensor_tensor(
                            out=y_sb[:, tile, dg * 512:(dg + 1) * 512],
                            in0=PB[bk][:, :],
                            scalar=gate_all[:, tile, e:e + 1],
                            in1=y_sb[:, tile, dg * 512:(dg + 1) * 512],
                            op0=ALU.mult, op1=ALU.add),
                            waits=[("T", tmy), ("V", ym), ("V", gate_ok),
                                   ("V", y_last.get((tile, dg)))],
                            inc="V")
                        pb_read[bk] = ("V", yv)
                        y_last[(tile, dg)] = yv
        hid_T["w13"] = last_hidmm_T
        hid_T["w2"] = tmy
    y_done = S.cnt["V"]
    for tile in range(NTILE):
        xl = S.dma("sync", xtile[tile % 2],
                   in_ap[tile * 128:(tile + 1) * 128, :],
                   waits=[("DS", last_xm)])
        xm = S.dma("sync", xm2, xb2[tile * 128:(tile + 1) * 128, :],
                   waits=[("V", y_done)] if tile == 0 else [])
        vv1 = S.op("vector", lambda v, tile=tile: v.tensor_add(
            out=xm2, in0=xm2, in1=y_sb[:, tile, :]),
            waits=[("DS", xm), ("V", y_done)])
        vv2 = S.op("vector", lambda v, tile=tile: v.scalar_tensor_tensor(
            out=xm2, in0=xm2, scalar=0.5,
            in1=xtile[tile % 2], op0=ALU.mult, op1=ALU.add),
            waits=[("DS", xl)], inc="V")
        last_xm = S.dma("sync", out_ap[tile * 128:(tile + 1) * 128, :],
                        xm2, waits=[("V", vv2)])
    ar.pop()
    return last_xm


def l1ffn_host_arrays(inputs):
    HD = 32
    inv_freq = 1.0 / (10000.0 ** (np.arange(0, HD, 2, dtype=np.float64)
                                  / HD))
    pos = np.arange(T, dtype=np.float64)
    cos = np.zeros((128, NTILE * 256), np.float32)
    sin = np.zeros((128, NTILE * 256), np.float32)
    for i in range(NTILE):
        p = pos[i * 128:(i + 1) * 128][:, None]
        ang = p * inv_freq[None, :]
        cos_t = np.cos(ang).astype(np.float32)
        sin_t = np.sin(ang).astype(np.float32)
        cos[:, i * 256:(i + 1) * 256] = np.tile(cos_t, (1, 16))
        sin[:, i * 256:(i + 1) * 256] = np.tile(sin_t, (1, 16))
    return {
        "l1f_wdown": np.ascontiguousarray(inputs["L1_ffn_w_down"]),
        "l1f_wq": np.ascontiguousarray(inputs["L1_ffn_wq"]),
        "l1f_wk": np.ascontiguousarray(inputs["L1_ffn_wk"]),
        "l1f_wv": np.ascontiguousarray(inputs["L1_ffn_wv"]),
        "l1f_wup": np.ascontiguousarray(inputs["L1_ffn_w_up"]),
        "l1f_router": np.ascontiguousarray(inputs["L1_ffn_router_w"]),
        "l1f_w1": np.ascontiguousarray(inputs["L1_ffn_w1"]),
        "l1f_w3": np.ascontiguousarray(inputs["L1_ffn_w3"]),
        "l1f_w2": np.ascontiguousarray(inputs["L1_ffn_w2"]),
        "l1f_g1m": np.ascontiguousarray(
            inputs["L1_ffn_g1"].reshape(8, 128).T),
        "l1f_g2row": np.ascontiguousarray(
            inputs["L1_ffn_g2"].reshape(1, D)),
        "c_cos": cos,
        "c_sin": sin,
    }


def declare_l1ffn(nc):
    def dt_(name, shape):
        return nc.dram_tensor(name, shape, F32, kind="ExternalInput").ap()
    return {
        "l1f_wdown": dt_("l1f_wdown", [D, 512]),
        "l1f_wq": dt_("l1f_wq", [512, 512]),
        "l1f_wk": dt_("l1f_wk", [512, 512]),
        "l1f_wv": dt_("l1f_wv", [512, 512]),
        "l1f_wup": dt_("l1f_wup", [512, D]),
        "l1f_router": dt_("l1f_router", [D, 8]),
        "l1f_w1": dt_("l1f_w1", [8, D, 2048]),
        "l1f_w3": dt_("l1f_w3", [8, D, 2048]),
        "l1f_w2": dt_("l1f_w2", [8, 2048, D]),
        "l1f_g1m": dt_("l1f_g1m", [128, 8]),
        "l1f_g2row": dt_("l1f_g2row", [1, D]),
        "c_cos": dt_("c_cos", [128, NTILE * 256]),
        "c_sin": dt_("c_sin", [128, NTILE * 256]),
    }


L1F_DBG_SHAPES = {
    "d_cT": ([128, 4 * T], BF16), "d_hT": ([128, 8 * T], BF16),
    "d_qT": ([128, 4 * T], BF16), "d_kT": ([128, 4 * T], BF16),
    "d_v": ([128, 16 * 512], BF16), "d_bias": ([128, 16], F32),
    "d_oT": ([128, 4 * T], BF16), "d_gate": ([128, 16 * 8], F32),
    "d_tT": ([128, 8 * T], BF16),
    "d_st": ([128, 32], F32),
    "d_gsc": ([128, 64], F32),
    "d_y": ([128, 8 * D], F32),
    "d_hid": ([128, 16 * 512], BF16),
}


def build_l1ffn_program(upto=None, dbg_names=(), xb2_out=False):
    nc = bass.Bass("TRN2", target_bir_lowering=False, debug=False)
    xin = nc.dram_tensor("xin", [T, D], F32, kind="ExternalInput").ap()
    xout = nc.dram_tensor("xout", [T, D], F32, kind="ExternalOutput").ap()
    xb2 = nc.dram_tensor("xb2", [T, D], F32,
                         kind="ExternalOutput" if xb2_out else
                         "Internal").ap()
    dbg = {}
    for nm in dbg_names:
        shp, dt = L1F_DBG_SHAPES[nm]
        dbg[nm] = nc.dram_tensor(nm, shp, dt, kind="ExternalOutput").ap()
    S = Sched(nc)
    with contextlib.ExitStack() as stk:
        ar, PB, PBT = _alloc_common(stk, nc, S)
        cst = declare_consts(S, nc, ar)
        W = declare_l1ffn(nc)
        emit_l1ffn(S, nc, ar, cst, W, xin, xout, xb2, PB, PBT, upto=upto,
                   dbg=dbg or None)
        S.op("sync", lambda s: s.wait_ge(S.sems["DS"], S.cnt["DS"]))
        block = stk.enter_context(nc.Block())
        S.emit(block)
    return nc





ACT_CONSTS = [1.0 / D, -0.5, LN_EPS, 1e-6, float(1.0 / np.sqrt(32.0)),
              0.5, 2.0, 1.0 / 512.0, 1.0, -1.0, 0.005, -0.005,
              float(np.sqrt(np.float32(0.01))), float(0.5 * np.pi)]


def make_const_arrays():
    return {
        "c_ident": np.eye(128, dtype=np.float32),
        "c_tri": np.triu(np.ones((128, 128), np.float32)),
        "c_ones": np.ones((128, 128), np.float32),
        "c_actc": np.tile(np.asarray(ACT_CONSTS, np.float32)[None, :],
                          (128, 1)),
        "c_trow": np.tile(np.arange(T, dtype=np.float32)[None, :], (128, 1)),
    }


def declare_consts(S, nc, ar):
    cst = {}
    d_ident = nc.dram_tensor("c_ident", [128, 128], F32,
                             kind="ExternalInput").ap()
    d_tri = nc.dram_tensor("c_tri", [128, 128], F32,
                           kind="ExternalInput").ap()
    d_ones = nc.dram_tensor("c_ones", [128, 128], F32,
                            kind="ExternalInput").ap()
    cst["ident_bf"] = ar.alloc((128, 128), BF16)
    cst["tri_f"] = ar.alloc((128, 128), F32)
    cst["tri_bf"] = ar.alloc((128, 128), BF16)
    cst["ones_f"] = ar.alloc((128, 128), F32)
    cst["ones_bf"] = ar.alloc((128, 128), BF16)
    d_actc = nc.dram_tensor("c_actc", [128, len(ACT_CONSTS)], F32,
                            kind="ExternalInput").ap()
    actc = ar.alloc((128, len(ACT_CONSTS)), F32)
    for i, v in enumerate(ACT_CONSTS):
        nc.const_aps.register(F32, float(v), actc[:, i:i + 1])
    S.dma("gpsimd", cst["ident_bf"], d_ident)
    S.dma("gpsimd", cst["tri_bf"], d_tri)
    S.dma("gpsimd", cst["ones_bf"], d_ones)
    cst["loaded_g"] = S.cnt["DG"]
    S.dma("sync", cst["tri_f"], d_tri)
    S.dma("sync", cst["ones_f"], d_ones)
    ld = S.dma("sync", actc, d_actc)
    cst["loaded"] = ld
    return cst


def l0ffn_host_arrays(inputs):
    out = {
        "l0_w1T": np.ascontiguousarray(inputs["L0_ffn_w1_w"].T),
        "l0_w2T": np.ascontiguousarray(inputs["L0_ffn_w2_w"].T),
        "l0_h1T": np.ascontiguousarray(inputs["L0_ffn_h1_w"].T),
        "l0_h2T": np.ascontiguousarray(inputs["L0_ffn_h2_w"].T),
        "l0_w3T": np.ascontiguousarray(inputs["L0_ffn_w3_w"].T),
        "l0_w3s_row": np.ascontiguousarray(
            inputs["L0_ffn_w3_s"].reshape(1, D)),
    }
    for nm, key in (("s1m", "L0_ffn_w1_s"), ("s2m", "L0_ffn_w2_s"),
                    ("h1m", "L0_ffn_h1_s"), ("h2m", "L0_ffn_h2_s")):
        out["l0_" + nm] = np.ascontiguousarray(
            inputs[key].reshape(32, 128).T)
    return out


def declare_l0ffn(S, nc, ar, cst):
    W = {}
    W["w1T"] = nc.dram_tensor("l0_w1T", [D, DHID], F32,
                              kind="ExternalInput").ap()
    W["w2T"] = nc.dram_tensor("l0_w2T", [D, DHID], F32,
                              kind="ExternalInput").ap()
    W["h1T"] = nc.dram_tensor("l0_h1T", [DHID, DHID], F32,
                              kind="ExternalInput").ap()
    W["h2T"] = nc.dram_tensor("l0_h2T", [DHID, DHID], F32,
                              kind="ExternalInput").ap()
    W["w3T"] = nc.dram_tensor("l0_w3T", [DHID, D], F32,
                              kind="ExternalInput").ap()
    for nm in ("s1m", "s2m", "h1m", "h2m"):
        d = nc.dram_tensor("l0_" + nm, [128, 32], F32,
                           kind="ExternalInput").ap()
        W[nm] = ar.alloc((128, 32), F32)
        S.dma("sync", W[nm], d)
    d = nc.dram_tensor("l0_w3s_row", [1, D], F32, kind="ExternalInput").ap()
    w3row_full = ar.alloc((128, D), F32)[0:1, :]
    ld = S.dma("sync", w3row_full, d)
    cst["w3s_row"] = w3row_full
    cst["loaded"] = ld
    return W



def _alloc_common(stk, nc, S):
    big = stk.enter_context(nc.sbuf_tensor("arena", [128, ARENA_WORDS], F32))
    PB = [stk.enter_context(nc.psum_tensor(f"pb{i}", [128, 512], F32))
          for i in range(7)]
    PBT = stk.enter_context(nc.psum_tensor("pbt", [128, 1024], BF16))
    for name in ("T", "V", "A", "P", "DS", "DG"):
        S.sems[name] = stk.enter_context(nc.semaphore(name))
    return Arena(big, ARENA_WORDS), PB, PBT


def build_l0ffn_program():
    nc = bass.Bass("TRN2", target_bir_lowering=False, debug=False)
    xin = nc.dram_tensor("xin", [T, D], F32, kind="ExternalInput").ap()
    xout = nc.dram_tensor("xout", [T, D], F32, kind="ExternalOutput").ap()
    S = Sched(nc)
    with contextlib.ExitStack() as stk:
        ar, PB, PBT = _alloc_common(stk, nc, S)
        cst = declare_consts(S, nc, ar)
        W = declare_l0ffn(S, nc, ar, cst)
        emit_l0ffn(S, nc, ar, cst, W, xin, xout, PB, PBT)
        S.op("sync", lambda s: s.wait_ge(S.sems["DS"], S.cnt["DS"]))
        block = stk.enter_context(nc.Block())
        S.emit(block)
    return nc


def build_fused_program():
    nc = bass.Bass("TRN2", target_bir_lowering=False, debug=False)
    xin = nc.dram_tensor("xin", [T, D], F32, kind="ExternalInput").ap()
    xout = nc.dram_tensor("xout", [T, D], F32, kind="ExternalOutput").ap()
    xbA = nc.dram_tensor("xbA", [T, D], F32, kind="Internal").ap()
    xbB = nc.dram_tensor("xbB", [T, D], F32, kind="Internal").ap()
    xb2 = nc.dram_tensor("xb2", [T, D], F32, kind="Internal").ap()
    S = Sched(nc)
    with contextlib.ExitStack() as stk:
        ar, PB, PBT = _alloc_common(stk, nc, S)
        cst = declare_consts(S, nc, ar)
        W0f = declare_l0ffn(S, nc, ar, cst)
        W0a = declare_l0attn(nc)
        W1a = declare_l1attn(nc)
        W1f = declare_l1ffn(nc)
        emit_l0ffn(S, nc, ar, cst, W0f, xin, xbA, PB, PBT)
        emit_l0attn(S, nc, ar, cst, W0a, xbA, xbB, PB, PBT)
        emit_l0ffn(S, nc, ar, cst, W0f, xbB, xbA, PB, PBT)
        emit_l1ffn(S, nc, ar, cst, W1f, xbA, xbB, xb2, PB, PBT)
        emit_l1attn(S, nc, ar, cst, W1a, xbB, xbA, PB, PBT)
        emit_l1ffn(S, nc, ar, cst, W1f, xbA, xout, xb2, PB, PBT)
        S.op("sync", lambda s: s.wait_ge(S.sems["DS"], S.cnt["DS"]))
        block = stk.enter_context(nc.Block())
        S.emit(block)
    return nc


_PROGRAMS = {}


def _get_program(name):
    if name not in _PROGRAMS:
        if name == "fused":
            _PROGRAMS[name] = build_fused_program()
        elif name == "l0ffn":
            _PROGRAMS[name] = build_l0ffn_program()
        elif name == "l0attn":
            _PROGRAMS[name] = build_l0attn_program()
        elif name == "l1ffn":
            _PROGRAMS[name] = build_l1ffn_program()
        elif name == "l1attn":
            _PROGRAMS[name] = build_l1attn_program()
    return _PROGRAMS[name]


def kernel(**inputs):
    inputs = {k: np.asarray(v) for k, v in inputs.items()}
    n = 8
    x = np.ascontiguousarray(inputs["L0_attn_x"], dtype=np.float32)

    base = dict(make_const_arrays())
    base.update(l0ffn_host_arrays(inputs))
    base.update(l0attn_host_arrays(inputs))
    base.update(l1attn_host_arrays(inputs))
    base.update(l1ffn_host_arrays(inputs))

    def run(name, xs):
        nc = _get_program(name)
        names = {t for t in nc.to_json()["tensors"]} if False else None
        in_maps = []
        for c in range(n):
            m = dict(base)
            m["xin"] = np.ascontiguousarray(xs[c])
            in_maps.append(m)
        res = run_bass_kernel_spmd(nc, in_maps, core_ids=list(range(n)))
        return [res.results[c]["xout"] for c in range(n)]

    xs = [x[c] for c in range(n)]
    import os
    if os.environ.get("MK_UNFUSED"):
        for stage in ("l0ffn", "l0attn", "l0ffn", "l1ffn", "l1attn",
                      "l1ffn"):
            xs = run(stage, xs)
    else:
        xs = run("fused", xs)
    out = np.stack([np.asarray(s) for s in xs], axis=0).astype(np.float32)
    return out
